# version 19
# baseline (speedup 1.0000x reference)
import numpy as np
import concourse.bass as bass
import concourse.mybir as mybir
from concourse.bass_utils import run_bass_kernel_spmd

F32 = mybir.dt.float32
BF16 = mybir.dt.bfloat16
I32 = mybir.dt.int32
U32 = mybir.dt.uint32
AF = mybir.ActivationFunctionType
ALU = mybir.AluOpType
AX = mybir.AxisListType

NCORE = 8
B = 2
L = 4096
D = 2048
DIN = 4096
DST = 16
IC = DIN // NCORE
TOK = (B * L) // NCORE
NTC = TOK // 128
WN = 512
NWIN = B * L // WN
WPB = L // WN
NJ = 4
DFF = 8192
DFC = DFF // NCORE

MAGIC = 12582912.0

SEM_NAMES = ["s_ld", "s_gd", "s_coll",
             "L_c", "L_w", "L_w2", "L_x", "L_xt", "L_wc0", "L_wc1", "L_wc2", "L_wc3",
             "L_w30", "L_w31", "L_h", "L_xo0", "L_xo1", "G_gr0", "G_gr1",
             "G_c", "G_wo", "G_wo2", "G_st", "G_rb", "G_wr", "G_wq", "G_wq2",
             "G_h0", "G_h1",
             "G_gy0", "G_gy1", "G_xq", "G_l30", "G_l31", "G_g", "G_gyl", "G_grd",
             "G_dbg",
             "s_mmA", "s_mmB", "s_mmC", "s_mmT", "s_mmS",
             "s_evA", "s_evB", "s_tpe", "s_xn", "s_dA", "s_scm",
             "s_vC", "s_vF", "s_vcast", "s_vstat", "s_vmisc", "s_vq", "s_vh",
             "s_vgy", "s_vbt", "s_vscan", "s_varg", "s_vout", "s_out",
             "s_out0", "s_out1"]


def _host_prepare(inp):
    x = np.ascontiguousarray(
        np.asarray(inp["L0_attn_hidden_states"], np.float32).reshape(B * L, D))
    W_in = np.asarray(inp["L0_attn_W_in"], np.float32)
    W_dt = np.asarray(inp["L0_attn_W_dt"], np.float32)
    b_dt = np.asarray(inp["L0_attn_b_dt"], np.float32)
    conv_w = np.asarray(inp["L0_attn_conv_w"], np.float32)
    conv_b = np.asarray(inp["L0_attn_conv_b"], np.float32)
    A_log = np.asarray(inp["L0_attn_A_log"], np.float32)
    D_par = np.asarray(inp["L0_attn_D_param"], np.float32)
    W_out = np.asarray(inp["L0_attn_W_out"], np.float32)
    w1 = np.asarray(inp["L0_ffn_w1"], np.float32)
    w2 = np.asarray(inp["L0_ffn_w2"], np.float32)
    w3 = np.asarray(inp["L0_ffn_w3"], np.float32)

    ident = np.eye(128, dtype=np.float32)
    indB = np.zeros((128, NJ * DST), np.float32)
    for j in range(NJ):
        for n in range(DST):
            indB[64 + j * 16 + n, j * DST + n] = 1.0
    indC = np.zeros((DST, 128, 128), np.float32)
    for n in range(DST):
        for p in range(128):
            indC[n, (p // 16) * 16 + n, p] = 1.0
    indU = np.zeros((NJ, 128, 128), np.float32)
    for j in range(NJ):
        for p in range(128):
            indU[j, j * 16 + (p % 16), p] = 1.0

    woutT = np.ascontiguousarray(W_out.T)

    in_maps = []
    for c in range(NCORE):
        zc = slice(c * IC, (c + 1) * IC)
        rows = np.concatenate([
            W_in[zc],
            W_in[DIN + c * 64:DIN + (c + 1) * 64],
            W_in[DIN + 512 + c * 64:DIN + 512 + (c + 1) * 64],
            W_in[DIN + 1024 + c * IC:DIN + 1024 + (c + 1) * IC],
            W_in[DIN + 1024 + DIN:],
        ], axis=0)
        in_maps.append(dict(
            x_all=x,
            x_tok=np.ascontiguousarray(x[c * TOK:(c + 1) * TOK]),
            winT=np.ascontiguousarray(rows.T),
            wdtT=np.ascontiguousarray(W_dt[zc].T),
            woutT=woutT,
            w1T=np.ascontiguousarray(w1[c * DFC:(c + 1) * DFC].T),
            w2T=np.ascontiguousarray(w2[c * DFC:(c + 1) * DFC].T),
            w3T=np.ascontiguousarray(w3[:, c * DFC:(c + 1) * DFC].T),
            bdt=np.ascontiguousarray(b_dt[zc]),
            convw=np.ascontiguousarray(conv_w[zc, 0, :]),
            convb=np.ascontiguousarray(conv_b[zc]),
            alog=np.ascontiguousarray(A_log[zc]),
            dpar=np.ascontiguousarray(D_par[zc]),
            ident=ident, indB=indB, indC=indC, indU=indU,
            tokbase=np.array([[c * NTC]], dtype=np.uint32),
            tokbase2=np.array([[c * 2]], dtype=np.uint32),
        ))
    return in_maps


class Walk:
    def __init__(self, sems, engine_name, eng):
        self.S = sems
        self.E = engine_name
        self.eng = eng
        self.cnt = {k: 0 for k in sems}
        self.lastwait = {}
        self.iA = self.iB = self.iC = self.iT = 0
        self.checks = []

    def me(self, name):
        return self.E == name

    def bump(self, sem, by=1):
        self.cnt[sem] += by
        return self.cnt[sem]

    def wait(self, sem, val):
        if val is None or val <= 0:
            return
        if self.lastwait.get(sem, -1) >= val:
            return
        self.lastwait[sem] = val
        self.eng.wait_ge(self.S[sem], val)


def _emit(nc, sems, E, eng, T, sim=False, stage=99, nwin=NWIN, wstage=99):
    s = Walk(sems, E, eng)
    S = s.S
    me = s.me

    def dma(queue, sem, out, in_, waits=(), slow=False):
        v = s.bump(sem, 16)
        s.checks.append((sem, v))
        if me(queue):
            for (sm, vl) in waits:
                s.wait(sm, vl)
            eng.dma_start(out=out, in_=in_,
                          allow_slow_non_contiguous=slow).then_inc(S[sem], 16)
        return (sem, v)

    prev_op = {}

    def op(engine, sem, waits, emitfn, dep=False):
        v = s.bump(sem, 1)
        s.checks.append((sem, v))
        w = list(waits)
        if dep and engine in prev_op:
            w.append(prev_op[engine])
        prev_op[engine] = (sem, v)
        if me(engine):
            for (sm, vl) in w:
                s.wait(sm, vl)
            emitfn(eng).then_inc(S[sem], 1)
        return v

    def coll(kind, alu, ins, outs, waits):
        v = s.bump('s_coll', 16 if sim else 1)
        s.checks.append(('s_coll', v))
        if me('gpsimd'):
            for (sm, vl) in waits:
                s.wait(sm, vl)
            if sim:
                n = ins[0].size()
                eng.dma_start(out=outs[0].flatten()[0:n],
                              in_=ins[0].flatten()[0:n]).then_inc(S['s_coll'], 16)
            else:
                eng.collective_compute(kind, alu, replica_groups=[list(range(NCORE))],
                                      ins=ins, outs=outs).then_inc(S['s_coll'], 1)
        return v

    def slotA():
        s.iA += 1
        i = s.iA
        return T['pA'][:, ((i - 1) % 2) * 512:(((i - 1) % 2) + 1) * 512], ('s_evA', i - 2)

    def slotB():
        s.iB += 1
        i = s.iB
        return T['pB'][:, 0:512], ('s_evB', i - 1)

    def slotC():
        s.iC += 1
        i = s.iC
        return T['pC'][:, ((i - 1) % 2) * 512:(((i - 1) % 2) + 1) * 512], ('s_vC', i - 2)

    def slotTG():
        s.iT += 1
        g = s.iT
        return 0, ('s_tpe', g - 1)

    def tp_group(src_tile, blocks, dst_fn, vwait):
        base, gate = slotTG()
        vt = 0
        for bi, blk in enumerate(blocks):
            ps = T['pT'][:, base + bi * 128:base + (bi + 1) * 128]
            g0 = [vwait, ('s_vcast', v_idb), gate] if bi == 0 else []
            vt = op('tensor', 's_mmT', g0,
                    lambda e, ps=ps, blk=blk, src_tile=src_tile: e.transpose(
                        ps, src_tile[:, blk * 128:(blk + 1) * 128], T['identb'][:]))
        bank = T['pT'][:, base:base + 1024]
        return op('scalar', 's_tpe', [('s_mmT', vt)],
                  lambda e, bank=bank: e.copy(out=dst_fn(), in_=bank))

    def winT(k, o):
        return T['winT'][:].rearrange("p (k o c) -> p k o c", k=16, o=10)[:, k, o, :]

    def xnfm(k):
        return T['xn_fm'][:].rearrange("p (k c) -> p k c", k=16)[:, k, :]

    def xqfm(k, q):
        return T['xq_fm'][:].rearrange("p (k c) -> p k c", k=16)[:, k, q * 128:(q + 1) * 128]

    def stv(kk):
        return T['stc'][:, kk * 8:(kk + 1) * 8]

    def stcol(kk, tt):
        return T['stc'][:, kk * 8 + tt:kk * 8 + tt + 1]

    if me('vector'):
        eng.memset(T['wabs'][:], 0.0)
        eng.memset(T['wsum_row'][:], 0.0)
        eng.memset(T['fcol'][:], 0.0)
        eng.memset(T['onescol'][:], 1.0)
        eng.memset(T['onescol32'][:], 1.0)
        eng.memset(T['utail'][:], 0.0)
        eng.memset(T['u_in'][:], 0.0)
        eng.memset(T['carry'][:], 0.0)
        eng.memset(T['hpm'][:], 0.0)
        eng.memset(T['hps'][:], 0.0)

    dma('gpsimd', 'G_c', T['identb'][:], T['ident_d'])
    dma('gpsimd', 'G_c', T['indB'][:], T['indB_d'])
    dma('gpsimd', 'G_c', T['indC'][:].rearrange("p (a c) -> p a c", a=16), T['indC_d'].transpose([1, 0, 2]))
    g_consts = dma('gpsimd', 'G_c', T['indU'][:].rearrange("p (a c) -> p a c", a=4), T['indU_d'].transpose([1, 0, 2]))
    for j in range(NJ):
        dma('sync', 'L_c', T['bdtc'][:, j:j + 1],
            T['bdt_d'].rearrange("(a p) -> a p", p=128)[j].unsqueeze(1))
        dma('sync', 'L_c', T['convbc'][:, j:j + 1],
            T['convb_d'].rearrange("(a p) -> a p", p=128)[j].unsqueeze(1))
        dma('sync', 'L_c', T['dparc'][:, j:j + 1],
            T['dpar_d'].rearrange("(a p) -> a p", p=128)[j].unsqueeze(1))
        dma('sync', 'L_c', T['convwc'][:, j * 4:(j + 1) * 4], T['convw_d'][j * 128:(j + 1) * 128, :])
        dma('sync', 'L_c', T['alogs'][:, j * 16:(j + 1) * 16], T['alog_d'][j * 128:(j + 1) * 128, :])
    dma('sync', 'L_c', T['tokb_s'][:], T['tokb_d'])
    dma('sync', 'L_c', T['tokb2_s'][:], T['tokb2_d'])
    v_consts = ('L_c', s.cnt['L_c'])
    v_tokb = v_consts
    v_gconsts = g_consts

    v_idb = op('vector', 's_vcast', [g_consts],
               lambda e: e.engine_nop())
    va = op('scalar', 's_scm', [v_consts],
            lambda e: e.activation(out=T['A_t'][:], in_=T['alogs'][:], func=AF.Exp))
    op('vector', 's_vcast', [('s_scm', va)],
       lambda e: e.tensor_scalar_mul(T['A_t'][:], T['A_t'][:], -1.0))
    op('vector', 's_vcast', [v_consts],
       lambda e: e.tensor_scalar_mul(T['bdt2c'][:], T['bdtc'][:], 2.0))

    scrhP = [T['scrh'], T['scrh2']]
    L_wP = ['L_w', 'L_w2']

    v = dma('sync', 'L_w', T['scrh'][:, 0:512], T['wdtT_d'], waits=[('s_vcast', v_idb)])
    vcast_h = [0, 0]
    vcast_h[0] = op('vector', 's_vcast', [v],
                    lambda e: e.tensor_copy(out=T['wdtT'][:], in_=T['scrh'][:, 0:512]))

    for o in range(10):
        p = o % 2
        src = T['winT_d'].rearrange("(k p) c -> p k c", p=128)[:, :, o * 128:(o + 1) * 128]
        v = dma('sync', L_wP[p], scrhP[p][:].rearrange("p (k c) -> p k c", c=128)[:, 0:16, :],
                src, waits=[('s_vcast', vcast_h[p])])
        vcast_h[p] = op('vector', 's_vcast', [v],
                        lambda e, o=o, p=p: e.tensor_copy(
                            out=T['winT'][:].rearrange("p (k o c) -> p k o c", k=16, o=10)[:, :, o, :],
                            in_=scrhP[p][:].rearrange("p (k c) -> p k c", c=128)[:, 0:16, :]))
    v_wincast = s.cnt['s_vcast']

    v_gd_w = ('G_wo', 0)
    for k in range(32):
        p = k % 2
        v = dma('sync', L_wP[p], scrhP[p][:, 0:2048], T['woutT_d'][k * 128:(k + 1) * 128, :],
                waits=[('s_vcast', vcast_h[p])])
        vcast = op('vector', 's_vcast', [v, v_gd_w],
                   lambda e, p=p: e.tensor_copy(out=T['scrhb'][:, 0:2048],
                                                in_=scrhP[p][:, 0:2048]))
        vcast_h[p] = vcast
        v_gd_w = dma('gpsimd', 'G_wo', T['woutTb_d'][k * 128:(k + 1) * 128, :],
                     T['scrhb'][:, 0:2048], waits=[('s_vcast', vcast)])

    v_stt = 0
    v_sqq = 0
    v_xload = 0
    for tt in range(NTC):
        v_xload = dma('sync', 'L_xt', T['xtok'][:], T['xtok_d'][tt * 128:(tt + 1) * 128, :],
                      waits=[('s_vstat', v_stt), ('s_scm', v_sqq)])
        v_sqq = op('scalar', 's_scm', [v_xload, ('s_vstat', v_stt)],
                   lambda e, tt=tt: e.activation(
                       out=T['scr32'][:, 0:2048], in_=T['xtok'][:], func=AF.Square,
                       accum_out=stcol(1, tt)))
        vred = op('vector', 's_vstat', [v_xload],
                  lambda e, tt=tt: e.reduce_sum(out=stcol(0, tt), in_=T['xtok'][:],
                                                axis=AX.X))
        op('vector', 's_vstat', [('s_vstat', vred)],
           lambda e, tt=tt: e.tensor_scalar_mul(stcol(2, tt), stcol(0, tt), 1.0 / D))
        op('vector', 's_vstat', [('s_scm', v_sqq)],
           lambda e, tt=tt: e.tensor_scalar_mul(stcol(3, tt), stcol(1, tt), 1.0 / D),
           dep=True)
        op('vector', 's_vstat', [],
           lambda e, tt=tt: e.tensor_mul(T['tmpc'][:, 0:1], stcol(2, tt), stcol(2, tt)),
           dep=True)
        op('vector', 's_vstat', [],
           lambda e, tt=tt: e.tensor_sub(stcol(3, tt), stcol(3, tt), T['tmpc'][:, 0:1]),
           dep=True)
        op('vector', 's_vstat', [('s_scm', v_sqq)],
           lambda e, tt=tt: e.tensor_scalar_sub(T['scr32'][:, 0:2048], T['xtok'][:],
                                                stcol(2, tt)), dep=True)
        v_stt = op('vector', 's_vstat', [],
                   lambda e, tt=tt: e.reduce_max(
                       out=stcol(7, tt), in_=T['scr32'][:, 0:2048],
                       axis=AX.X, apply_absolute_value=True), dep=True)

    v_eps = op('vector', 's_vstat', [],
               lambda e: e.tensor_scalar_add(stv(4), stv(3), 1e-5), dep=True)
    vr0 = op('scalar', 's_scm', [('s_vstat', v_eps)],
             lambda e: e.activation(out=stv(4), in_=stv(4), func=AF.Sqrt))
    op('vector', 's_vstat', [('s_scm', vr0)],
       lambda e: e.reciprocal(out=stv(4), in_=stv(4)))
    vr = op('scalar', 's_scm', [('s_vstat', s.cnt['s_vstat'])],
            lambda e: e.nop())
    op('vector', 's_vstat', [('s_scm', vr)], lambda e: e.tensor_mul(stv(5), stv(2), stv(4)),
       dep=True)
    op('vector', 's_vstat', [], lambda e: e.tensor_scalar_mul(stv(5), stv(5), -1.0),
       dep=True)
    op('vector', 's_vstat', [], lambda e: e.reciprocal(out=stv(6), in_=stv(7)), dep=True)
    v_statq = op('vector', 's_vstat', [],
                 lambda e: e.tensor_scalar_mul(stv(6), stv(6), 127.0), dep=True)

    v_pay = 0
    for tt in range(NTC):
        dma('gpsimd', 'G_st', T['agst_in'][0, tt * 128:(tt + 1) * 128].rearrange("(p o) -> p o", o=1),
            T['stc'][:, 4 * 8 + tt:4 * 8 + tt + 1], waits=[('s_vstat', v_statq)])
        v_pay = dma('gpsimd', 'G_st', T['agst_in'][1, tt * 128:(tt + 1) * 128].rearrange("(p o) -> p o", o=1),
                    T['stc'][:, 5 * 8 + tt:5 * 8 + tt + 1], waits=[('s_vstat', v_statq)])
    v_stAG = coll("AllGather", ALU.bypass, [T['agst_in'][:]], [T['agst_out'][:]],
                  [v_pay])
    v_rb = 0
    for cc in range(NCORE):
        for kk in range(2):
            for tl in range(8):
                v_rb = dma('gpsimd', 'G_rb',
                           T['coltiles'][:].rearrange("p (c k t) -> p c k t", c=8, k=2)
                           [:, cc, kk, tl:tl + 1],
                           T['agst_out'][cc, kk * TOK + tl * 128:kk * TOK + (tl + 1) * 128]
                           .rearrange("(p o) -> p o", o=1),
                           waits=[('s_coll', v_stAG)])

    def dbg_taps(keys):
        if not T.get('dbg'):
            return
        allw = [('s_vstat', s.cnt['s_vstat']), ('s_scm', s.cnt['s_scm']),
                ('s_vq', s.cnt['s_vq']), ('s_vout', s.cnt['s_vout']),
                ('s_vh', s.cnt['s_vh']), ('s_coll', s.cnt['s_coll']),
                ('G_st', s.cnt['G_st']), ('G_wq', s.cnt['G_wq']),
                ('G_wq2', s.cnt['G_wq2']),
                ('G_gy0', s.cnt['G_gy0']), ('G_gy1', s.cnt['G_gy1']),
                ('G_h0', s.cnt['G_h0']), ('G_h1', s.cnt['G_h1']),
                ('G_g', s.cnt['G_g'])]
        v_dbg = None
        for k, src in keys:
            v_dbg = dma('gpsimd', 'G_dbg', T[k][:], src, waits=allw)
        if v_dbg is not None and me('gpsimd'):
            s.wait(*v_dbg)

    def taps_l1():
        return [('d_stc', T['stc'][:]), ('d_agst', T['agst_in'][:])]

    def taps_l2():
        return taps_l1() + [('d_wq', T['agwq_in'][:]), ('d_fcol', T['fcol'][:])]

    if stage < 1:
        dbg_taps(taps_l1())
        return s

    def r_col(tt):
        return T['coltiles'][:].rearrange("p (c k t) -> p c k t", c=8, k=2)[:, tt // 8, 0,
                                                                            tt % 8:tt % 8 + 1]

    def b_col(tt):
        return T['coltiles'][:].rearrange("p (c k t) -> p c k t", c=8, k=2)[:, tt // 8, 1,
                                                                            tt % 8:tt % 8 + 1]

    wsrc = [('w1T_d', 16, 1024, 0), ('w2T_d', 16, 1024, DFC * D),
            ('w3T_d', 8, 2048, 2 * DFC * D)]
    v_ab = 0
    v_abS_p = [0, 0]
    acc_col = [T['tmpc'][:, 1:2], T['tmpc'][:, 4:5]]
    ti = 0
    for mi, (wd, ntile, wid, off) in enumerate(wsrc):
        for t in range(ntile):
            p = ti % 2
            ti += 1
            v = dma('sync', L_wP[p], scrhP[p][:, 0:wid], T[wd][t * 128:(t + 1) * 128, :],
                    waits=[('s_scm', v_abS_p[p]), ('s_vcast', vcast_h[p])])
            v_abS_p[p] = op('scalar', 's_scm', [v, ('s_vq', v_ab), v_gd_w],
                            lambda e, wid=wid, p=p: e.activation(
                                out=T['scrhb'][:, 0:wid], in_=scrhP[p][:, 0:wid], func=AF.Abs,
                                accum_out=acc_col[p]))
            v_ab = op('vector', 's_vq', [('s_scm', v_abS_p[p])],
                      lambda e, mi=mi, p=p: e.tensor_add(T['wabs'][:, mi:mi + 1],
                                                         T['wabs'][:, mi:mi + 1], acc_col[p]),
                      dep=True)
    v_absF = max(v_abS_p)
    psS, gateS = slotA()
    vmS = op('tensor', 's_mmS', [('s_vq', v_ab), ('s_vcast', v_idb), gateS],
             lambda e, psS=psS: e.matmul(psS[0:1, 0:4], T['onescol32'][:], T['wabs'][:, 0:4],
                                         start=True, stop=True))
    vev = op('scalar', 's_evA', [('s_mmS', vmS)],
             lambda e, psS=psS: e.copy(out=T['wsum_row'][:, 0:4], in_=psS[0:1, 0:4]))
    v_gdr = dma('gpsimd', 'G_wr', T['agwr_in'][:], T['wsum_row'][:], waits=[('s_evA', vev)])
    v_arW = coll("AllReduce", ALU.add, [T['agwr_in'][:]], [T['agwr_out'][:]],
                 [v_gdr])
    v_wsr = dma('gpsimd', 'G_wr', T['wsum_row'][:], T['agwr_out'][:], waits=[('s_coll', v_arW)])
    op('vector', 's_vq', [v_wsr],
       lambda e: e.tensor_scalar_mul(T['wsum_row'][:, 0:4], T['wsum_row'][:, 0:4],
                                     1.0 / (DFF * D)))
    op('vector', 's_vq', [],
       lambda e: e.tensor_scalar_max(T['wsum_row'][:, 0:4], T['wsum_row'][:, 0:4], 1e-5),
       dep=True)
    vW3b = op('vector', 's_vq', [],
              lambda e: e.reciprocal(out=T['wsum_row'][:, 4:8], in_=T['wsum_row'][:, 0:4]),
              dep=True)
    psS2, gateS2 = slotA()
    vmS2 = op('tensor', 's_mmS', [('s_vq', vW3b), gateS2],
              lambda e, psS2=psS2: e.matmul(psS2[:, 0:8],
                                            T['onescol32'][0:1, 0:1].broadcast_to([1, 128]),
                                            T['wsum_row'][:, 0:8], start=True, stop=True))
    vsc = op('scalar', 's_evA', [('s_mmS', vmS2)],
             lambda e, psS2=psS2: e.copy(out=T['swcol'][:], in_=psS2[:, 0:8]))

    v_qd = ('G_wq', 0)
    v_qv_p = [0, 0]
    ti = 0
    for mi, (wd, ntile, wid, off) in enumerate(wsrc):
        for t in range(ntile):
            p = ti % 2
            ti += 1
            v = dma('sync', L_wP[p], scrhP[p][:, 0:wid], T[wd][t * 128:(t + 1) * 128, :],
                    waits=[('s_vq', v_qv_p[p]), ('s_scm', v_absF), ('s_vcast', vcast_h[p])])
            op('vector', 's_vq', [v, ('s_evA', vsc)],
               lambda e, mi=mi, wid=wid, p=p: e.tensor_scalar(
                   out=T['scr32'][:, 0:wid], in0=scrhP[p][:, 0:wid],
                   scalar1=T['swcol'][:, 4 + mi:5 + mi], scalar2=MAGIC,
                   op0=ALU.mult, op1=ALU.add))
            op('vector', 's_vq', [],
               lambda e, wid=wid: e.tensor_scalar_sub(
                   T['scr32'][:, 0:wid], T['scr32'][:, 0:wid], MAGIC))
            for hh in range(0, wid, 1024):
                op('vector', 's_vq', [],
                   lambda e, hh=hh: e.tensor_copy(out=T['scri'][:],
                                                  in_=T['scr32'][:, hh:hh + 1024]))
                op('vector', 's_vq', [v_qd] if hh == 0 else [],
                   lambda e, hh=hh: e.tensor_copy(out=T['scrhb'][:, hh:hh + 1024],
                                                  in_=T['scri'][:]))
            v_qv_p[p] = op('vector', 's_vq', [],
                           lambda e, wid=wid: e.tensor_scalar(
                               out=T['scrhb'][:, 0:wid], in0=T['scrhb'][:, 0:wid],
                               scalar1=1.0, scalar2=-1.0, op0=ALU.min, op1=ALU.max))
            v_qd = dma('gpsimd', 'G_wq',
                       T['agwq_in'].rearrange("(r c) -> r c", c=wid)
                       [(off // wid) + t * 128:(off // wid) + (t + 1) * 128, :],
                       T['scrhb'][:, 0:wid], waits=[('s_vq', v_qv_p[p])])
    v_wqAG = coll("AllGather", ALU.bypass, [T['agwq_in'][:]], [T['agwq_out'][:]],
                  [v_qd])

    op('vector', 's_vstat', [('s_vstat', v_statq)],
       lambda e: e.tensor_mul(T['fcol'][:, 0:8], stv(4), stv(4)), dep=True)
    op('vector', 's_vstat', [],
       lambda e: e.tensor_mul(T['fcol'][:, 0:8], T['fcol'][:, 0:8], stv(3)), dep=True)
    op('vector', 's_vstat', [],
       lambda e: e.tensor_scalar_add(T['fcol'][:, 0:8], T['fcol'][:, 0:8], 1e-6), dep=True)
    vrs0 = op('scalar', 's_scm', [('s_vstat', s.cnt['s_vstat'])],
              lambda e: e.activation(out=T['fcol'][:, 0:8], in_=T['fcol'][:, 0:8],
                                     func=AF.Sqrt))
    op('vector', 's_vstat', [('s_scm', vrs0)],
       lambda e: e.reciprocal(out=T['fcol'][:, 0:8], in_=T['fcol'][:, 0:8]))
    vrs = op('scalar', 's_scm', [('s_vstat', s.cnt['s_vstat'])],
             lambda e: e.nop())
    op('vector', 's_vstat', [('s_scm', vrs)],
       lambda e: e.tensor_mul(T['fcol'][:, 8:16], T['fcol'][:, 0:8], stv(4)), dep=True)
    op('vector', 's_vstat', [],
       lambda e: e.tensor_mul(T['fcol'][:, 8:16], T['fcol'][:, 8:16], stv(7)), dep=True)
    op('vector', 's_vstat', [],
       lambda e: e.tensor_scalar_mul(T['fcol'][:, 8:16], T['fcol'][:, 8:16], 1.0 / 127.0),
       dep=True)
    op('vector', 's_vstat', [('s_evA', vsc)],
       lambda e: e.tensor_scalar_mul(T['fcol'][:, 16:24], T['fcol'][:, 8:16],
                                     T['swcol'][:, 0:1]), dep=True)
    v_fc = op('vector', 's_vstat', [],
              lambda e: e.tensor_scalar_mul(T['fcol'][:, 24:32], T['fcol'][:, 8:16],
                                            T['swcol'][:, 1:2]), dep=True)

    def d1_col(ft):
        return T['fcol'][:, 16 + ft:17 + ft]

    if stage < 2:
        dbg_taps(taps_l2())
        return s

    tokreg2 = [None]
    if me('gpsimd'):
        s.wait(*v_tokb)
        reg2 = eng.alloc_register("tokbase2")
        eng.reg_load(reg2, T['tokb2_s'][0:1, 0:1])
        tokreg2[0] = eng.snap(reg2, min_val=0, max_val=(B * L) // 512 - 2)

    def prep_xq(fh):
        vt_last = 0
        v_prev = 0
        for q in range(4):
            ft = fh * 4 + q
            vxl = dma('sync', 'L_xt', T['xtok'][:], T['xtok_d'][ft * 128:(ft + 1) * 128, :],
                      waits=[('s_vq', v_prev), ('s_vstat', v_statq)])
            op('vector', 's_vq', [vxl, ('s_vstat', v_statq)],
               lambda e, ft=ft: e.tensor_scalar(
                   out=T['scr32'][:, 0:2048], in0=T['xtok'][:],
                   scalar1=T['stc'][:, 2 * 8 + ft:2 * 8 + ft + 1],
                   scalar2=T['stc'][:, 6 * 8 + ft:6 * 8 + ft + 1],
                   op0=ALU.subtract, op1=ALU.mult))
            op('vector', 's_vq', [],
               lambda e: e.tensor_scalar_add(
                   T['scr32'][:, 0:2048], T['scr32'][:, 0:2048], MAGIC))
            op('vector', 's_vq', [],
               lambda e: e.tensor_scalar_sub(
                   T['scr32'][:, 0:2048], T['scr32'][:, 0:2048], MAGIC))
            for hh in range(0, 2048, 1024):
                op('vector', 's_vq', [],
                   lambda e, hh=hh: e.tensor_copy(out=T['scri'][:],
                                                  in_=T['scr32'][:, hh:hh + 1024]))
                v_prev = op('vector', 's_vq',
                            [('s_tpe', vt_last)] if hh == 0 else [],
                            lambda e, hh=hh: e.tensor_copy(
                                out=T['xq_tok'][:, hh:hh + 1024], in_=T['scri'][:]))
            for h in range(2):
                def dstf(q=q, h=h):
                    return T['xq_fm'][:].rearrange("p (k c) -> p k c", k=16)[
                        :, h * 8:(h + 1) * 8, q * 128:(q + 1) * 128]
                vt_last = tp_group(T['xq_tok'], list(range(h * 8, (h + 1) * 8)),
                                   dstf, ('s_vq', v_prev))
        return vt_last

    do_ffn = stage >= 3
    fh_ready = [prep_xq(0) if do_ffn else 0, None]

    ffn_jobs = [(fh, cd, pr) for fh in range(2) for cd in range(16) for pr in range(2)]
    wch_lastmm = []
    job_ix = [0]
    kstep = [0]
    v_h_done = [('G_h0', 0), ('G_h1', 0)]
    v_sil_par = [('s_vF', 0), ('s_vF', 0)]
    v_tmpc_free = [0]
    v_pf_free = [('s_vF', 0), ('s_vF', 0)]
    jstate = {'vsl': [0, 0]}

    def ffn_step():
        fh, cd, pr = ffn_jobs[job_ix[0]]
        ks = kstep[0]
        mi, k = divmod(ks, 16)
        fts = [fh * 4 + pr * 2, fh * 4 + pr * 2 + 1]
        rd, cc2 = cd // 2, cd % 2
        moff = mi * DFC * D
        src = T['agwq_out'].rearrange("(r e) -> r e", r=NCORE)[rd] \
            .rearrange("(q c) -> q c", c=1024)[moff // 1024 + k * 128:
                                               moff // 1024 + (k + 1) * 128,
                                               cc2 * 512:(cc2 + 1) * 512]
        wsem = 's_mmA' if mi == 0 else 's_mmC'
        wchg = [wch_lastmm[-2]] if len(wch_lastmm) >= 2 else []
        vload = dma('sync' if ks % 2 == 0 else 'scalar', 'L_wc%d' % (ks % 2),
                    T['wch'][ks % 2][:], src,
                    waits=[('s_coll', v_wqAG)] + wchg)
        vm = 0
        for q, ft in enumerate(fts):
            g0 = []
            if k == 0:
                g0 = [('s_tpe', fh_ready[fh]),
                      v_pf_free[q] if mi == 0 else ('s_scm', jstate['vsl'][q])]
            vm = op('tensor', wsem, [vload] + g0,
                    lambda e, q=q, ks=ks, k=k, ft=ft: e.matmul(
                        T['pF'][:, q * 512:(q + 1) * 512], xqfm(k, ft % 4),
                        T['wch'][ks % 2][:], start=(k == 0), stop=(k == 15)))
        wch_lastmm.append((wsem, vm))
        jp = (job_ix[0] % 2) * 1024
        if k == 15 and mi == 0:
            vsl2 = []
            for q, ft in enumerate(fts):
                vsl2.append(op('scalar', 's_scm',
                               [(wsem, vm), ('s_vstat', v_fc), v_sil_par[job_ix[0] % 2]],
                               lambda e, q=q, ft=ft, jp=jp: e.activation(
                                   out=T['silu1'][:, jp + q * 512:jp + (q + 1) * 512],
                                   in_=T['pF'][:, q * 512:(q + 1) * 512],
                                   func=AF.Silu, scale=d1_col(ft))))
            jstate['vsl'] = vsl2
        if k == 15 and mi == 1:
            for q, ft in enumerate(fts):
                vh = op('vector', 's_vF', [(wsem, vm), ('s_scm', jstate['vsl'][q]),
                                           v_h_done[q]],
                        lambda e, q=q, jp=jp: e.tensor_mul(
                            T['hchunk'][q][:], T['silu1'][:, jp + q * 512:jp + (q + 1) * 512],
                            T['pF'][:, q * 512:(q + 1) * 512]))
                v_pf_free[q] = ('s_vF', vh)
                v_sil_par[job_ix[0] % 2] = ('s_vF', vh)
                vsq2 = op('scalar', 's_scm', [('s_vF', vh), ('s_vh', v_tmpc_free[0])],
                          lambda e, q=q: e.activation(
                              out=T['scrhb'][:, 0:512], in_=T['hchunk'][q][:],
                              func=AF.Square, accum_out=T['tmpc'][:, 2:3]))
                op('vector', 's_vh', [('s_scm', vsq2)],
                   lambda e, ft=ft: e.tensor_add(T['hps'][:, ft:ft + 1],
                                                 T['hps'][:, ft:ft + 1], T['tmpc'][:, 2:3]),
                   dep=True)
                vrm = op('vector', 's_vh', [],
                         lambda e, q=q: e.reduce_max(out=T['tmpc'][:, 3:4],
                                                     in_=T['hchunk'][q][:],
                                                     axis=AX.X, apply_absolute_value=True),
                         dep=True)
                v_tmpc_free[0] = op('vector', 's_vh', [('s_vh', vrm)],
                                    lambda e, ft=ft: e.tensor_max(
                                        T['hpm'][:, ft:ft + 1], T['hpm'][:, ft:ft + 1],
                                        T['tmpc'][:, 3:4]))
                v_h_done[q] = dma('gpsimd', 'G_h%d' % q,
                                  T['h_d'][ft * 128:(ft + 1) * 128, cd * 512:(cd + 1) * 512],
                                  T['hchunk'][q][:], waits=[('s_vh', v_tmpc_free[0])])
        kstep[0] += 1
        if kstep[0] == 32:
            kstep[0] = 0
            job_ix[0] += 1

    def ffn_pump(n, fh_cap):
        if not do_ffn:
            return
        for _ in range(n):
            if job_ix[0] >= len(ffn_jobs) or ffn_jobs[job_ix[0]][0] > fh_cap:
                return
            ffn_step()

    v_gy_hist = [('G_gy0', 0), ('G_gy1', 0)]
    ixg = 0
    v_xn_hist = []
    v_tp_hist = []
    win_mark = None
    for w in range(nwin):
        bat, wb = w // WPB, w % WPB
        tts = [w * 4 + q for q in range(4)]
        if w == 8 and do_ffn:
            fh_ready[1] = prep_xq(1)
        if wstage < 1:
            continue

        v_tp_last = 0
        for q, tt in enumerate(tts):
            gq = []
            if len(v_xn_hist) >= 1:
                gq.append(('s_xn', v_xn_hist[-1]))
            v = dma('sync', 'L_x', T['xg'][0][:], T['x_all_d'][tt * 128:(tt + 1) * 128, :],
                    waits=gq)
            if wstage == 11:
                continue
            gtp = [('s_tpe', v_tp_hist[-1])] if len(v_tp_hist) >= 1 else []
            vx = op('scalar', 's_xn', [v, v_rb] + gtp,
                    lambda e, tt=tt: e.activation(
                        out=T['xn_tok'][0][:], in_=T['xg'][0][:], func=AF.Identity,
                        scale=r_col(tt), bias=b_col(tt)))
            v_xn_hist.append(vx)
            if wstage == 12:
                ixg += 1
                continue
            for h in range(2):
                def dstf(q=q, h=h):
                    return T['xn_fm'][:].rearrange("p (k c) -> p k c", k=16)[
                        :, h * 8:(h + 1) * 8, q * 128:(q + 1) * 128]
                v_tp_last = tp_group(T['xn_tok'][0], list(range(h * 8, (h + 1) * 8)),
                                     dstf, ('s_xn', vx))
            v_tp_hist.append(v_tp_last)
            ixg += 1

        pg = []
        if win_mark is not None:
            pg = [('s_vgy', win_mark['s_vgy']), ('s_mmB', win_mark['s_mmB']),
                  ('s_mmC', win_mark['s_mmC']), ('s_dA', win_mark['s_dA']),
                  ('s_vbt', win_mark['s_vbt'])]

        if wstage < 2:
            continue
        vevA_o = {}
        for o in range(10):
            ps, gate = slotA()
            vm = 0
            for k in range(16):
                g0 = [('s_tpe', v_tp_last), gate] if k == 0 else []
                vm = op('tensor', 's_mmA' if k == 15 else 's_mmS', g0,
                        lambda e, ps=ps, o=o, k=k: e.matmul(
                            ps, winT(k, o), xnfm(k), start=(k == 0), stop=(k == 15)))
            if o < 4:
                vevA_o[o] = op('scalar', 's_evA', [('s_mmA', vm)] + pg,
                               lambda e, ps=ps, o=o: e.activation(
                                   out=T['z_sil'][:, o * 512:(o + 1) * 512], in_=ps,
                                   func=AF.Silu))
            elif o == 4:
                vevA_o[o] = op('scalar', 's_evA', [('s_mmA', vm)] + pg,
                               lambda e, ps=ps: e.copy(out=T['B_fm'][:], in_=ps))
            elif o < 9:
                vevA_o[o] = op('scalar', 's_evA', [('s_mmA', vm)] + pg,
                               lambda e, ps=ps, o=o: e.copy(
                                   out=T['C_fm'][:, (o - 5) * 512:(o - 4) * 512], in_=ps))
            else:
                vevA_o[o] = op('scalar', 's_evA', [('s_mmA', vm)] + pg,
                               lambda e, ps=ps: e.copy(out=T['dtl_fm'][:], in_=ps))

        if wstage < 3:
            continue
        vdelta = {}
        for j in range(NJ):
            ps, gate = slotA()
            vm = op('tensor', 's_mmA', [('s_evA', vevA_o[9]), gate],
                    lambda e, ps=ps, j=j: e.matmul(
                        ps, T['wdtT'][:, j * 128:(j + 1) * 128], T['dtl_fm'][:],
                        start=True, stop=True))
            ve1 = op('scalar', 's_scm', [('s_mmA', vm)],
                     lambda e, ps=ps, j=j: e.activation(
                         out=T['e_tmp'][:], in_=ps, func=AF.Exp, bias=T['bdt2c'][:, j:j + 1]))
            vdelta[j] = op('scalar', 's_evA', [('s_scm', ve1)] + pg,
                           lambda e, j=j: e.activation(
                               out=T['delta'][:, j * 512:(j + 1) * 512], in_=T['e_tmp'][:],
                               func=AF.Ln, bias=1.0))

        if wstage < 4:
            continue
        v_du = 0
        for j in range(NJ):
            ps, gate = slotA()
            vm = op('tensor', 's_mmA', [('s_evA', vevA_o[4]), v_gconsts, gate],
                    lambda e, ps=ps, j=j: e.matmul(
                        ps, T['indU'][:].rearrange("p (a c) -> p a c", a=4)[:, j, :],
                        T['B_fm'][:], start=True, stop=True))
            base = j * 516
            vt1 = op('vector', 's_vmisc', [],
                     lambda e, j=j, base=base: e.tensor_copy(
                         out=T['utail'][:, j * 4:j * 4 + 3],
                         in_=T['u_in'][:, base + 512:base + 515]))
            vev2 = op('scalar', 's_evA', [('s_mmA', vm), ('s_vmisc', vt1)] + pg,
                      lambda e, ps=ps, base=base: e.copy(
                          out=T['u_in'][:, base + 3:base + 515], in_=ps))
            if wb == 0:
                op('vector', 's_vmisc', [('s_evA', vev2)],
                   lambda e, base=base: e.memset(T['u_in'][:, base:base + 3], 0.0))
            else:
                op('vector', 's_vmisc', [('s_evA', vev2)],
                   lambda e, j=j, base=base: e.tensor_copy(
                       out=T['u_in'][:, base:base + 3], in_=T['utail'][:, j * 4:j * 4 + 3]))
            op('vector', 's_vmisc', [],
               lambda e, j=j, base=base: e.tensor_scalar_mul(
                   T['cacc'][:], T['u_in'][:, base:base + 512],
                   T['convwc'][:, j * 4:j * 4 + 1]), dep=True)
            for tp in range(1, 4):
                op('vector', 's_vmisc', [],
                   lambda e, j=j, base=base, tp=tp: e.scalar_tensor_tensor(
                       out=T['cacc'][:], in0=T['u_in'][:, base + tp:base + tp + 512],
                       scalar=T['convwc'][:, j * 4 + tp:j * 4 + tp + 1], in1=T['cacc'][:],
                       op0=ALU.mult, op1=ALU.add))
            vcv = s.cnt['s_vmisc']
            vu = op('scalar', 's_scm', [('s_vmisc', vcv)] + pg,
                    lambda e, j=j: e.activation(
                        out=T['u_exp'][:, j * 512:(j + 1) * 512], in_=T['cacc'][:],
                        func=AF.Silu, bias=T['convbc'][:, j:j + 1]))
            v_du = op('vector', 's_vmisc', [('s_evA', vdelta[j]), ('s_scm', vu)],
                      lambda e, j=j: e.tensor_mul(
                          T['du'][:, j * 512:(j + 1) * 512],
                          T['delta'][:, j * 512:(j + 1) * 512],
                          T['u_exp'][:, j * 512:(j + 1) * 512]))

        if wstage < 5:
            continue

        def emit_B(jn):
            psb, gateB = slotB()
            vmb = op('tensor', 's_mmB', [('s_evA', vevA_o[4]), v_gconsts, gateB],
                     lambda e, psb=psb, jn=jn: e.matmul(
                         psb, T['indB'][:, jn:jn + 1].broadcast_to([128, 128]),
                         T['B_fm'][:], start=True, stop=True))
            ib = s.cnt['s_evB']
            vbe = op('scalar', 's_evB',
                     [('s_mmB', vmb), ('s_vbt', max(0, s.cnt['s_vbt'] - 1))],
                     lambda e, psb=psb, ib=ib: e.copy(out=T['Bb'][ib % 2][:], in_=psb))
            return (vbe, ib)

        def emit_DA(jn):
            jj = jn // DST
            ida = s.cnt['s_dA']
            vda = op('scalar', 's_dA',
                     [('s_evA', vdelta[jj]), ('s_vscan', max(0, s.cnt['s_vscan'] - 1))],
                     lambda e, jj=jj, jn=jn, ida=ida: e.activation(
                         out=T['dAt'][ida % 2][:], in_=T['delta'][:, jj * 512:(jj + 1) * 512],
                         func=AF.Exp,
                         scale=T['A_t'][:, jn:jn + 1]))
            return (vda, ida)

        preDA = emit_DA(0)
        preB = emit_B(0)
        for j in range(NJ):
            for n in range(DST):
                jn = j * DST + n
                vda, ida = preDA
                vbe, ib = preB
                psc, gateC = slotC()
                vmc = op('tensor', 's_mmC', [('s_evA', vevA_o[5 + j]), v_gconsts, gateC],
                        lambda e, psc=psc, n=n, j=j: e.matmul(
                            psc, T['indC'][:].rearrange("p (a c) -> p a c", a=16)[:, n, :],
                            T['C_fm'][:, j * 512:(j + 1) * 512], start=True, stop=True))
                ibt = s.cnt['s_vbt']
                vb2 = op('vector', 's_vbt', [('s_evB', vbe), ('s_vmisc', v_du)],
                         lambda e, j=j, ib=ib, ibt=ibt: e.tensor_mul(
                             T['bt'][ibt % 2][:], T['du'][:, j * 512:(j + 1) * 512],
                             T['Bb'][ib % 2][:]))
                init = 0.0 if wb == 0 else T['carry'][:, jn:jn + 1]
                isc = s.cnt['s_vscan']
                vscn = op('vector', 's_vscan', [('s_dA', vda)],
                          lambda e, ida=ida, ibt=ibt, isc=isc, init=init: e.tensor_tensor_scan(
                              out=T['st_s'][isc % 2][:], data0=T['dAt'][ida % 2][:],
                              data1=T['bt'][ibt % 2][:], initial=init,
                              op0=ALU.mult, op1=ALU.add))
                op('vector', 's_vmisc', [('s_vscan', vscn)],
                   lambda e, jn=jn, isc=isc: e.tensor_copy(
                       out=T['carry'][:, jn:jn + 1], in_=T['st_s'][isc % 2][:, WN - 1:WN]))
                op('vector', 's_vC', [('s_mmC', vmc), ('s_vscan', vscn)],
                   lambda e, psc=psc, n=n, isc=isc: e.tensor_mul(
                       T['prods'][:, (n % 8) * 512:(n % 8 + 1) * 512], psc,
                       T['st_s'][isc % 2][:]))
                if jn < NJ * DST - 1:
                    preDA = emit_DA(jn + 1)
                    preB = emit_B(jn + 1)
                ffn_pump(0 if w < 2 else (3 if w < 8 else 2), 0 if w < 8 else 1)
                if n % 8 == 7:
                    for stp, cnt2 in ((1, 4), (2, 2), (4, 1)):
                        for q2 in range(cnt2):
                            a0 = (q2 * 2 * stp) * 512
                            a1 = (q2 * 2 * stp + stp) * 512
                            op('vector', 's_vmisc', [],
                               lambda e, a0=a0, a1=a1: e.tensor_add(
                                   T['prods'][:, a0:a0 + 512], T['prods'][:, a0:a0 + 512],
                                   T['prods'][:, a1:a1 + 512]))
                    if n == 7:
                        op('vector', 's_vmisc', [],
                           lambda e: e.tensor_copy(out=T['yhalf'][:],
                                                   in_=T['prods'][:, 0:512]))
                    else:
                        op('vector', 's_vmisc', [],
                           lambda e: e.tensor_add(T['ytmp'][:], T['prods'][:, 0:512],
                                                  T['yhalf'][:]))
            op('vector', 's_vmisc', [],
               lambda e, j=j: e.scalar_tensor_tensor(
                   out=T['ytmp'][:], in0=T['u_exp'][:, j * 512:(j + 1) * 512],
                   scalar=T['dparc'][:, j:j + 1], in1=T['ytmp'][:],
                   op0=ALU.mult, op1=ALU.add))
            igy = s.cnt['s_vgy']
            vgy = op('vector', 's_vgy',
                     [('s_evA', vevA_o[j]), v_gy_hist[igy % 2]],
                     lambda e, j=j, igy=igy: e.tensor_mul(
                         T['gy_t'][igy % 2][:], T['ytmp'][:],
                         T['z_sil'][:, j * 512:(j + 1) * 512]))
            v_gy_hist[igy % 2] = dma('gpsimd', 'G_gy%d' % (igy % 2),
                            T['gy_in_d'][j * 128:(j + 1) * 128,
                                         bat * L + wb * WN:bat * L + wb * WN + WN],
                            T['gy_t'][igy % 2][:], waits=[('s_vgy', vgy)])

        win_mark = {k: s.cnt[k] for k in ('s_vgy', 's_mmB', 's_mmC', 's_dA', 's_vbt')}

    if do_ffn and fh_ready[1] is None:
        fh_ready[1] = prep_xq(1)
    while do_ffn and job_ix[0] < 64:
        ffn_step()
    if stage < 4:
        dbg_taps(taps_l2() + [('d_gy', T['gy_in_d'][:]), ('d_h', T['h_d'][:])])
        return s

    v_gyAG = coll("AllGather", ALU.bypass, [T['gy_in_d'][:]], [T['gy_all_d'][:]],
                  [v_gy_hist[0], v_gy_hist[1]])

    v_hs = s.cnt['s_vh']
    op('vector', 's_vh', [('s_vh', v_hs), v_h_done[0], v_h_done[1]],
       lambda e: e.reciprocal(out=T['fcol'][:, 32:40], in_=T['hpm'][:]), dep=True)
    op('vector', 's_vh', [],
       lambda e: e.tensor_scalar_mul(T['fcol'][:, 32:40], T['fcol'][:, 32:40], 127.0),
       dep=True)
    op('vector', 's_vh', [],
       lambda e: e.tensor_mul(T['fcol'][:, 40:48], T['fcol'][:, 24:32], T['fcol'][:, 24:32]),
       dep=True)
    op('vector', 's_vh', [],
       lambda e: e.tensor_mul(T['fcol'][:, 40:48], T['fcol'][:, 40:48], T['hps'][:]),
       dep=True)
    op('vector', 's_vh', [],
       lambda e: e.tensor_scalar_mul(T['fcol'][:, 40:48], T['fcol'][:, 40:48],
                                     1.0 / DFF), dep=True)
    v_pre = op('vector', 's_vh', [],
               lambda e: e.tensor_scalar_add(T['fcol'][:, 40:48], T['fcol'][:, 40:48],
                                             1e-6), dep=True)
    vrs2a = op('scalar', 's_scm', [('s_vh', v_pre)],
               lambda e: e.activation(out=T['fcol'][:, 40:48], in_=T['fcol'][:, 40:48],
                                      func=AF.Sqrt))
    op('vector', 's_vh', [('s_scm', vrs2a)],
       lambda e: e.reciprocal(out=T['fcol'][:, 40:48], in_=T['fcol'][:, 40:48]))
    vrs2 = op('scalar', 's_scm', [('s_vh', s.cnt['s_vh'])],
              lambda e: e.nop())
    op('vector', 's_vh', [('s_scm', vrs2)],
       lambda e: e.tensor_mul(T['fcol'][:, 48:56], T['fcol'][:, 40:48], T['fcol'][:, 24:32]),
       dep=True)
    op('vector', 's_vh', [],
       lambda e: e.tensor_mul(T['fcol'][:, 48:56], T['fcol'][:, 48:56], T['hpm'][:]),
       dep=True)
    op('vector', 's_vh', [],
       lambda e: e.tensor_scalar_mul(T['fcol'][:, 48:56], T['fcol'][:, 48:56], 1.0 / 127.0),
       dep=True)
    v_dq3 = op('vector', 's_vh', [],
               lambda e: e.tensor_scalar_mul(T['fcol'][:, 48:56], T['fcol'][:, 48:56],
                                             T['swcol'][:, 2:3]), dep=True)

    def dq3_col(ft):
        return T['fcol'][:, 48 + ft:49 + ft]

    v_xqh_done = ('G_xq', 0)
    v_hq = 0
    for ft in range(NTC):
        for hc in range(4):
            v = dma('sync', 'L_h', T['scr32'][:, 0:2048],
                    T['h_d'][ft * 128:(ft + 1) * 128, hc * 2048:(hc + 1) * 2048],
                    waits=[('s_vh', v_hq), v_h_done[0], v_h_done[1]])
            op('vector', 's_vh', [v, ('s_vh', v_dq3)],
               lambda e, ft=ft: e.tensor_scalar(
                   out=T['scr32'][:, 0:2048], in0=T['scr32'][:, 0:2048],
                   scalar1=T['fcol'][:, 32 + ft:33 + ft], scalar2=MAGIC,
                   op0=ALU.mult, op1=ALU.add))
            op('vector', 's_vh', [],
               lambda e: e.tensor_scalar_sub(
                   T['scr32'][:, 0:2048], T['scr32'][:, 0:2048], MAGIC))
            for hh in range(0, 2048, 1024):
                op('vector', 's_vh', [],
                   lambda e, hh=hh: e.tensor_copy(out=T['scri'][:],
                                                  in_=T['scr32'][:, hh:hh + 1024]))
                v_hq = op('vector', 's_vh', [v_xqh_done] if hh == 0 else [],
                          lambda e, hh=hh: e.tensor_copy(out=T['xq_tok'][:, hh:hh + 1024],
                                                         in_=T['scri'][:]))
            vev3 = 0
            for h in range(2):
                def dstf(h=h):
                    return T['xqh_w'][:, h * 1024:(h + 1) * 1024]
                vev3 = tp_group(T['xq_tok'], list(range(h * 8, (h + 1) * 8)),
                                dstf, ('s_vh', v_hq))
            v_xqh_done = dma('gpsimd', 'G_xq',
                             T['xqh_d'].rearrange("(f p k c) -> f p k c", f=NTC, k=64, p=128)
                             [ft][:, hc * 16:(hc + 1) * 16, :],
                             T['xqh_w'][:].rearrange("p (k c) -> p k c", k=16),
                             waits=[('s_tpe', vev3)])

    if stage < 5:
        dbg_taps(taps_l2() + [('d_gy', T['gy_in_d'][:]), ('d_h', T['h_d'][:])])
        return s

    v_gw = ('G_g', 0)
    l3A = []
    l3B = []
    v_zfree = s.cnt['s_vgy']
    for fp in range(4):
        fts = [fp * 2, fp * 2 + 1]
        lb3w = [v_xqh_done]
        if fp == 0:
            lb3w += [('s_mmA', s.cnt['s_mmA']), ('s_mmS', s.cnt['s_mmS']),
                     ('s_mmB', s.cnt['s_mmB']), ('s_vgy', v_zfree)]
        else:
            lb3w += [('s_mmA', l3A[-1]), ('s_mmB', l3B[-1])]
        v_l3b = 0
        for q in range(2):
            v_l3b = dma('scalar', 'G_l3%d' % q,
                        T['winT'][:].rearrange("p (k c) -> p k c", c=128)
                        [:, q * 64:(q + 1) * 64, :],
                        T['xqh_d'].rearrange("(f p k c) -> f p k c", f=NTC, k=64, p=128)
                        [fts[q]], waits=lb3w)
        v_l3b = [('G_l30', s.cnt['G_l30']), ('G_l31', s.cnt['G_l31'])]
        for mh in range(2):
            s.iA += 2
            gA = ('s_evA', s.iA - 2)
            s.iB += 2
            gB = ('s_evB', s.iB - 2)
            vmmA = vmmB = 0
            for k in range(64):
                rdk, lk = k // 8, k % 8
                src = T['agwq_out'].rearrange("(r e) -> r e", r=NCORE)[rdk] \
                    .rearrange("(q c) -> q c", c=2048)[(2 * DFC * D) // 2048 + lk * 128:
                                                       (2 * DFC * D) // 2048 + (lk + 1) * 128,
                                                       mh * 1024:(mh + 1) * 1024]
                g3 = ([('s_mmA', l3A[-2]), ('s_mmB', l3B[-2])]
                      if len(l3A) >= 2 else [('s_vgy', v_zfree)])
                vload = dma('sync', 'L_w3%d' % (k % 2), T['w3ch'][k % 2], src, waits=g3)
                for hx in range(2):
                    vmmA = op('tensor', 's_mmA',
                              [vload, gA] + v_l3b if (k == 0 and hx == 0)
                              else [vload],
                              lambda e, k=k, hx=hx: e.matmul(
                                  T['pA'][:, hx * 512:(hx + 1) * 512],
                                  T['winT'][:, k * 128:(k + 1) * 128],
                                  T['w3ch'][k % 2][:, hx * 512:(hx + 1) * 512],
                                  start=(k == 0), stop=(k == 63)))
                    vmmB = op('tensor', 's_mmB',
                              [gB, v_pf_free[0], v_pf_free[1]]
                              if (k == 0 and hx == 0) else [],
                              lambda e, k=k, hx=hx: e.matmul(
                                  T['pF'][:, hx * 512:(hx + 1) * 512],
                                  T['winT'][:, 8192 + k * 128:8192 + (k + 1) * 128],
                                  T['w3ch'][k % 2][:, hx * 512:(hx + 1) * 512],
                                  start=(k == 0), stop=(k == 63)))
                l3A.append(vmmA)
                l3B.append(vmmB)
            for q, (psname, msem, vmm, esem) in enumerate(
                    [('pA', 's_mmA', vmmA, 's_evA'), ('pF', 's_mmB', vmmB, 's_evB')]):
                for half in range(2):
                    op('scalar', esem, [(msem, vmm), ('s_vh', v_dq3),
                                        v_gw],
                       lambda e, psname=psname, half=half, ft=fts[q], mh=mh: e.activation(
                           out=T['g_t'][:, (ft % 2) * 2048 + mh * 1024 + half * 512:
                                         (ft % 2) * 2048 + mh * 1024 + (half + 1) * 512],
                           in_=T[psname][:, half * 512:(half + 1) * 512],
                           func=AF.Silu, scale=dq3_col(fts[q])))
        for q, ft in enumerate(fts):
            v_gw = dma('gpsimd', 'G_g', T['g_d'][ft * 128:(ft + 1) * 128, :],
                       T['g_t'][:, (ft % 2) * 2048:(ft % 2) * 2048 + 2048],
                       waits=[('s_evA', s.cnt['s_evA']), ('s_evB', s.cnt['s_evB'])])

    if stage < 6:
        dbg_taps(taps_l2() + [('d_gy', T['gy_in_d'][:]), ('d_h', T['h_d'][:]),
                              ('d_g', T['g_d'][:])])
        return s

    banks = [T['pA'][:, 0:512], T['pA'][:, 512:1024],
             T['pF'][:, 0:512], T['pF'][:, 512:1024]]
    v_otmp = [('s_vC', 0)] * 4
    v_tg_mm = 0
    nxg = [0]
    for tg in range(2):
        glv = 0
        for ka in range(4):
            glv = s.bump('G_gyl', 16)
            s.checks.append(('G_gyl', glv))
            if me('gpsimd'):
                if ka == 0:
                    s.wait('s_coll', v_gyAG)
                    if tg == 0:
                        s.wait('s_mmA', s.cnt['s_mmA'])
                        s.wait('s_mmS', s.cnt['s_mmS'])
                        s.wait('s_mmC', s.cnt['s_mmC'])
                    else:
                        s.wait('s_mmC', v_tg_mm)
                tbv = tokreg2[0] + tg
                eng.dma_start(
                    out=T['winT'][:].rearrange("p (k c) -> p k c", k=40)
                    [:, ka * 8:(ka + 1) * 8, :],
                    in_=T['gy_all_d'].rearrange("(k p) (tb c) -> p k tb c",
                                                p=128, c=512)
                    [:, ka * 8:(ka + 1) * 8, tbv, :]).then_inc(S['G_gyl'], 16)
        v_gyl = ('G_gyl', glv)
        for mc in range(4):
            vload = 0
            vm = 0
            for k in range(32):
                g4 = [wch_lastmm[-4]] if len(wch_lastmm) >= 4 else []
                if tg == 0 and mc == 0 and k == 0:
                    g4 += [('s_mmA', s.cnt['s_mmA']), ('s_mmB', s.cnt['s_mmB']),
                           ('s_mmC', s.cnt['s_mmC'])]
                vload = dma('sync' if k % 2 == 0 else 'scalar', 'L_wc%d' % (k % 4),
                            T['wch4'][k % 4][:],
                            T['woutTb_d'][k * 128:(k + 1) * 128, mc * 512:(mc + 1) * 512],
                            waits=g4 + [v_gd_w])
                for q in range(4):
                    gq = []
                    if k == 0:
                        gq = [v_gyl, v_otmp[q]]
                        if tg == 0 and mc == 0:
                            gq += [('s_evA', s.cnt['s_evA']), ('s_evB', s.cnt['s_evB'])]
                    vm = op('tensor', 's_mmC', [vload] + gq,
                            lambda e, q=q, k=k: e.matmul(
                                banks[q],
                                T['winT'][:, k * 512 + q * 128:k * 512 + (q + 1) * 128],
                                T['wch4'][k % 4][:], start=(k == 0), stop=(k == 31)))
                wch_lastmm.append(('s_mmC', vm))
            v_tg_mm = vm
            for q in range(4):
                tt = tg * 4 + q
                ix = nxg[0] % 2
                nxg[0] += 1
                vxr = dma('sync', 'L_xo%d' % ix, T['xch'][ix][:],
                          T['xtok_d'][tt * 128:(tt + 1) * 128, mc * 512:(mc + 1) * 512],
                          waits=[('s_vout', max(0, s.cnt['s_vout'] - 1))])
                vgr = dma('gpsimd', 'G_gr%d' % ix, T['gch'][ix][:],
                          T['g_d'][tt * 128:(tt + 1) * 128, mc * 512:(mc + 1) * 512],
                          waits=[v_gw, ('s_vC', max(0, s.cnt['s_vC'] - 1))])
                v_otmp[q] = ('s_vC', op('vector', 's_vC', [('s_mmC', vm), vgr],
                                        lambda e, q=q, ix=ix: e.tensor_mul(
                                            T['otmp'][ix], banks[q], T['gch'][ix][:])))
                so = 's_out%d' % ix
                vo = op('vector', 's_vout', [vxr, (so, s.cnt[so])],
                        lambda e, ix=ix: e.tensor_add(
                            T['out_t'][ix], T['otmp'][ix], T['xch'][ix][:]))
                v_out = s.bump(so, 16)
                s.checks.append((so, v_out))
                if me('gpsimd'):
                    s.wait('s_vout', vo)
                    eng.dma_start(out=T['out_d'][tt * 128:(tt + 1) * 128,
                                                 mc * 512:(mc + 1) * 512],
                                  in_=T['out_t'][ix]).then_inc(S[so], 16)

    if T.get('dbg'):
        dma('gpsimd', 'G_dbg', T['d_gy'][:], T['gy_in_d'][:],
            waits=[('s_vout', s.cnt['s_vout'])])
        dma('gpsimd', 'G_dbg', T['d_agst'][:], T['agst_in'][:])
        dma('gpsimd', 'G_dbg', T['d_h'][:], T['h_d'][:])
        dma('gpsimd', 'G_dbg', T['d_g'][:], T['g_d'][:])
        dma('gpsimd', 'G_dbg', T['d_wq'][:], T['agwq_in'][:])
        dma('gpsimd', 'G_dbg', T['d_stc'][:], T['stc'][:])
        v_dbg = dma('gpsimd', 'G_dbg', T['d_fcol'][:], T['fcol'][:])
        if me('gpsimd'):
            s.wait(*v_dbg)
    s.wait('s_out0', s.cnt['s_out0'])
    s.wait('s_out1', s.cnt['s_out1'])
    return s


_cache = {}


def _build(sim=False, stage=99, nwin=NWIN, wstage=99, dbg=False):
    nc = bass.Bass("TRN2", target_bir_lowering=False)
    dt = nc.dram_tensor
    T = {}
    T['x_all_d'] = dt("x_all", [B * L, D], F32, kind="ExternalInput").ap()
    T['xtok_d'] = dt("x_tok", [TOK, D], F32, kind="ExternalInput").ap()
    T['winT_d'] = dt("winT", [D, 1280], F32, kind="ExternalInput").ap()
    T['wdtT_d'] = dt("wdtT", [128, 512], F32, kind="ExternalInput").ap()
    T['woutT_d'] = dt("woutT", [DIN, D], F32, kind="ExternalInput").ap()
    T['w1T_d'] = dt("w1T", [D, DFC], F32, kind="ExternalInput").ap()
    T['w2T_d'] = dt("w2T", [D, DFC], F32, kind="ExternalInput").ap()
    T['w3T_d'] = dt("w3T", [DFC, D], F32, kind="ExternalInput").ap()
    T['bdt_d'] = dt("bdt", [IC], F32, kind="ExternalInput").ap()
    T['convw_d'] = dt("convw", [IC, 4], F32, kind="ExternalInput").ap()
    T['convb_d'] = dt("convb", [IC], F32, kind="ExternalInput").ap()
    T['alog_d'] = dt("alog", [IC, DST], F32, kind="ExternalInput").ap()
    T['dpar_d'] = dt("dpar", [IC], F32, kind="ExternalInput").ap()
    T['ident_d'] = dt("ident", [128, 128], F32, kind="ExternalInput").ap()
    T['indB_d'] = dt("indB", [128, 64], F32, kind="ExternalInput").ap()
    T['indC_d'] = dt("indC", [16, 128, 128], F32, kind="ExternalInput").ap()
    T['indU_d'] = dt("indU", [4, 128, 128], F32, kind="ExternalInput").ap()
    T['tokb_d'] = dt("tokbase", [1, 1], U32, kind="ExternalInput").ap()
    T['tokb2_d'] = dt("tokbase2", [1, 1], U32, kind="ExternalInput").ap()
    T['out_d'] = dt("out", [TOK, D], F32, kind="ExternalOutput").ap()
    T['woutTb_d'] = dt("woutTb", [DIN, D], BF16, kind="Internal").ap()
    T['gy_in_d'] = dt("gy_in", [IC, B * L], BF16, kind="Internal").ap()
    T['gy_all_d'] = dt("gy_all", [DIN, B * L], BF16, kind="Internal").ap()
    T['agst_in'] = dt("agst_in", [2, TOK], F32, kind="Internal").ap()
    T['agst_out'] = dt("agst_out", [NCORE, 2 * TOK], F32, kind="Internal").ap()
    T['agwr_in'] = dt("agwr_in", [1, 8], F32, kind="Internal").ap()
    T['agwr_out'] = dt("agwr_out", [1, 8], F32, kind="Internal").ap()
    T['agwq_in'] = dt("agwq_in", [3 * DFC * D], BF16, kind="Internal").ap()
    T['agwq_out'] = dt("agwq_out", [NCORE * 3 * DFC * D], BF16, kind="Internal").ap()
    T['h_d'] = dt("h_scr", [TOK, DFF], F32, kind="Internal").ap()
    T['xqh_d'] = dt("xqh_scr", [NTC * 64 * 128 * 128], BF16, kind="Internal").ap()
    T['g_d'] = dt("g_scr", [TOK, D], BF16, kind="Internal").ap()
    if dbg:
        T['d_gy'] = dt("d_gy", [IC, B * L], BF16, kind="ExternalOutput").ap()
        T['d_agst'] = dt("d_agst", [2, TOK], F32, kind="ExternalOutput").ap()
        T['d_h'] = dt("d_h", [TOK, DFF], F32, kind="ExternalOutput").ap()
        T['d_g'] = dt("d_g", [TOK, D], BF16, kind="ExternalOutput").ap()
        T['d_wq'] = dt("d_wq", [3 * DFC * D], BF16, kind="ExternalOutput").ap()
        T['d_stc'] = dt("d_stc", [128, 64], F32, kind="ExternalOutput").ap()
        T['d_fcol'] = dt("d_fcol", [128, 64], F32, kind="ExternalOutput").ap()
    T['dbg'] = dbg

    from contextlib import ExitStack
    ctx = ExitStack()
    with ctx:
        def sb(name, shape, dtp):
            return ctx.enter_context(nc.sbuf_tensor(name, shape, dtp))

        T['winT'] = sb("winT_s", [128, 16 * 10 * 128], BF16)
        T['wdtT'] = sb("wdtT_s", [128, 512], BF16)
        T['identb'] = sb("identb_s", [128, 128], BF16)
        T['indB'] = sb("indB_s", [128, 64], BF16)
        T['indC'] = sb("indC_s", [128, 16 * 128], BF16)
        T['indU'] = sb("indU_s", [128, 4 * 128], BF16)
        T['bdtc'] = sb("bdtc", [128, 4], F32)
        T['bdt2c'] = sb("bdt2c", [128, 4], F32)
        T['convbc'] = sb("convbc", [128, 4], F32)
        T['dparc'] = sb("dparc", [128, 4], F32)
        T['convwc'] = sb("convwc", [128, 16], F32)
        T['alogs'] = sb("alogs", [128, 64], F32)
        T['A_t'] = sb("A_t", [128, 64], F32)
        T['tokb_s'] = sb("tokb_s", [1, 1], U32)
        T['tokb2_s'] = sb("tokb2_s", [1, 1], U32)
        T['scr32'] = sb("scr32", [128, 2048], F32)
        T['scri'] = sb("scri", [128, 1024], I32)
        T['scrh'] = sb("scrh", [128, 2048], F32)
        T['scrh2'] = sb("scrh2", [128, 2048], F32)
        T['scrhb'] = sb("scrhb", [128, 2048], BF16)
        T['xtok'] = sb("xtok", [128, 2048], F32)
        T['stc'] = sb("stc", [128, 64], F32)
        T['tmpc'] = sb("tmpc", [128, 8], F32)
        T['coltiles'] = sb("coltiles", [128, 128], F32)
        T['fcol'] = sb("fcol", [128, 64], F32)
        T['wabs'] = sb("wabs", [128, 4], F32)
        T['onescol'] = sb("onescol", [128, 1], BF16)
        T['onescol32'] = sb("onescol32", [128, 1], F32)
        T['wsum_row'] = sb("wsum_row", [1, 8], F32)
        T['wsum_row2'] = sb("wsum_row2", [1, 4], F32)
        T['wsum_row2b'] = sb("wsum_row2b", [1, 4], BF16)
        T['swcol'] = sb("swcol", [128, 8], F32)
        T['xg'] = [sb("xg0", [128, 2048], F32)]
        T['xn_tok'] = [sb("xnt0", [128, 2048], BF16)]
        T['xn_fm'] = sb("xn_fm", [128, 16 * 512], BF16)
        T['z_sil'] = sb("z_sil", [128, 4 * 512], BF16)
        T['B_fm'] = sb("B_fm", [128, 512], BF16)
        T['C_fm'] = sb("C_fm", [128, 4 * 512], BF16)
        T['dtl_fm'] = sb("dtl_fm", [128, 512], BF16)
        T['delta'] = sb("delta", [128, 4 * 512], BF16)
        T['e_tmp'] = sb("e_tmp", [128, 512], F32)
        T['u_in'] = sb("u_in", [128, 4 * 516], F32)
        T['utail'] = sb("utail", [128, 16], F32)
        T['cacc'] = sb("cacc", [128, 512], F32)
        T['u_exp'] = sb("u_exp", [128, 4 * 512], BF16)
        T['du'] = sb("du", [128, 4 * 512], BF16)
        T['dAt'] = [sb("dAt0", [128, 512], F32), sb("dAt1", [128, 512], F32)]
        T['Bb'] = [sb("Bb0", [128, 512], BF16), sb("Bb1", [128, 512], BF16)]
        T['bt'] = [sb("bt0", [128, 512], BF16), sb("bt1", [128, 512], BF16)]
        T['st_s'] = [sb("st0", [128, 512], F32), sb("st1", [128, 512], F32)]
        T['prods'] = sb("prods", [128, 8 * 512], BF16)
        T['yhalf'] = sb("yhalf", [128, 512], F32)
        T['ytmp'] = sb("ytmp", [128, 512], F32)
        T['gy_t'] = [sb("gyt0", [128, 512], BF16), sb("gyt1", [128, 512], BF16)]
        T['carry'] = sb("carry", [128, 64], F32)
        T['xq_tok'] = sb("xq_tok", [128, 2048], BF16)
        T['xq_fm'] = sb("xq_fm", [128, 16 * 512], BF16)
        T['wch'] = [sb("wch0", [128, 512], BF16), sb("wch1", [128, 512], BF16)]
        T['silu1'] = sb("silu1", [128, 2048], BF16)
        T['hchunk'] = [sb("hch0", [128, 512], F32), sb("hch1", [128, 512], F32)]
        T['hps'] = sb("hps", [128, 8], F32)
        T['hpm'] = sb("hpm", [128, 8], F32)
        T['l3l0'] = [sb("l3l0a", [128, 128], BF16), sb("l3l0b", [128, 128], BF16)]
        T['l3l1'] = [sb("l3l1a", [128, 128], BF16), sb("l3l1b", [128, 128], BF16)]


        T['xqh_w'] = T['scrhb']
        T['g_t'] = T['xn_fm'][:, 0:4096]
        T['otmp'] = [T['scr32'][:, 0:512], T['scr32'][:, 1536:2048]]
        T['out_t'] = [T['scr32'][:, 512:1024], T['scr32'][:, 1024:1536]]
        T['w3ch'] = [T['z_sil'][:, 0:1024], T['z_sil'][:, 1024:2048]]
        T['gyb'] = T['winT'][:, 0:32 * 512]
        T['wch4'] = T['wch'] + [T['xq_fm'][:, 0:512], T['xq_fm'][:, 512:1024]]
        T['gch'] = [T['xq_fm'][:, 1024:1536], T['xq_fm'][:, 1536:2048]]
        T['xch'] = [T['scrh'][:, 0:512], T['scrh'][:, 512:1024]]
        T['pA'] = ctx.enter_context(nc.psum_tensor("pA", [128, 1024], F32))
        T['pB'] = ctx.enter_context(nc.psum_tensor("pB", [128, 512], F32))
        T['pC'] = ctx.enter_context(nc.psum_tensor("pC", [128, 1024], F32))
        T['pF'] = ctx.enter_context(nc.psum_tensor("pF", [128, 1024], F32))
        T['pT'] = ctx.enter_context(nc.psum_tensor("pT", [128, 1024], BF16))

        sems = {nm: ctx.enter_context(nc.semaphore(nm)) for nm in SEM_NAMES}
        block = ctx.enter_context(nc.Block())
        walks = {}

        @block.sync
        def _(eng):
            walks['sync'] = _emit(nc, sems, 'sync', eng, T, sim=sim, stage=stage, nwin=nwin, wstage=wstage)

        @block.gpsimd
        def _(eng):
            walks['gpsimd'] = _emit(nc, sems, 'gpsimd', eng, T, sim=sim, stage=stage, nwin=nwin, wstage=wstage)

        @block.scalar
        def _(eng):
            walks['scalar'] = _emit(nc, sems, 'scalar', eng, T, sim=sim, stage=stage, nwin=nwin, wstage=wstage)

        @block.vector
        def _(eng):
            walks['vector'] = _emit(nc, sems, 'vector', eng, T, sim=sim, stage=stage, nwin=nwin, wstage=wstage)

        @block.tensor
        def _(eng):
            walks['tensor'] = _emit(nc, sems, 'tensor', eng, T, sim=sim, stage=stage, nwin=nwin, wstage=wstage)

        ref = walks['sync'].checks
        for nm2, wk in walks.items():
            assert wk.checks == ref, f"schedule divergence in {nm2}"
    return nc


def kernel(**inputs):
    in_maps = _host_prepare(inputs)
    if 'nc' not in _cache:
        _cache['nc'] = _build()
    res = run_bass_kernel_spmd(_cache['nc'], in_maps, core_ids=list(range(NCORE)))
    out = np.concatenate([r["out"] for r in res.results], axis=0)
    return np.ascontiguousarray(out.reshape(B, L, D).astype(np.float32))


# version 20
# speedup vs baseline: 1.1108x; 1.1108x over previous
import numpy as np
import concourse.bass as bass
import concourse.mybir as mybir
from concourse.bass_utils import run_bass_kernel_spmd

F32 = mybir.dt.float32
BF16 = mybir.dt.bfloat16
I32 = mybir.dt.int32
U32 = mybir.dt.uint32
AF = mybir.ActivationFunctionType
ALU = mybir.AluOpType
AX = mybir.AxisListType

NCORE = 8
B = 2
L = 4096
D = 2048
DIN = 4096
DST = 16
IC = DIN // NCORE
TOK = (B * L) // NCORE
NTC = TOK // 128
WN = 512
NWIN = B * L // WN
WPB = L // WN
NJ = 4
DFF = 8192
DFC = DFF // NCORE

MAGIC = 12582912.0

SEM_NAMES = ["s_ld", "s_gd", "s_coll",
             "L_c", "L_w", "L_w2", "L_x", "L_xt", "L_wc0", "L_wc1", "L_wc2", "L_wc3",
             "L_w30", "L_w31", "L_h", "L_xo0", "L_xo1", "G_gr0", "G_gr1",
             "G_c", "G_wo", "G_wo2", "G_st", "G_rb", "G_wr", "G_wq", "G_wq2",
             "G_h0", "G_h1",
             "G_gy0", "G_gy1", "G_xq", "G_l30", "G_l31", "G_g", "G_gyl", "G_grd",
             "G_dbg",
             "s_mmA", "s_mmB", "s_mmC", "s_mmT", "s_mmS",
             "s_evA", "s_evB", "s_tpe", "s_xn", "s_dA", "s_scm",
             "s_vC", "s_vF", "s_vcast", "s_vstat", "s_vmisc", "s_vq", "s_vh",
             "s_vgy", "s_vbt", "s_vscan", "s_varg", "s_vout", "s_out",
             "s_out0", "s_out1"]


def _host_prepare(inp):
    x = np.ascontiguousarray(
        np.asarray(inp["L0_attn_hidden_states"], np.float32).reshape(B * L, D))
    W_in = np.asarray(inp["L0_attn_W_in"], np.float32)
    W_dt = np.asarray(inp["L0_attn_W_dt"], np.float32)
    b_dt = np.asarray(inp["L0_attn_b_dt"], np.float32)
    conv_w = np.asarray(inp["L0_attn_conv_w"], np.float32)
    conv_b = np.asarray(inp["L0_attn_conv_b"], np.float32)
    A_log = np.asarray(inp["L0_attn_A_log"], np.float32)
    D_par = np.asarray(inp["L0_attn_D_param"], np.float32)
    W_out = np.asarray(inp["L0_attn_W_out"], np.float32)
    w1 = np.asarray(inp["L0_ffn_w1"], np.float32)
    w2 = np.asarray(inp["L0_ffn_w2"], np.float32)
    w3 = np.asarray(inp["L0_ffn_w3"], np.float32)

    ident = np.eye(128, dtype=np.float32)
    indB = np.zeros((128, NJ * DST), np.float32)
    for j in range(NJ):
        for n in range(DST):
            indB[64 + j * 16 + n, j * DST + n] = 1.0
    indC = np.zeros((DST, 128, 128), np.float32)
    for n in range(DST):
        for p in range(128):
            indC[n, (p // 16) * 16 + n, p] = 1.0
    indU = np.zeros((NJ, 128, 128), np.float32)
    for j in range(NJ):
        for p in range(128):
            indU[j, j * 16 + (p % 16), p] = 1.0

    woutT = np.ascontiguousarray(W_out.T)

    in_maps = []
    for c in range(NCORE):
        zc = slice(c * IC, (c + 1) * IC)
        rows = np.concatenate([
            W_in[zc],
            W_in[DIN + c * 64:DIN + (c + 1) * 64],
            W_in[DIN + 512 + c * 64:DIN + 512 + (c + 1) * 64],
            W_in[DIN + 1024 + c * IC:DIN + 1024 + (c + 1) * IC],
            W_in[DIN + 1024 + DIN:],
        ], axis=0)
        in_maps.append(dict(
            x_all=x,
            x_tok=np.ascontiguousarray(x[c * TOK:(c + 1) * TOK]),
            winT=np.ascontiguousarray(rows.T),
            wdtT=np.ascontiguousarray(W_dt[zc].T),
            woutT=woutT,
            w1T=np.ascontiguousarray(w1[c * DFC:(c + 1) * DFC].T),
            w2T=np.ascontiguousarray(w2[c * DFC:(c + 1) * DFC].T),
            w3T=np.ascontiguousarray(w3[:, c * DFC:(c + 1) * DFC].T),
            bdt=np.ascontiguousarray(b_dt[zc]),
            convw=np.ascontiguousarray(conv_w[zc, 0, :]),
            convb=np.ascontiguousarray(conv_b[zc]),
            alog=np.ascontiguousarray(A_log[zc]),
            dpar=np.ascontiguousarray(D_par[zc]),
            ident=ident, indB=indB, indC=indC, indU=indU,
            tokbase=np.array([[c * NTC]], dtype=np.uint32),
            tokbase2=np.array([[c * 2]], dtype=np.uint32),
        ))
    return in_maps


class Walk:
    def __init__(self, sems, engine_name, eng):
        self.S = sems
        self.E = engine_name
        self.eng = eng
        self.cnt = {k: 0 for k in sems}
        self.lastwait = {}
        self.iA = self.iB = self.iC = self.iT = 0
        self.checks = []

    def me(self, name):
        return self.E == name

    def bump(self, sem, by=1):
        self.cnt[sem] += by
        return self.cnt[sem]

    def wait(self, sem, val):
        if val is None or val <= 0:
            return
        if self.lastwait.get(sem, -1) >= val:
            return
        self.lastwait[sem] = val
        self.eng.wait_ge(self.S[sem], val)


def _emit(nc, sems, E, eng, T, sim=False, stage=99, nwin=NWIN, wstage=99):
    s = Walk(sems, E, eng)
    S = s.S
    me = s.me

    def dma(queue, sem, out, in_, waits=(), slow=False):
        v = s.bump(sem, 16)
        s.checks.append((sem, v))
        if me(queue):
            for (sm, vl) in waits:
                s.wait(sm, vl)
            eng.dma_start(out=out, in_=in_,
                          allow_slow_non_contiguous=slow).then_inc(S[sem], 16)
        return (sem, v)

    prev_op = {}

    def op(engine, sem, waits, emitfn, dep=False):
        v = s.bump(sem, 1)
        s.checks.append((sem, v))
        w = list(waits)
        if dep and engine in prev_op:
            w.append(prev_op[engine])
        prev_op[engine] = (sem, v)
        if me(engine):
            for (sm, vl) in w:
                s.wait(sm, vl)
            emitfn(eng).then_inc(S[sem], 1)
        return v

    def coll(kind, alu, ins, outs, waits):
        v = s.bump('s_coll', 16 if sim else 1)
        s.checks.append(('s_coll', v))
        if me('gpsimd'):
            for (sm, vl) in waits:
                s.wait(sm, vl)
            if sim:
                n = ins[0].size()
                eng.dma_start(out=outs[0].flatten()[0:n],
                              in_=ins[0].flatten()[0:n]).then_inc(S['s_coll'], 16)
            else:
                eng.collective_compute(kind, alu, replica_groups=[list(range(NCORE))],
                                      ins=ins, outs=outs).then_inc(S['s_coll'], 1)
        return v

    def slotA():
        s.iA += 1
        i = s.iA
        return T['pA'][:, ((i - 1) % 2) * 512:(((i - 1) % 2) + 1) * 512], ('s_evA', i - 2)

    def slotB():
        s.iB += 1
        i = s.iB
        return T['pB'][:, 0:512], ('s_evB', i - 1)

    def slotC():
        s.iC += 1
        i = s.iC
        return T['pC'][:, ((i - 1) % 2) * 512:(((i - 1) % 2) + 1) * 512], ('s_vC', i - 2)

    def slotTG():
        s.iT += 1
        g = s.iT
        return 0, ('s_tpe', g - 1)

    def tp_group(src_tile, blocks, dst_fn, vwait):
        base, gate = slotTG()
        vt = 0
        for bi, blk in enumerate(blocks):
            ps = T['pT'][:, base + bi * 128:base + (bi + 1) * 128]
            g0 = [vwait, ('s_vcast', v_idb), gate] if bi == 0 else []
            vt = op('tensor', 's_mmT', g0,
                    lambda e, ps=ps, blk=blk, src_tile=src_tile: e.transpose(
                        ps, src_tile[:, blk * 128:(blk + 1) * 128], T['identb'][:]))
        bank = T['pT'][:, base:base + 1024]
        return op('scalar', 's_tpe', [('s_mmT', vt)],
                  lambda e, bank=bank: e.copy(out=dst_fn(), in_=bank))

    def winT(k, o):
        return T['winT'][:].rearrange("p (k o c) -> p k o c", k=16, o=10)[:, k, o, :]

    def xnfm(k):
        return T['xn_fm'][:].rearrange("p (k c) -> p k c", k=16)[:, k, :]

    def xqfm(k, q):
        return T['xq_fm'][:].rearrange("p (k c) -> p k c", k=16)[:, k, q * 128:(q + 1) * 128]

    def stv(kk):
        return T['stc'][:, kk * 8:(kk + 1) * 8]

    def stcol(kk, tt):
        return T['stc'][:, kk * 8 + tt:kk * 8 + tt + 1]

    if me('vector'):
        eng.memset(T['wabs'][:], 0.0)
        eng.memset(T['wsum_row'][:], 0.0)
        eng.memset(T['fcol'][:], 0.0)
        eng.memset(T['onescol'][:], 1.0)
        eng.memset(T['onescol32'][:], 1.0)
        eng.memset(T['utail'][:], 0.0)
        eng.memset(T['u_in'][:], 0.0)
        eng.memset(T['carry'][:], 0.0)
        eng.memset(T['hpm'][:], 0.0)
        eng.memset(T['hps'][:], 0.0)

    dma('gpsimd', 'G_c', T['identb'][:], T['ident_d'])
    dma('gpsimd', 'G_c', T['indB'][:], T['indB_d'])
    dma('gpsimd', 'G_c', T['indC'][:].rearrange("p (a c) -> p a c", a=16), T['indC_d'].transpose([1, 0, 2]))
    g_consts = dma('gpsimd', 'G_c', T['indU'][:].rearrange("p (a c) -> p a c", a=4), T['indU_d'].transpose([1, 0, 2]))
    for j in range(NJ):
        dma('sync', 'L_c', T['bdtc'][:, j:j + 1],
            T['bdt_d'].rearrange("(a p) -> a p", p=128)[j].unsqueeze(1))
        dma('sync', 'L_c', T['convbc'][:, j:j + 1],
            T['convb_d'].rearrange("(a p) -> a p", p=128)[j].unsqueeze(1))
        dma('sync', 'L_c', T['dparc'][:, j:j + 1],
            T['dpar_d'].rearrange("(a p) -> a p", p=128)[j].unsqueeze(1))
        dma('sync', 'L_c', T['convwc'][:, j * 4:(j + 1) * 4], T['convw_d'][j * 128:(j + 1) * 128, :])
        dma('sync', 'L_c', T['alogs'][:, j * 16:(j + 1) * 16], T['alog_d'][j * 128:(j + 1) * 128, :])
    dma('sync', 'L_c', T['tokb_s'][:], T['tokb_d'])
    dma('sync', 'L_c', T['tokb2_s'][:], T['tokb2_d'])
    v_consts = ('L_c', s.cnt['L_c'])
    v_tokb = v_consts
    v_gconsts = g_consts

    v_idb = op('vector', 's_vcast', [g_consts],
               lambda e: e.engine_nop())
    va = op('scalar', 's_scm', [v_consts],
            lambda e: e.activation(out=T['A_t'][:], in_=T['alogs'][:], func=AF.Exp))
    op('vector', 's_vcast', [('s_scm', va)],
       lambda e: e.tensor_scalar_mul(T['A_t'][:], T['A_t'][:], -1.0))
    op('vector', 's_vcast', [v_consts],
       lambda e: e.tensor_scalar_mul(T['bdt2c'][:], T['bdtc'][:], 2.0))

    scrhP = [T['scrh'], T['scrh2']]
    L_wP = ['L_w', 'L_w2']

    v = dma('sync', 'L_w', T['scrh'][:, 0:512], T['wdtT_d'], waits=[('s_vcast', v_idb)])
    vcast_h = [0, 0]
    vcast_h[0] = op('vector', 's_vcast', [v],
                    lambda e: e.tensor_copy(out=T['wdtT'][:], in_=T['scrh'][:, 0:512]))

    for o in range(10):
        p = o % 2
        src = T['winT_d'].rearrange("(k p) c -> p k c", p=128)[:, :, o * 128:(o + 1) * 128]
        v = dma('sync', L_wP[p], scrhP[p][:].rearrange("p (k c) -> p k c", c=128)[:, 0:16, :],
                src, waits=[('s_vcast', vcast_h[p])])
        vcast_h[p] = op('vector', 's_vcast', [v],
                        lambda e, o=o, p=p: e.tensor_copy(
                            out=T['winT'][:].rearrange("p (k o c) -> p k o c", k=16, o=10)[:, :, o, :],
                            in_=scrhP[p][:].rearrange("p (k c) -> p k c", c=128)[:, 0:16, :]))
    v_wincast = s.cnt['s_vcast']

    v_gd_w = ('G_wo', 0)
    for k in range(32):
        p = k % 2
        v = dma('sync', L_wP[p], scrhP[p][:, 0:2048], T['woutT_d'][k * 128:(k + 1) * 128, :],
                waits=[('s_vcast', vcast_h[p])])
        vcast = op('vector', 's_vcast', [v, v_gd_w],
                   lambda e, p=p: e.tensor_copy(out=T['scrhb'][:, 0:2048],
                                                in_=scrhP[p][:, 0:2048]))
        vcast_h[p] = vcast
        v_gd_w = dma('gpsimd', 'G_wo', T['woutTb_d'][k * 128:(k + 1) * 128, :],
                     T['scrhb'][:, 0:2048], waits=[('s_vcast', vcast)])

    v_stt = 0
    v_sqq = 0
    v_xload = 0
    for tt in range(NTC):
        v_xload = dma('sync', 'L_xt', T['xtok'][:], T['xtok_d'][tt * 128:(tt + 1) * 128, :],
                      waits=[('s_vstat', v_stt), ('s_scm', v_sqq)])
        v_sqq = op('scalar', 's_scm', [v_xload, ('s_vstat', v_stt)],
                   lambda e, tt=tt: e.activation(
                       out=T['scr32'][:, 0:2048], in_=T['xtok'][:], func=AF.Square,
                       accum_out=stcol(1, tt)))
        vred = op('vector', 's_vstat', [v_xload],
                  lambda e, tt=tt: e.reduce_sum(out=stcol(0, tt), in_=T['xtok'][:],
                                                axis=AX.X))
        op('vector', 's_vstat', [('s_vstat', vred)],
           lambda e, tt=tt: e.tensor_scalar_mul(stcol(2, tt), stcol(0, tt), 1.0 / D))
        op('vector', 's_vstat', [('s_scm', v_sqq)],
           lambda e, tt=tt: e.tensor_scalar_mul(stcol(3, tt), stcol(1, tt), 1.0 / D),
           dep=True)
        op('vector', 's_vstat', [],
           lambda e, tt=tt: e.tensor_mul(T['tmpc'][:, 0:1], stcol(2, tt), stcol(2, tt)),
           dep=True)
        op('vector', 's_vstat', [],
           lambda e, tt=tt: e.tensor_sub(stcol(3, tt), stcol(3, tt), T['tmpc'][:, 0:1]),
           dep=True)
        op('vector', 's_vstat', [('s_scm', v_sqq)],
           lambda e, tt=tt: e.tensor_scalar_sub(T['scr32'][:, 0:2048], T['xtok'][:],
                                                stcol(2, tt)), dep=True)
        v_stt = op('vector', 's_vstat', [],
                   lambda e, tt=tt: e.reduce_max(
                       out=stcol(7, tt), in_=T['scr32'][:, 0:2048],
                       axis=AX.X, apply_absolute_value=True), dep=True)

    v_eps = op('vector', 's_vstat', [],
               lambda e: e.tensor_scalar_add(stv(4), stv(3), 1e-5), dep=True)
    vr0 = op('scalar', 's_scm', [('s_vstat', v_eps)],
             lambda e: e.activation(out=stv(4), in_=stv(4), func=AF.Sqrt))
    op('vector', 's_vstat', [('s_scm', vr0)],
       lambda e: e.reciprocal(out=stv(4), in_=stv(4)))
    vr = op('scalar', 's_scm', [('s_vstat', s.cnt['s_vstat'])],
            lambda e: e.nop())
    op('vector', 's_vstat', [('s_scm', vr)], lambda e: e.tensor_mul(stv(5), stv(2), stv(4)),
       dep=True)
    op('vector', 's_vstat', [], lambda e: e.tensor_scalar_mul(stv(5), stv(5), -1.0),
       dep=True)
    op('vector', 's_vstat', [], lambda e: e.reciprocal(out=stv(6), in_=stv(7)), dep=True)
    v_statq = op('vector', 's_vstat', [],
                 lambda e: e.tensor_scalar_mul(stv(6), stv(6), 127.0), dep=True)

    v_pay = 0
    for tt in range(NTC):
        dma('gpsimd', 'G_st', T['agst_in'][0, tt * 128:(tt + 1) * 128].rearrange("(p o) -> p o", o=1),
            T['stc'][:, 4 * 8 + tt:4 * 8 + tt + 1], waits=[('s_vstat', v_statq)])
        v_pay = dma('gpsimd', 'G_st', T['agst_in'][1, tt * 128:(tt + 1) * 128].rearrange("(p o) -> p o", o=1),
                    T['stc'][:, 5 * 8 + tt:5 * 8 + tt + 1], waits=[('s_vstat', v_statq)])
    v_stAG = coll("AllGather", ALU.bypass, [T['agst_in'][:]], [T['agst_out'][:]],
                  [v_pay])
    v_rb = 0
    for cc in range(NCORE):
        for kk in range(2):
            for tl in range(8):
                v_rb = dma('gpsimd', 'G_rb',
                           T['coltiles'][:].rearrange("p (c k t) -> p c k t", c=8, k=2)
                           [:, cc, kk, tl:tl + 1],
                           T['agst_out'][cc, kk * TOK + tl * 128:kk * TOK + (tl + 1) * 128]
                           .rearrange("(p o) -> p o", o=1),
                           waits=[('s_coll', v_stAG)])

    def dbg_taps(keys):
        if not T.get('dbg'):
            return
        allw = [('s_vstat', s.cnt['s_vstat']), ('s_scm', s.cnt['s_scm']),
                ('s_vq', s.cnt['s_vq']), ('s_vout', s.cnt['s_vout']),
                ('s_vh', s.cnt['s_vh']), ('s_coll', s.cnt['s_coll']),
                ('G_st', s.cnt['G_st']), ('G_wq', s.cnt['G_wq']),
                ('G_wq2', s.cnt['G_wq2']),
                ('G_gy0', s.cnt['G_gy0']), ('G_gy1', s.cnt['G_gy1']),
                ('G_h0', s.cnt['G_h0']), ('G_h1', s.cnt['G_h1']),
                ('G_g', s.cnt['G_g'])]
        v_dbg = None
        for k, src in keys:
            v_dbg = dma('gpsimd', 'G_dbg', T[k][:], src, waits=allw)
        if v_dbg is not None and me('gpsimd'):
            s.wait(*v_dbg)

    def taps_l1():
        return [('d_stc', T['stc'][:]), ('d_agst', T['agst_in'][:])]

    def taps_l2():
        return taps_l1() + [('d_wq', T['agwq_in'][:]), ('d_fcol', T['fcol'][:])]

    if stage < 1:
        dbg_taps(taps_l1())
        return s

    def r_col(tt):
        return T['coltiles'][:].rearrange("p (c k t) -> p c k t", c=8, k=2)[:, tt // 8, 0,
                                                                            tt % 8:tt % 8 + 1]

    def b_col(tt):
        return T['coltiles'][:].rearrange("p (c k t) -> p c k t", c=8, k=2)[:, tt // 8, 1,
                                                                            tt % 8:tt % 8 + 1]

    wsrc = [('w1T_d', 16, 1024, 0), ('w2T_d', 16, 1024, DFC * D),
            ('w3T_d', 8, 2048, 2 * DFC * D)]
    v_ab = 0
    v_abS_p = [0, 0]
    acc_col = [T['tmpc'][:, 1:2], T['tmpc'][:, 4:5]]
    ti = 0
    for mi, (wd, ntile, wid, off) in enumerate(wsrc):
        for t in range(ntile):
            p = ti % 2
            ti += 1
            v = dma('sync', L_wP[p], scrhP[p][:, 0:wid], T[wd][t * 128:(t + 1) * 128, :],
                    waits=[('s_scm', v_abS_p[p]), ('s_vcast', vcast_h[p])])
            v_abS_p[p] = op('scalar', 's_scm', [v, ('s_vq', v_ab), v_gd_w],
                            lambda e, wid=wid, p=p: e.activation(
                                out=T['scrhb'][:, 0:wid], in_=scrhP[p][:, 0:wid], func=AF.Abs,
                                accum_out=acc_col[p]))
            v_ab = op('vector', 's_vq', [('s_scm', v_abS_p[p])],
                      lambda e, mi=mi, p=p: e.tensor_add(T['wabs'][:, mi:mi + 1],
                                                         T['wabs'][:, mi:mi + 1], acc_col[p]),
                      dep=True)
    v_absF = max(v_abS_p)
    psS, gateS = slotA()
    vmS = op('tensor', 's_mmS', [('s_vq', v_ab), ('s_vcast', v_idb), gateS],
             lambda e, psS=psS: e.matmul(psS[0:1, 0:4], T['onescol32'][:], T['wabs'][:, 0:4],
                                         start=True, stop=True))
    vev = op('scalar', 's_evA', [('s_mmS', vmS)],
             lambda e, psS=psS: e.copy(out=T['wsum_row'][:, 0:4], in_=psS[0:1, 0:4]))
    v_gdr = dma('gpsimd', 'G_wr', T['agwr_in'][:], T['wsum_row'][:], waits=[('s_evA', vev)])
    v_arW = coll("AllReduce", ALU.add, [T['agwr_in'][:]], [T['agwr_out'][:]],
                 [v_gdr])
    v_wsr = dma('gpsimd', 'G_wr', T['wsum_row'][:], T['agwr_out'][:], waits=[('s_coll', v_arW)])
    op('vector', 's_vq', [v_wsr],
       lambda e: e.tensor_scalar_mul(T['wsum_row'][:, 0:4], T['wsum_row'][:, 0:4],
                                     1.0 / (DFF * D)))
    op('vector', 's_vq', [],
       lambda e: e.tensor_scalar_max(T['wsum_row'][:, 0:4], T['wsum_row'][:, 0:4], 1e-5),
       dep=True)
    vW3b = op('vector', 's_vq', [],
              lambda e: e.reciprocal(out=T['wsum_row'][:, 4:8], in_=T['wsum_row'][:, 0:4]),
              dep=True)
    psS2, gateS2 = slotA()
    vmS2 = op('tensor', 's_mmS', [('s_vq', vW3b), gateS2],
              lambda e, psS2=psS2: e.matmul(psS2[:, 0:8],
                                            T['onescol32'][0:1, 0:1].broadcast_to([1, 128]),
                                            T['wsum_row'][:, 0:8], start=True, stop=True))
    vsc = op('scalar', 's_evA', [('s_mmS', vmS2)],
             lambda e, psS2=psS2: e.copy(out=T['swcol'][:], in_=psS2[:, 0:8]))

    v_qd = ('G_wq', 0)
    v_qv_p = [0, 0]
    ti = 0
    for mi, (wd, ntile, wid, off) in enumerate(wsrc):
        for t in range(ntile):
            p = ti % 2
            ti += 1
            v = dma('sync', L_wP[p], scrhP[p][:, 0:wid], T[wd][t * 128:(t + 1) * 128, :],
                    waits=[('s_vq', v_qv_p[p]), ('s_scm', v_absF), ('s_vcast', vcast_h[p])])
            op('vector', 's_vq', [v, ('s_evA', vsc)],
               lambda e, mi=mi, wid=wid, p=p: e.tensor_scalar(
                   out=T['scr32'][:, 0:wid], in0=scrhP[p][:, 0:wid],
                   scalar1=T['swcol'][:, 4 + mi:5 + mi], scalar2=MAGIC,
                   op0=ALU.mult, op1=ALU.add))
            op('vector', 's_vq', [],
               lambda e, wid=wid: e.tensor_scalar_sub(
                   T['scr32'][:, 0:wid], T['scr32'][:, 0:wid], MAGIC))
            for hh in range(0, wid, 1024):
                op('vector', 's_vq', [],
                   lambda e, hh=hh: e.tensor_copy(out=T['scri'][:],
                                                  in_=T['scr32'][:, hh:hh + 1024]))
                op('vector', 's_vq', [v_qd] if hh == 0 else [],
                   lambda e, hh=hh: e.tensor_copy(out=T['scrhb'][:, hh:hh + 1024],
                                                  in_=T['scri'][:]))
            v_qv_p[p] = op('vector', 's_vq', [],
                           lambda e, wid=wid: e.tensor_scalar(
                               out=T['scrhb'][:, 0:wid], in0=T['scrhb'][:, 0:wid],
                               scalar1=1.0, scalar2=-1.0, op0=ALU.min, op1=ALU.max))
            v_qd = dma('gpsimd', 'G_wq',
                       T['agwq_in'].rearrange("(r c) -> r c", c=wid)
                       [(off // wid) + t * 128:(off // wid) + (t + 1) * 128, :],
                       T['scrhb'][:, 0:wid], waits=[('s_vq', v_qv_p[p])])
    v_wqAG = coll("AllGather", ALU.bypass, [T['agwq_in'][:]], [T['agwq_out'][:]],
                  [v_qd])

    op('vector', 's_vstat', [('s_vstat', v_statq)],
       lambda e: e.tensor_mul(T['fcol'][:, 0:8], stv(4), stv(4)), dep=True)
    op('vector', 's_vstat', [],
       lambda e: e.tensor_mul(T['fcol'][:, 0:8], T['fcol'][:, 0:8], stv(3)), dep=True)
    op('vector', 's_vstat', [],
       lambda e: e.tensor_scalar_add(T['fcol'][:, 0:8], T['fcol'][:, 0:8], 1e-6), dep=True)
    vrs0 = op('scalar', 's_scm', [('s_vstat', s.cnt['s_vstat'])],
              lambda e: e.activation(out=T['fcol'][:, 0:8], in_=T['fcol'][:, 0:8],
                                     func=AF.Sqrt))
    op('vector', 's_vstat', [('s_scm', vrs0)],
       lambda e: e.reciprocal(out=T['fcol'][:, 0:8], in_=T['fcol'][:, 0:8]))
    vrs = op('scalar', 's_scm', [('s_vstat', s.cnt['s_vstat'])],
             lambda e: e.nop())
    op('vector', 's_vstat', [('s_scm', vrs)],
       lambda e: e.tensor_mul(T['fcol'][:, 8:16], T['fcol'][:, 0:8], stv(4)), dep=True)
    op('vector', 's_vstat', [],
       lambda e: e.tensor_mul(T['fcol'][:, 8:16], T['fcol'][:, 8:16], stv(7)), dep=True)
    op('vector', 's_vstat', [],
       lambda e: e.tensor_scalar_mul(T['fcol'][:, 8:16], T['fcol'][:, 8:16], 1.0 / 127.0),
       dep=True)
    op('vector', 's_vstat', [('s_evA', vsc)],
       lambda e: e.tensor_scalar_mul(T['fcol'][:, 16:24], T['fcol'][:, 8:16],
                                     T['swcol'][:, 0:1]), dep=True)
    v_fc = op('vector', 's_vstat', [],
              lambda e: e.tensor_scalar_mul(T['fcol'][:, 24:32], T['fcol'][:, 8:16],
                                            T['swcol'][:, 1:2]), dep=True)

    def d1_col(ft):
        return T['fcol'][:, 16 + ft:17 + ft]

    if stage < 2:
        dbg_taps(taps_l2())
        return s

    tokreg2 = [None]
    if me('gpsimd'):
        s.wait(*v_tokb)
        reg2 = eng.alloc_register("tokbase2")
        eng.reg_load(reg2, T['tokb2_s'][0:1, 0:1])
        tokreg2[0] = eng.snap(reg2, min_val=0, max_val=(B * L) // 512 - 2)

    def prep_xq(fh):
        vt_last = 0
        v_prev = 0
        for q in range(4):
            ft = fh * 4 + q
            vxl = dma('sync', 'L_xt', T['xtok'][:], T['xtok_d'][ft * 128:(ft + 1) * 128, :],
                      waits=[('s_vq', v_prev), ('s_vstat', v_statq)])
            op('vector', 's_vq', [vxl, ('s_vstat', v_statq)],
               lambda e, ft=ft: e.tensor_scalar(
                   out=T['scr32'][:, 0:2048], in0=T['xtok'][:],
                   scalar1=T['stc'][:, 2 * 8 + ft:2 * 8 + ft + 1],
                   scalar2=T['stc'][:, 6 * 8 + ft:6 * 8 + ft + 1],
                   op0=ALU.subtract, op1=ALU.mult))
            op('vector', 's_vq', [],
               lambda e: e.tensor_scalar_add(
                   T['scr32'][:, 0:2048], T['scr32'][:, 0:2048], MAGIC))
            op('vector', 's_vq', [],
               lambda e: e.tensor_scalar_sub(
                   T['scr32'][:, 0:2048], T['scr32'][:, 0:2048], MAGIC))
            for hh in range(0, 2048, 1024):
                op('vector', 's_vq', [],
                   lambda e, hh=hh: e.tensor_copy(out=T['scri'][:],
                                                  in_=T['scr32'][:, hh:hh + 1024]))
                v_prev = op('vector', 's_vq',
                            [('s_tpe', vt_last)] if hh == 0 else [],
                            lambda e, hh=hh: e.tensor_copy(
                                out=T['xq_tok'][:, hh:hh + 1024], in_=T['scri'][:]))
            for h in range(2):
                def dstf(q=q, h=h):
                    return T['xq_fm'][:].rearrange("p (k c) -> p k c", k=16)[
                        :, h * 8:(h + 1) * 8, q * 128:(q + 1) * 128]
                vt_last = tp_group(T['xq_tok'], list(range(h * 8, (h + 1) * 8)),
                                   dstf, ('s_vq', v_prev))
        return vt_last

    do_ffn = stage >= 3
    fh_ready = [prep_xq(0) if do_ffn else 0, None]

    ffn_jobs = [(fh, cd, pr) for fh in range(2) for cd in range(16) for pr in range(2)]
    wch_lastmm = []
    job_ix = [0]
    kstep = [0]
    v_h_done = [('G_h0', 0), ('G_h1', 0)]
    v_sil_par = [('s_vF', 0), ('s_vF', 0)]
    v_tmpc_free = [0]
    v_pf_free = [('s_vF', 0), ('s_vF', 0)]
    jstate = {'vsl': [0, 0]}

    def ffn_step():
        fh, cd, pr = ffn_jobs[job_ix[0]]
        ks = kstep[0]
        mi, k = divmod(ks, 16)
        fts = [fh * 4 + pr * 2, fh * 4 + pr * 2 + 1]
        rd, cc2 = cd // 2, cd % 2
        moff = mi * DFC * D
        src = T['agwq_out'].rearrange("(r e) -> r e", r=NCORE)[rd] \
            .rearrange("(q c) -> q c", c=1024)[moff // 1024 + k * 128:
                                               moff // 1024 + (k + 1) * 128,
                                               cc2 * 512:(cc2 + 1) * 512]
        wsem = 's_mmA' if mi == 0 else 's_mmC'
        wchg = [wch_lastmm[-2]] if len(wch_lastmm) >= 2 else []
        vload = dma('sync', 'L_wc%d' % (ks % 2), T['wch'][ks % 2][:], src,
                    waits=[('s_coll', v_wqAG)] + wchg)
        vm = 0
        for q, ft in enumerate(fts):
            g0 = []
            if k == 0:
                g0 = [('s_tpe', fh_ready[fh]),
                      v_pf_free[q] if mi == 0 else ('s_scm', jstate['vsl'][q])]
            vm = op('tensor', wsem, [vload] + g0,
                    lambda e, q=q, ks=ks, k=k, ft=ft: e.matmul(
                        T['pF'][:, q * 512:(q + 1) * 512], xqfm(k, ft % 4),
                        T['wch'][ks % 2][:], start=(k == 0), stop=(k == 15)))
        wch_lastmm.append((wsem, vm))
        jp = (job_ix[0] % 2) * 1024
        if k == 15 and mi == 0:
            vsl2 = []
            for q, ft in enumerate(fts):
                vsl2.append(op('scalar', 's_scm',
                               [(wsem, vm), ('s_vstat', v_fc), v_sil_par[job_ix[0] % 2]],
                               lambda e, q=q, ft=ft, jp=jp: e.activation(
                                   out=T['silu1'][:, jp + q * 512:jp + (q + 1) * 512],
                                   in_=T['pF'][:, q * 512:(q + 1) * 512],
                                   func=AF.Silu, scale=d1_col(ft))))
            jstate['vsl'] = vsl2
        if k == 15 and mi == 1:
            for q, ft in enumerate(fts):
                vh = op('vector', 's_vF', [(wsem, vm), ('s_scm', jstate['vsl'][q]),
                                           v_h_done[q]],
                        lambda e, q=q, jp=jp: e.tensor_mul(
                            T['hchunk'][q][:], T['silu1'][:, jp + q * 512:jp + (q + 1) * 512],
                            T['pF'][:, q * 512:(q + 1) * 512]))
                v_pf_free[q] = ('s_vF', vh)
                v_sil_par[job_ix[0] % 2] = ('s_vF', vh)
                vsq2 = op('scalar', 's_scm', [('s_vF', vh), ('s_vh', v_tmpc_free[0])],
                          lambda e, q=q: e.activation(
                              out=T['scrhb'][:, 0:512], in_=T['hchunk'][q][:],
                              func=AF.Square, accum_out=T['tmpc'][:, 2:3]))
                op('vector', 's_vh', [('s_scm', vsq2)],
                   lambda e, ft=ft: e.tensor_add(T['hps'][:, ft:ft + 1],
                                                 T['hps'][:, ft:ft + 1], T['tmpc'][:, 2:3]),
                   dep=True)
                vrm = op('vector', 's_vh', [],
                         lambda e, q=q: e.reduce_max(out=T['tmpc'][:, 3:4],
                                                     in_=T['hchunk'][q][:],
                                                     axis=AX.X, apply_absolute_value=True),
                         dep=True)
                v_tmpc_free[0] = op('vector', 's_vh', [('s_vh', vrm)],
                                    lambda e, ft=ft: e.tensor_max(
                                        T['hpm'][:, ft:ft + 1], T['hpm'][:, ft:ft + 1],
                                        T['tmpc'][:, 3:4]))
                v_h_done[q] = dma('gpsimd', 'G_h%d' % q,
                                  T['h_d'][ft * 128:(ft + 1) * 128, cd * 512:(cd + 1) * 512],
                                  T['hchunk'][q][:], waits=[('s_vh', v_tmpc_free[0])])
        kstep[0] += 1
        if kstep[0] == 32:
            kstep[0] = 0
            job_ix[0] += 1

    def ffn_pump(n, fh_cap):
        if not do_ffn:
            return
        for _ in range(n):
            if job_ix[0] >= len(ffn_jobs) or ffn_jobs[job_ix[0]][0] > fh_cap:
                return
            ffn_step()

    v_gy_hist = [('G_gy0', 0), ('G_gy1', 0)]
    ixg = 0
    v_xn_hist = []
    v_tp_hist = []
    win_mark = None
    for w in range(nwin):
        bat, wb = w // WPB, w % WPB
        tts = [w * 4 + q for q in range(4)]
        if w == 8 and do_ffn:
            fh_ready[1] = prep_xq(1)
        if wstage < 1:
            continue

        v_tp_last = 0
        for q, tt in enumerate(tts):
            gq = []
            if len(v_xn_hist) >= 1:
                gq.append(('s_xn', v_xn_hist[-1]))
            v = dma('sync', 'L_x', T['xg'][0][:], T['x_all_d'][tt * 128:(tt + 1) * 128, :],
                    waits=gq)
            if wstage == 11:
                continue
            gtp = [('s_tpe', v_tp_hist[-1])] if len(v_tp_hist) >= 1 else []
            vx = op('scalar', 's_xn', [v, v_rb] + gtp,
                    lambda e, tt=tt: e.activation(
                        out=T['xn_tok'][0][:], in_=T['xg'][0][:], func=AF.Identity,
                        scale=r_col(tt), bias=b_col(tt)))
            v_xn_hist.append(vx)
            if wstage == 12:
                ixg += 1
                continue
            for h in range(2):
                def dstf(q=q, h=h):
                    return T['xn_fm'][:].rearrange("p (k c) -> p k c", k=16)[
                        :, h * 8:(h + 1) * 8, q * 128:(q + 1) * 128]
                v_tp_last = tp_group(T['xn_tok'][0], list(range(h * 8, (h + 1) * 8)),
                                     dstf, ('s_xn', vx))
            v_tp_hist.append(v_tp_last)
            ixg += 1

        pg = []
        if win_mark is not None:
            pg = [('s_vgy', win_mark['s_vgy']), ('s_mmB', win_mark['s_mmB']),
                  ('s_mmC', win_mark['s_mmC']), ('s_dA', win_mark['s_dA']),
                  ('s_vbt', win_mark['s_vbt'])]

        if wstage < 2:
            continue
        vevA_o = {}
        for o in range(10):
            ps, gate = slotA()
            vm = 0
            for k in range(16):
                g0 = [('s_tpe', v_tp_last), gate] if k == 0 else []
                vm = op('tensor', 's_mmA' if k == 15 else 's_mmS', g0,
                        lambda e, ps=ps, o=o, k=k: e.matmul(
                            ps, winT(k, o), xnfm(k), start=(k == 0), stop=(k == 15)))
            if o < 4:
                vevA_o[o] = op('scalar', 's_evA', [('s_mmA', vm)] + pg,
                               lambda e, ps=ps, o=o: e.activation(
                                   out=T['z_sil'][:, o * 512:(o + 1) * 512], in_=ps,
                                   func=AF.Silu))
            elif o == 4:
                vevA_o[o] = op('scalar', 's_evA', [('s_mmA', vm)] + pg,
                               lambda e, ps=ps: e.copy(out=T['B_fm'][:], in_=ps))
            elif o < 9:
                vevA_o[o] = op('scalar', 's_evA', [('s_mmA', vm)] + pg,
                               lambda e, ps=ps, o=o: e.copy(
                                   out=T['C_fm'][:, (o - 5) * 512:(o - 4) * 512], in_=ps))
            else:
                vevA_o[o] = op('scalar', 's_evA', [('s_mmA', vm)] + pg,
                               lambda e, ps=ps: e.copy(out=T['dtl_fm'][:], in_=ps))

        if wstage < 3:
            continue
        vdelta = {}
        for j in range(NJ):
            ps, gate = slotA()
            vm = op('tensor', 's_mmA', [('s_evA', vevA_o[9]), gate],
                    lambda e, ps=ps, j=j: e.matmul(
                        ps, T['wdtT'][:, j * 128:(j + 1) * 128], T['dtl_fm'][:],
                        start=True, stop=True))
            ve1 = op('scalar', 's_scm', [('s_mmA', vm)],
                     lambda e, ps=ps, j=j: e.activation(
                         out=T['e_tmp'][:], in_=ps, func=AF.Exp, bias=T['bdt2c'][:, j:j + 1]))
            vdelta[j] = op('scalar', 's_evA', [('s_scm', ve1)] + pg,
                           lambda e, j=j: e.activation(
                               out=T['delta'][:, j * 512:(j + 1) * 512], in_=T['e_tmp'][:],
                               func=AF.Ln, bias=1.0))

        if wstage < 4:
            continue
        v_du = 0
        for j in range(NJ):
            ps, gate = slotA()
            vm = op('tensor', 's_mmA', [('s_evA', vevA_o[4]), v_gconsts, gate],
                    lambda e, ps=ps, j=j: e.matmul(
                        ps, T['indU'][:].rearrange("p (a c) -> p a c", a=4)[:, j, :],
                        T['B_fm'][:], start=True, stop=True))
            base = j * 516
            vt1 = op('vector', 's_vmisc', [],
                     lambda e, j=j, base=base: e.tensor_copy(
                         out=T['utail'][:, j * 4:j * 4 + 3],
                         in_=T['u_in'][:, base + 512:base + 515]))
            vev2 = op('scalar', 's_evA', [('s_mmA', vm), ('s_vmisc', vt1)] + pg,
                      lambda e, ps=ps, base=base: e.copy(
                          out=T['u_in'][:, base + 3:base + 515], in_=ps))
            if wb == 0:
                op('vector', 's_vmisc', [('s_evA', vev2)],
                   lambda e, base=base: e.memset(T['u_in'][:, base:base + 3], 0.0))
            else:
                op('vector', 's_vmisc', [('s_evA', vev2)],
                   lambda e, j=j, base=base: e.tensor_copy(
                       out=T['u_in'][:, base:base + 3], in_=T['utail'][:, j * 4:j * 4 + 3]))
            op('vector', 's_vmisc', [],
               lambda e, j=j, base=base: e.tensor_scalar_mul(
                   T['cacc'][:], T['u_in'][:, base:base + 512],
                   T['convwc'][:, j * 4:j * 4 + 1]), dep=True)
            for tp in range(1, 4):
                op('vector', 's_vmisc', [],
                   lambda e, j=j, base=base, tp=tp: e.scalar_tensor_tensor(
                       out=T['cacc'][:], in0=T['u_in'][:, base + tp:base + tp + 512],
                       scalar=T['convwc'][:, j * 4 + tp:j * 4 + tp + 1], in1=T['cacc'][:],
                       op0=ALU.mult, op1=ALU.add))
            vcv = s.cnt['s_vmisc']
            vu = op('scalar', 's_scm', [('s_vmisc', vcv)] + pg,
                    lambda e, j=j: e.activation(
                        out=T['u_exp'][:, j * 512:(j + 1) * 512], in_=T['cacc'][:],
                        func=AF.Silu, bias=T['convbc'][:, j:j + 1]))
            v_du = op('vector', 's_vmisc', [('s_evA', vdelta[j]), ('s_scm', vu)],
                      lambda e, j=j: e.tensor_mul(
                          T['du'][:, j * 512:(j + 1) * 512],
                          T['delta'][:, j * 512:(j + 1) * 512],
                          T['u_exp'][:, j * 512:(j + 1) * 512]))

        if wstage < 5:
            continue

        def emit_B(jn):
            psb, gateB = slotB()
            vmb = op('tensor', 's_mmB', [('s_evA', vevA_o[4]), v_gconsts, gateB],
                     lambda e, psb=psb, jn=jn: e.matmul(
                         psb, T['indB'][:, jn:jn + 1].broadcast_to([128, 128]),
                         T['B_fm'][:], start=True, stop=True))
            ib = s.cnt['s_evB']
            vbe = op('scalar', 's_evB',
                     [('s_mmB', vmb), ('s_vbt', max(0, s.cnt['s_vbt'] - 1))],
                     lambda e, psb=psb, ib=ib: e.copy(out=T['Bb'][ib % 2][:], in_=psb))
            return (vbe, ib)

        def emit_DA(jn):
            jj = jn // DST
            ida = s.cnt['s_dA']
            vda = op('scalar', 's_dA',
                     [('s_evA', vdelta[jj]), ('s_vscan', max(0, s.cnt['s_vscan'] - 1))],
                     lambda e, jj=jj, jn=jn, ida=ida: e.activation(
                         out=T['dAt'][ida % 2][:], in_=T['delta'][:, jj * 512:(jj + 1) * 512],
                         func=AF.Exp,
                         scale=T['A_t'][:, jn:jn + 1]))
            return (vda, ida)

        preDA = emit_DA(0)
        preB = emit_B(0)
        for j in range(NJ):
            for n in range(DST):
                jn = j * DST + n
                vda, ida = preDA
                vbe, ib = preB
                psc, gateC = slotC()
                vmc = op('tensor', 's_mmC', [('s_evA', vevA_o[5 + j]), v_gconsts, gateC],
                        lambda e, psc=psc, n=n, j=j: e.matmul(
                            psc, T['indC'][:].rearrange("p (a c) -> p a c", a=16)[:, n, :],
                            T['C_fm'][:, j * 512:(j + 1) * 512], start=True, stop=True))
                ibt = s.cnt['s_vbt']
                vb2 = op('vector', 's_vbt', [('s_evB', vbe), ('s_vmisc', v_du)],
                         lambda e, j=j, ib=ib, ibt=ibt: e.tensor_mul(
                             T['bt'][ibt % 2][:], T['du'][:, j * 512:(j + 1) * 512],
                             T['Bb'][ib % 2][:]))
                init = 0.0 if wb == 0 else T['carry'][:, jn:jn + 1]
                isc = s.cnt['s_vscan']
                vscn = op('vector', 's_vscan', [('s_dA', vda)],
                          lambda e, ida=ida, ibt=ibt, isc=isc, init=init: e.tensor_tensor_scan(
                              out=T['st_s'][isc % 2][:], data0=T['dAt'][ida % 2][:],
                              data1=T['bt'][ibt % 2][:], initial=init,
                              op0=ALU.mult, op1=ALU.add))
                op('vector', 's_vmisc', [('s_vscan', vscn)],
                   lambda e, jn=jn, isc=isc: e.tensor_copy(
                       out=T['carry'][:, jn:jn + 1], in_=T['st_s'][isc % 2][:, WN - 1:WN]))
                op('vector', 's_vC', [('s_mmC', vmc), ('s_vscan', vscn)],
                   lambda e, psc=psc, n=n, isc=isc: e.tensor_mul(
                       T['prods'][:, (n % 8) * 512:(n % 8 + 1) * 512], psc,
                       T['st_s'][isc % 2][:]))
                if jn < NJ * DST - 1:
                    preDA = emit_DA(jn + 1)
                    preB = emit_B(jn + 1)
                ffn_pump(0 if w < 2 else (3 if w < 8 else 2), 0 if w < 8 else 1)
                if n % 8 == 7:
                    for stp, cnt2 in ((1, 4), (2, 2), (4, 1)):
                        for q2 in range(cnt2):
                            a0 = (q2 * 2 * stp) * 512
                            a1 = (q2 * 2 * stp + stp) * 512
                            op('vector', 's_vmisc', [],
                               lambda e, a0=a0, a1=a1: e.tensor_add(
                                   T['prods'][:, a0:a0 + 512], T['prods'][:, a0:a0 + 512],
                                   T['prods'][:, a1:a1 + 512]))
                    if n == 7:
                        op('vector', 's_vmisc', [],
                           lambda e: e.tensor_copy(out=T['yhalf'][:],
                                                   in_=T['prods'][:, 0:512]))
                    else:
                        op('vector', 's_vmisc', [],
                           lambda e: e.tensor_add(T['ytmp'][:], T['prods'][:, 0:512],
                                                  T['yhalf'][:]))
            op('vector', 's_vmisc', [],
               lambda e, j=j: e.scalar_tensor_tensor(
                   out=T['ytmp'][:], in0=T['u_exp'][:, j * 512:(j + 1) * 512],
                   scalar=T['dparc'][:, j:j + 1], in1=T['ytmp'][:],
                   op0=ALU.mult, op1=ALU.add))
            igy = s.cnt['s_vgy']
            vgy = op('vector', 's_vgy',
                     [('s_evA', vevA_o[j]), v_gy_hist[igy % 2]],
                     lambda e, j=j, igy=igy: e.tensor_mul(
                         T['gy_t'][igy % 2][:], T['ytmp'][:],
                         T['z_sil'][:, j * 512:(j + 1) * 512]))
            v_gy_hist[igy % 2] = dma('gpsimd', 'G_gy%d' % (igy % 2),
                            T['gy_in_d'][j * 128:(j + 1) * 128,
                                         bat * L + wb * WN:bat * L + wb * WN + WN],
                            T['gy_t'][igy % 2][:], waits=[('s_vgy', vgy)])

        win_mark = {k: s.cnt[k] for k in ('s_vgy', 's_mmB', 's_mmC', 's_dA', 's_vbt')}

    if do_ffn and fh_ready[1] is None:
        fh_ready[1] = prep_xq(1)
    while do_ffn and job_ix[0] < 64:
        ffn_step()
    if stage < 4:
        dbg_taps(taps_l2() + [('d_gy', T['gy_in_d'][:]), ('d_h', T['h_d'][:])])
        return s

    v_gyAG = coll("AllGather", ALU.bypass, [T['gy_in_d'][:]], [T['gy_all_d'][:]],
                  [v_gy_hist[0], v_gy_hist[1]])

    v_hs = s.cnt['s_vh']
    op('vector', 's_vh', [('s_vh', v_hs), v_h_done[0], v_h_done[1]],
       lambda e: e.reciprocal(out=T['fcol'][:, 32:40], in_=T['hpm'][:]), dep=True)
    op('vector', 's_vh', [],
       lambda e: e.tensor_scalar_mul(T['fcol'][:, 32:40], T['fcol'][:, 32:40], 127.0),
       dep=True)
    op('vector', 's_vh', [],
       lambda e: e.tensor_mul(T['fcol'][:, 40:48], T['fcol'][:, 24:32], T['fcol'][:, 24:32]),
       dep=True)
    op('vector', 's_vh', [],
       lambda e: e.tensor_mul(T['fcol'][:, 40:48], T['fcol'][:, 40:48], T['hps'][:]),
       dep=True)
    op('vector', 's_vh', [],
       lambda e: e.tensor_scalar_mul(T['fcol'][:, 40:48], T['fcol'][:, 40:48],
                                     1.0 / DFF), dep=True)
    v_pre = op('vector', 's_vh', [],
               lambda e: e.tensor_scalar_add(T['fcol'][:, 40:48], T['fcol'][:, 40:48],
                                             1e-6), dep=True)
    vrs2a = op('scalar', 's_scm', [('s_vh', v_pre)],
               lambda e: e.activation(out=T['fcol'][:, 40:48], in_=T['fcol'][:, 40:48],
                                      func=AF.Sqrt))
    op('vector', 's_vh', [('s_scm', vrs2a)],
       lambda e: e.reciprocal(out=T['fcol'][:, 40:48], in_=T['fcol'][:, 40:48]))
    vrs2 = op('scalar', 's_scm', [('s_vh', s.cnt['s_vh'])],
              lambda e: e.nop())
    op('vector', 's_vh', [('s_scm', vrs2)],
       lambda e: e.tensor_mul(T['fcol'][:, 48:56], T['fcol'][:, 40:48], T['fcol'][:, 24:32]),
       dep=True)
    op('vector', 's_vh', [],
       lambda e: e.tensor_mul(T['fcol'][:, 48:56], T['fcol'][:, 48:56], T['hpm'][:]),
       dep=True)
    op('vector', 's_vh', [],
       lambda e: e.tensor_scalar_mul(T['fcol'][:, 48:56], T['fcol'][:, 48:56], 1.0 / 127.0),
       dep=True)
    v_dq3 = op('vector', 's_vh', [],
               lambda e: e.tensor_scalar_mul(T['fcol'][:, 48:56], T['fcol'][:, 48:56],
                                             T['swcol'][:, 2:3]), dep=True)

    def dq3_col(ft):
        return T['fcol'][:, 48 + ft:49 + ft]

    v_xqh_done = ('G_xq', 0)
    v_hq = 0
    for ft in range(NTC):
        for hc in range(4):
            v = dma('sync', 'L_h', T['scr32'][:, 0:2048],
                    T['h_d'][ft * 128:(ft + 1) * 128, hc * 2048:(hc + 1) * 2048],
                    waits=[('s_vh', v_hq), v_h_done[0], v_h_done[1]])
            op('vector', 's_vh', [v, ('s_vh', v_dq3)],
               lambda e, ft=ft: e.tensor_scalar(
                   out=T['scr32'][:, 0:2048], in0=T['scr32'][:, 0:2048],
                   scalar1=T['fcol'][:, 32 + ft:33 + ft], scalar2=MAGIC,
                   op0=ALU.mult, op1=ALU.add))
            op('vector', 's_vh', [],
               lambda e: e.tensor_scalar_sub(
                   T['scr32'][:, 0:2048], T['scr32'][:, 0:2048], MAGIC))
            for hh in range(0, 2048, 1024):
                op('vector', 's_vh', [],
                   lambda e, hh=hh: e.tensor_copy(out=T['scri'][:],
                                                  in_=T['scr32'][:, hh:hh + 1024]))
                v_hq = op('vector', 's_vh', [v_xqh_done] if hh == 0 else [],
                          lambda e, hh=hh: e.tensor_copy(out=T['xq_tok'][:, hh:hh + 1024],
                                                         in_=T['scri'][:]))
            vev3 = 0
            for h in range(2):
                def dstf(h=h):
                    return T['xqh_w'][:, h * 1024:(h + 1) * 1024]
                vev3 = tp_group(T['xq_tok'], list(range(h * 8, (h + 1) * 8)),
                                dstf, ('s_vh', v_hq))
            v_xqh_done = dma('gpsimd', 'G_xq',
                             T['xqh_d'].rearrange("(f p k c) -> f p k c", f=NTC, k=64, p=128)
                             [ft][:, hc * 16:(hc + 1) * 16, :],
                             T['xqh_w'][:].rearrange("p (k c) -> p k c", k=16),
                             waits=[('s_tpe', vev3)])

    if stage < 5:
        dbg_taps(taps_l2() + [('d_gy', T['gy_in_d'][:]), ('d_h', T['h_d'][:])])
        return s

    v_gw = ('G_g', 0)
    l3A = []
    l3B = []
    v_zfree = s.cnt['s_vgy']
    for fp in range(4):
        fts = [fp * 2, fp * 2 + 1]
        lb3w = [v_xqh_done]
        if fp == 0:
            lb3w += [('s_mmA', s.cnt['s_mmA']), ('s_mmS', s.cnt['s_mmS']),
                     ('s_mmB', s.cnt['s_mmB']), ('s_vgy', v_zfree)]
        else:
            lb3w += [('s_mmA', l3A[-1]), ('s_mmB', l3B[-1])]
        v_l3b = 0
        for q in range(2):
            v_l3b = dma('scalar', 'G_l3%d' % q,
                        T['winT'][:].rearrange("p (k c) -> p k c", c=128)
                        [:, q * 64:(q + 1) * 64, :],
                        T['xqh_d'].rearrange("(f p k c) -> f p k c", f=NTC, k=64, p=128)
                        [fts[q]], waits=lb3w)
        v_l3b = [('G_l30', s.cnt['G_l30']), ('G_l31', s.cnt['G_l31'])]
        for mh in range(2):
            s.iA += 2
            gA = ('s_evA', s.iA - 2)
            s.iB += 2
            gB = ('s_evB', s.iB - 2)
            vmmA = vmmB = 0
            for k in range(64):
                rdk, lk = k // 8, k % 8
                src = T['agwq_out'].rearrange("(r e) -> r e", r=NCORE)[rdk] \
                    .rearrange("(q c) -> q c", c=2048)[(2 * DFC * D) // 2048 + lk * 128:
                                                       (2 * DFC * D) // 2048 + (lk + 1) * 128,
                                                       mh * 1024:(mh + 1) * 1024]
                g3 = ([('s_mmA', l3A[-2]), ('s_mmB', l3B[-2])]
                      if len(l3A) >= 2 else [('s_vgy', v_zfree)])
                vload = dma('sync', 'L_w3%d' % (k % 2), T['w3ch'][k % 2], src, waits=g3)
                for hx in range(2):
                    vmmA = op('tensor', 's_mmA',
                              [vload, gA] + v_l3b if (k == 0 and hx == 0)
                              else [vload],
                              lambda e, k=k, hx=hx: e.matmul(
                                  T['pA'][:, hx * 512:(hx + 1) * 512],
                                  T['winT'][:, k * 128:(k + 1) * 128],
                                  T['w3ch'][k % 2][:, hx * 512:(hx + 1) * 512],
                                  start=(k == 0), stop=(k == 63)))
                    vmmB = op('tensor', 's_mmB',
                              [gB, v_pf_free[0], v_pf_free[1]]
                              if (k == 0 and hx == 0) else [],
                              lambda e, k=k, hx=hx: e.matmul(
                                  T['pF'][:, hx * 512:(hx + 1) * 512],
                                  T['winT'][:, 8192 + k * 128:8192 + (k + 1) * 128],
                                  T['w3ch'][k % 2][:, hx * 512:(hx + 1) * 512],
                                  start=(k == 0), stop=(k == 63)))
                l3A.append(vmmA)
                l3B.append(vmmB)
            for q, (psname, msem, vmm, esem) in enumerate(
                    [('pA', 's_mmA', vmmA, 's_evA'), ('pF', 's_mmB', vmmB, 's_evB')]):
                for half in range(2):
                    op('scalar', esem, [(msem, vmm), ('s_vh', v_dq3),
                                        v_gw],
                       lambda e, psname=psname, half=half, ft=fts[q], mh=mh: e.activation(
                           out=T['g_t'][:, (ft % 2) * 2048 + mh * 1024 + half * 512:
                                         (ft % 2) * 2048 + mh * 1024 + (half + 1) * 512],
                           in_=T[psname][:, half * 512:(half + 1) * 512],
                           func=AF.Silu, scale=dq3_col(fts[q])))
        for q, ft in enumerate(fts):
            v_gw = dma('gpsimd', 'G_g', T['g_d'][ft * 128:(ft + 1) * 128, :],
                       T['g_t'][:, (ft % 2) * 2048:(ft % 2) * 2048 + 2048],
                       waits=[('s_evA', s.cnt['s_evA']), ('s_evB', s.cnt['s_evB'])])

    if stage < 6:
        dbg_taps(taps_l2() + [('d_gy', T['gy_in_d'][:]), ('d_h', T['h_d'][:]),
                              ('d_g', T['g_d'][:])])
        return s

    banks = [T['pA'][:, 0:512], T['pA'][:, 512:1024],
             T['pF'][:, 0:512], T['pF'][:, 512:1024]]
    v_otmp = [('s_vC', 0)] * 4
    v_tg_mm = 0
    nxg = [0]
    for tg in range(2):
        glv = 0
        for ka in range(4):
            glv = s.bump('G_gyl', 16)
            s.checks.append(('G_gyl', glv))
            if me('gpsimd'):
                if ka == 0:
                    s.wait('s_coll', v_gyAG)
                    if tg == 0:
                        s.wait('s_mmA', s.cnt['s_mmA'])
                        s.wait('s_mmS', s.cnt['s_mmS'])
                        s.wait('s_mmC', s.cnt['s_mmC'])
                    else:
                        s.wait('s_mmC', v_tg_mm)
                tbv = tokreg2[0] + tg
                eng.dma_start(
                    out=T['winT'][:].rearrange("p (k c) -> p k c", k=40)
                    [:, ka * 8:(ka + 1) * 8, :],
                    in_=T['gy_all_d'].rearrange("(k p) (tb c) -> p k tb c",
                                                p=128, c=512)
                    [:, ka * 8:(ka + 1) * 8, tbv, :]).then_inc(S['G_gyl'], 16)
        v_gyl = ('G_gyl', glv)
        for mc in range(4):
            vload = 0
            vm = 0
            for k in range(32):
                g4 = [wch_lastmm[-4]] if len(wch_lastmm) >= 4 else []
                if tg == 0 and mc == 0 and k == 0:
                    g4 += [('s_mmA', s.cnt['s_mmA']), ('s_mmB', s.cnt['s_mmB']),
                           ('s_mmC', s.cnt['s_mmC'])]
                vload = dma('sync' if k % 2 == 0 else 'scalar', 'L_wc%d' % (k % 4),
                            T['wch4'][k % 4][:],
                            T['woutTb_d'][k * 128:(k + 1) * 128, mc * 512:(mc + 1) * 512],
                            waits=g4 + [v_gd_w])
                for q in range(4):
                    gq = []
                    if k == 0:
                        gq = [v_gyl, v_otmp[q]]
                        if tg == 0 and mc == 0:
                            gq += [('s_evA', s.cnt['s_evA']), ('s_evB', s.cnt['s_evB'])]
                    vm = op('tensor', 's_mmC', [vload] + gq,
                            lambda e, q=q, k=k: e.matmul(
                                banks[q],
                                T['winT'][:, k * 512 + q * 128:k * 512 + (q + 1) * 128],
                                T['wch4'][k % 4][:], start=(k == 0), stop=(k == 31)))
                wch_lastmm.append(('s_mmC', vm))
            v_tg_mm = vm
            for q in range(4):
                tt = tg * 4 + q
                ix = nxg[0] % 2
                nxg[0] += 1
                vxr = dma('sync', 'L_xo%d' % ix, T['xch'][ix][:],
                          T['xtok_d'][tt * 128:(tt + 1) * 128, mc * 512:(mc + 1) * 512],
                          waits=[('s_vout', max(0, s.cnt['s_vout'] - 1))])
                vgr = dma('gpsimd', 'G_gr%d' % ix, T['gch'][ix][:],
                          T['g_d'][tt * 128:(tt + 1) * 128, mc * 512:(mc + 1) * 512],
                          waits=[v_gw, ('s_vC', max(0, s.cnt['s_vC'] - 1))])
                v_otmp[q] = ('s_vC', op('vector', 's_vC', [('s_mmC', vm), vgr],
                                        lambda e, q=q, ix=ix: e.tensor_mul(
                                            T['otmp'][ix], banks[q], T['gch'][ix][:])))
                so = 's_out%d' % ix
                vo = op('vector', 's_vout', [vxr, (so, s.cnt[so])],
                        lambda e, ix=ix: e.tensor_add(
                            T['out_t'][ix], T['otmp'][ix], T['xch'][ix][:]))
                v_out = s.bump(so, 16)
                s.checks.append((so, v_out))
                if me('gpsimd'):
                    s.wait('s_vout', vo)
                    eng.dma_start(out=T['out_d'][tt * 128:(tt + 1) * 128,
                                                 mc * 512:(mc + 1) * 512],
                                  in_=T['out_t'][ix]).then_inc(S[so], 16)

    if T.get('dbg'):
        dma('gpsimd', 'G_dbg', T['d_gy'][:], T['gy_in_d'][:],
            waits=[('s_vout', s.cnt['s_vout'])])
        dma('gpsimd', 'G_dbg', T['d_agst'][:], T['agst_in'][:])
        dma('gpsimd', 'G_dbg', T['d_h'][:], T['h_d'][:])
        dma('gpsimd', 'G_dbg', T['d_g'][:], T['g_d'][:])
        dma('gpsimd', 'G_dbg', T['d_wq'][:], T['agwq_in'][:])
        dma('gpsimd', 'G_dbg', T['d_stc'][:], T['stc'][:])
        v_dbg = dma('gpsimd', 'G_dbg', T['d_fcol'][:], T['fcol'][:])
        if me('gpsimd'):
            s.wait(*v_dbg)
    s.wait('s_out0', s.cnt['s_out0'])
    s.wait('s_out1', s.cnt['s_out1'])
    return s


_cache = {}


def _build(sim=False, stage=99, nwin=NWIN, wstage=99, dbg=False):
    nc = bass.Bass("TRN2", target_bir_lowering=False)
    dt = nc.dram_tensor
    T = {}
    T['x_all_d'] = dt("x_all", [B * L, D], F32, kind="ExternalInput").ap()
    T['xtok_d'] = dt("x_tok", [TOK, D], F32, kind="ExternalInput").ap()
    T['winT_d'] = dt("winT", [D, 1280], F32, kind="ExternalInput").ap()
    T['wdtT_d'] = dt("wdtT", [128, 512], F32, kind="ExternalInput").ap()
    T['woutT_d'] = dt("woutT", [DIN, D], F32, kind="ExternalInput").ap()
    T['w1T_d'] = dt("w1T", [D, DFC], F32, kind="ExternalInput").ap()
    T['w2T_d'] = dt("w2T", [D, DFC], F32, kind="ExternalInput").ap()
    T['w3T_d'] = dt("w3T", [DFC, D], F32, kind="ExternalInput").ap()
    T['bdt_d'] = dt("bdt", [IC], F32, kind="ExternalInput").ap()
    T['convw_d'] = dt("convw", [IC, 4], F32, kind="ExternalInput").ap()
    T['convb_d'] = dt("convb", [IC], F32, kind="ExternalInput").ap()
    T['alog_d'] = dt("alog", [IC, DST], F32, kind="ExternalInput").ap()
    T['dpar_d'] = dt("dpar", [IC], F32, kind="ExternalInput").ap()
    T['ident_d'] = dt("ident", [128, 128], F32, kind="ExternalInput").ap()
    T['indB_d'] = dt("indB", [128, 64], F32, kind="ExternalInput").ap()
    T['indC_d'] = dt("indC", [16, 128, 128], F32, kind="ExternalInput").ap()
    T['indU_d'] = dt("indU", [4, 128, 128], F32, kind="ExternalInput").ap()
    T['tokb_d'] = dt("tokbase", [1, 1], U32, kind="ExternalInput").ap()
    T['tokb2_d'] = dt("tokbase2", [1, 1], U32, kind="ExternalInput").ap()
    T['out_d'] = dt("out", [TOK, D], F32, kind="ExternalOutput").ap()
    T['woutTb_d'] = dt("woutTb", [DIN, D], BF16, kind="Internal").ap()
    T['gy_in_d'] = dt("gy_in", [IC, B * L], BF16, kind="Internal").ap()
    T['gy_all_d'] = dt("gy_all", [DIN, B * L], BF16, kind="Internal").ap()
    T['agst_in'] = dt("agst_in", [2, TOK], F32, kind="Internal").ap()
    T['agst_out'] = dt("agst_out", [NCORE, 2 * TOK], F32, kind="Internal").ap()
    T['agwr_in'] = dt("agwr_in", [1, 8], F32, kind="Internal").ap()
    T['agwr_out'] = dt("agwr_out", [1, 8], F32, kind="Internal").ap()
    T['agwq_in'] = dt("agwq_in", [3 * DFC * D], BF16, kind="Internal").ap()
    T['agwq_out'] = dt("agwq_out", [NCORE * 3 * DFC * D], BF16, kind="Internal").ap()
    T['h_d'] = dt("h_scr", [TOK, DFF], F32, kind="Internal").ap()
    T['xqh_d'] = dt("xqh_scr", [NTC * 64 * 128 * 128], BF16, kind="Internal").ap()
    T['g_d'] = dt("g_scr", [TOK, D], BF16, kind="Internal").ap()
    if dbg:
        T['d_gy'] = dt("d_gy", [IC, B * L], BF16, kind="ExternalOutput").ap()
        T['d_agst'] = dt("d_agst", [2, TOK], F32, kind="ExternalOutput").ap()
        T['d_h'] = dt("d_h", [TOK, DFF], F32, kind="ExternalOutput").ap()
        T['d_g'] = dt("d_g", [TOK, D], BF16, kind="ExternalOutput").ap()
        T['d_wq'] = dt("d_wq", [3 * DFC * D], BF16, kind="ExternalOutput").ap()
        T['d_stc'] = dt("d_stc", [128, 64], F32, kind="ExternalOutput").ap()
        T['d_fcol'] = dt("d_fcol", [128, 64], F32, kind="ExternalOutput").ap()
    T['dbg'] = dbg

    from contextlib import ExitStack
    ctx = ExitStack()
    with ctx:
        def sb(name, shape, dtp):
            return ctx.enter_context(nc.sbuf_tensor(name, shape, dtp))

        T['winT'] = sb("winT_s", [128, 16 * 10 * 128], BF16)
        T['wdtT'] = sb("wdtT_s", [128, 512], BF16)
        T['identb'] = sb("identb_s", [128, 128], BF16)
        T['indB'] = sb("indB_s", [128, 64], BF16)
        T['indC'] = sb("indC_s", [128, 16 * 128], BF16)
        T['indU'] = sb("indU_s", [128, 4 * 128], BF16)
        T['bdtc'] = sb("bdtc", [128, 4], F32)
        T['bdt2c'] = sb("bdt2c", [128, 4], F32)
        T['convbc'] = sb("convbc", [128, 4], F32)
        T['dparc'] = sb("dparc", [128, 4], F32)
        T['convwc'] = sb("convwc", [128, 16], F32)
        T['alogs'] = sb("alogs", [128, 64], F32)
        T['A_t'] = sb("A_t", [128, 64], F32)
        T['tokb_s'] = sb("tokb_s", [1, 1], U32)
        T['tokb2_s'] = sb("tokb2_s", [1, 1], U32)
        T['scr32'] = sb("scr32", [128, 2048], F32)
        T['scri'] = sb("scri", [128, 1024], I32)
        T['scrh'] = sb("scrh", [128, 2048], F32)
        T['scrh2'] = sb("scrh2", [128, 2048], F32)
        T['scrhb'] = sb("scrhb", [128, 2048], BF16)
        T['xtok'] = sb("xtok", [128, 2048], F32)
        T['stc'] = sb("stc", [128, 64], F32)
        T['tmpc'] = sb("tmpc", [128, 8], F32)
        T['coltiles'] = sb("coltiles", [128, 128], F32)
        T['fcol'] = sb("fcol", [128, 64], F32)
        T['wabs'] = sb("wabs", [128, 4], F32)
        T['onescol'] = sb("onescol", [128, 1], BF16)
        T['onescol32'] = sb("onescol32", [128, 1], F32)
        T['wsum_row'] = sb("wsum_row", [1, 8], F32)
        T['wsum_row2'] = sb("wsum_row2", [1, 4], F32)
        T['wsum_row2b'] = sb("wsum_row2b", [1, 4], BF16)
        T['swcol'] = sb("swcol", [128, 8], F32)
        T['xg'] = [sb("xg0", [128, 2048], F32)]
        T['xn_tok'] = [sb("xnt0", [128, 2048], BF16)]
        T['xn_fm'] = sb("xn_fm", [128, 16 * 512], BF16)
        T['z_sil'] = sb("z_sil", [128, 4 * 512], BF16)
        T['B_fm'] = sb("B_fm", [128, 512], BF16)
        T['C_fm'] = sb("C_fm", [128, 4 * 512], BF16)
        T['dtl_fm'] = sb("dtl_fm", [128, 512], BF16)
        T['delta'] = sb("delta", [128, 4 * 512], BF16)
        T['e_tmp'] = sb("e_tmp", [128, 512], F32)
        T['u_in'] = sb("u_in", [128, 4 * 516], F32)
        T['utail'] = sb("utail", [128, 16], F32)
        T['cacc'] = sb("cacc", [128, 512], F32)
        T['u_exp'] = sb("u_exp", [128, 4 * 512], BF16)
        T['du'] = sb("du", [128, 4 * 512], BF16)
        T['dAt'] = [sb("dAt0", [128, 512], F32), sb("dAt1", [128, 512], F32)]
        T['Bb'] = [sb("Bb0", [128, 512], BF16), sb("Bb1", [128, 512], BF16)]
        T['bt'] = [sb("bt0", [128, 512], BF16), sb("bt1", [128, 512], BF16)]
        T['st_s'] = [sb("st0", [128, 512], F32), sb("st1", [128, 512], F32)]
        T['prods'] = sb("prods", [128, 8 * 512], BF16)
        T['yhalf'] = sb("yhalf", [128, 512], F32)
        T['ytmp'] = sb("ytmp", [128, 512], F32)
        T['gy_t'] = [sb("gyt0", [128, 512], BF16), sb("gyt1", [128, 512], BF16)]
        T['carry'] = sb("carry", [128, 64], F32)
        T['xq_tok'] = sb("xq_tok", [128, 2048], BF16)
        T['xq_fm'] = sb("xq_fm", [128, 16 * 512], BF16)
        T['wch'] = [sb("wch0", [128, 512], BF16), sb("wch1", [128, 512], BF16)]
        T['silu1'] = sb("silu1", [128, 2048], BF16)
        T['hchunk'] = [sb("hch0", [128, 512], F32), sb("hch1", [128, 512], F32)]
        T['hps'] = sb("hps", [128, 8], F32)
        T['hpm'] = sb("hpm", [128, 8], F32)
        T['l3l0'] = [sb("l3l0a", [128, 128], BF16), sb("l3l0b", [128, 128], BF16)]
        T['l3l1'] = [sb("l3l1a", [128, 128], BF16), sb("l3l1b", [128, 128], BF16)]


        T['xqh_w'] = T['scrhb']
        T['g_t'] = T['xn_fm'][:, 0:4096]
        T['otmp'] = [T['scr32'][:, 0:512], T['scr32'][:, 1536:2048]]
        T['out_t'] = [T['scr32'][:, 512:1024], T['scr32'][:, 1024:1536]]
        T['w3ch'] = [T['z_sil'][:, 0:1024], T['z_sil'][:, 1024:2048]]
        T['gyb'] = T['winT'][:, 0:32 * 512]
        T['wch4'] = T['wch'] + [T['xq_fm'][:, 0:512], T['xq_fm'][:, 512:1024]]
        T['gch'] = [T['xq_fm'][:, 1024:1536], T['xq_fm'][:, 1536:2048]]
        T['xch'] = [T['scrh'][:, 0:512], T['scrh'][:, 512:1024]]
        T['pA'] = ctx.enter_context(nc.psum_tensor("pA", [128, 1024], F32))
        T['pB'] = ctx.enter_context(nc.psum_tensor("pB", [128, 512], F32))
        T['pC'] = ctx.enter_context(nc.psum_tensor("pC", [128, 1024], F32))
        T['pF'] = ctx.enter_context(nc.psum_tensor("pF", [128, 1024], F32))
        T['pT'] = ctx.enter_context(nc.psum_tensor("pT", [128, 1024], BF16))

        sems = {nm: ctx.enter_context(nc.semaphore(nm)) for nm in SEM_NAMES}
        block = ctx.enter_context(nc.Block())
        walks = {}

        @block.sync
        def _(eng):
            walks['sync'] = _emit(nc, sems, 'sync', eng, T, sim=sim, stage=stage, nwin=nwin, wstage=wstage)

        @block.gpsimd
        def _(eng):
            walks['gpsimd'] = _emit(nc, sems, 'gpsimd', eng, T, sim=sim, stage=stage, nwin=nwin, wstage=wstage)

        @block.scalar
        def _(eng):
            walks['scalar'] = _emit(nc, sems, 'scalar', eng, T, sim=sim, stage=stage, nwin=nwin, wstage=wstage)

        @block.vector
        def _(eng):
            walks['vector'] = _emit(nc, sems, 'vector', eng, T, sim=sim, stage=stage, nwin=nwin, wstage=wstage)

        @block.tensor
        def _(eng):
            walks['tensor'] = _emit(nc, sems, 'tensor', eng, T, sim=sim, stage=stage, nwin=nwin, wstage=wstage)

        ref = walks['sync'].checks
        for nm2, wk in walks.items():
            assert wk.checks == ref, f"schedule divergence in {nm2}"
    return nc


def kernel(**inputs):
    in_maps = _host_prepare(inputs)
    if 'nc' not in _cache:
        _cache['nc'] = _build()
    res = run_bass_kernel_spmd(_cache['nc'], in_maps, core_ids=list(range(NCORE)))
    out = np.concatenate([r["out"] for r in res.results], axis=0)
    return np.ascontiguousarray(out.reshape(B, L, D).astype(np.float32))


# version 21
# speedup vs baseline: 1.1473x; 1.0329x over previous
import numpy as np
import concourse.bass as bass
import concourse.mybir as mybir
from concourse.bass_utils import run_bass_kernel_spmd

F32 = mybir.dt.float32
BF16 = mybir.dt.bfloat16
I32 = mybir.dt.int32
U32 = mybir.dt.uint32
AF = mybir.ActivationFunctionType
ALU = mybir.AluOpType
AX = mybir.AxisListType

NCORE = 8
B = 2
L = 4096
D = 2048
DIN = 4096
DST = 16
IC = DIN // NCORE
TOK = (B * L) // NCORE
NTC = TOK // 128
WN = 512
NWIN = B * L // WN
WPB = L // WN
NJ = 4
DFF = 8192
DFC = DFF // NCORE

MAGIC = 12582912.0

SEM_NAMES = ["s_ld", "s_gd", "s_coll",
             "L_c", "L_w", "L_w2", "L_x", "L_xt", "L_wc0", "L_wc1", "L_wc2", "L_wc3",
             "L_w30", "L_w31", "L_h", "L_xo0", "L_xo1", "G_gr0", "G_gr1",
             "G_c", "G_wo", "G_wo2", "G_st", "G_rb", "G_wr", "G_wq", "G_wq2",
             "G_h0", "G_h1",
             "G_gy0", "G_gy1", "G_xq", "G_l30", "G_l31", "G_g", "G_gyl", "G_grd",
             "G_dbg",
             "s_mmA", "s_mmB", "s_mmC", "s_mmT", "s_mmS",
             "s_evA", "s_evB", "s_tpe", "s_xn", "s_dA", "s_scm",
             "s_vC", "s_vF", "s_vcast", "s_vstat", "s_vmisc", "s_vq", "s_vh",
             "s_vgy", "s_vbt", "s_vscan", "s_varg", "s_vout", "s_out",
             "s_out0", "s_out1"]


def _host_prepare(inp):
    x = np.ascontiguousarray(
        np.asarray(inp["L0_attn_hidden_states"], np.float32).reshape(B * L, D))
    W_in = np.asarray(inp["L0_attn_W_in"], np.float32)
    W_dt = np.asarray(inp["L0_attn_W_dt"], np.float32)
    b_dt = np.asarray(inp["L0_attn_b_dt"], np.float32)
    conv_w = np.asarray(inp["L0_attn_conv_w"], np.float32)
    conv_b = np.asarray(inp["L0_attn_conv_b"], np.float32)
    A_log = np.asarray(inp["L0_attn_A_log"], np.float32)
    D_par = np.asarray(inp["L0_attn_D_param"], np.float32)
    W_out = np.asarray(inp["L0_attn_W_out"], np.float32)
    w1 = np.asarray(inp["L0_ffn_w1"], np.float32)
    w2 = np.asarray(inp["L0_ffn_w2"], np.float32)
    w3 = np.asarray(inp["L0_ffn_w3"], np.float32)

    ident = np.eye(128, dtype=np.float32)
    indB = np.zeros((128, NJ * DST), np.float32)
    for j in range(NJ):
        for n in range(DST):
            indB[64 + j * 16 + n, j * DST + n] = 1.0
    indC = np.zeros((DST, 128, 128), np.float32)
    for n in range(DST):
        for p in range(128):
            indC[n, (p // 16) * 16 + n, p] = 1.0
    indU = np.zeros((NJ, 128, 128), np.float32)
    for j in range(NJ):
        for p in range(128):
            indU[j, j * 16 + (p % 16), p] = 1.0

    woutT = np.ascontiguousarray(W_out.T)

    in_maps = []
    for c in range(NCORE):
        zc = slice(c * IC, (c + 1) * IC)
        rows = np.concatenate([
            W_in[zc],
            W_in[DIN + c * 64:DIN + (c + 1) * 64],
            W_in[DIN + 512 + c * 64:DIN + 512 + (c + 1) * 64],
            W_in[DIN + 1024 + c * IC:DIN + 1024 + (c + 1) * IC],
            W_in[DIN + 1024 + DIN:],
        ], axis=0)
        in_maps.append(dict(
            x_all=x,
            x_tok=np.ascontiguousarray(x[c * TOK:(c + 1) * TOK]),
            winT=np.ascontiguousarray(rows.T),
            wdtT=np.ascontiguousarray(W_dt[zc].T),
            woutT=woutT,
            w1T=np.ascontiguousarray(w1[c * DFC:(c + 1) * DFC].T),
            w2T=np.ascontiguousarray(w2[c * DFC:(c + 1) * DFC].T),
            w3T=np.ascontiguousarray(w3[:, c * DFC:(c + 1) * DFC].T),
            bdt=np.ascontiguousarray(b_dt[zc]),
            convw=np.ascontiguousarray(conv_w[zc, 0, :]),
            convb=np.ascontiguousarray(conv_b[zc]),
            alog=np.ascontiguousarray(A_log[zc]),
            dpar=np.ascontiguousarray(D_par[zc]),
            ident=ident, indB=indB, indC=indC, indU=indU,
            tokbase=np.array([[c * NTC]], dtype=np.uint32),
            tokbase2=np.array([[c * 2]], dtype=np.uint32),
        ))
    return in_maps


class Walk:
    def __init__(self, sems, engine_name, eng):
        self.S = sems
        self.E = engine_name
        self.eng = eng
        self.cnt = {k: 0 for k in sems}
        self.lastwait = {}
        self.iA = self.iB = self.iC = self.iT = 0
        self.checks = []

    def me(self, name):
        return self.E == name

    def bump(self, sem, by=1):
        self.cnt[sem] += by
        return self.cnt[sem]

    def wait(self, sem, val):
        if val is None or val <= 0:
            return
        if self.lastwait.get(sem, -1) >= val:
            return
        self.lastwait[sem] = val
        self.eng.wait_ge(self.S[sem], val)


def _emit(nc, sems, E, eng, T, sim=False, stage=99, nwin=NWIN, wstage=99):
    s = Walk(sems, E, eng)
    S = s.S
    me = s.me

    def dma(queue, sem, out, in_, waits=(), slow=False):
        v = s.bump(sem, 16)
        s.checks.append((sem, v))
        if me(queue):
            for (sm, vl) in waits:
                s.wait(sm, vl)
            eng.dma_start(out=out, in_=in_,
                          allow_slow_non_contiguous=slow).then_inc(S[sem], 16)
        return (sem, v)

    prev_op = {}

    def op(engine, sem, waits, emitfn, dep=False):
        v = s.bump(sem, 1)
        s.checks.append((sem, v))
        w = list(waits)
        if dep and engine in prev_op:
            w.append(prev_op[engine])
        prev_op[engine] = (sem, v)
        if me(engine):
            for (sm, vl) in w:
                s.wait(sm, vl)
            emitfn(eng).then_inc(S[sem], 1)
        return v

    def coll(kind, alu, ins, outs, waits):
        v = s.bump('s_coll', 16 if sim else 1)
        s.checks.append(('s_coll', v))
        if me('gpsimd'):
            for (sm, vl) in waits:
                s.wait(sm, vl)
            if sim:
                n = ins[0].size()
                eng.dma_start(out=outs[0].flatten()[0:n],
                              in_=ins[0].flatten()[0:n]).then_inc(S['s_coll'], 16)
            else:
                eng.collective_compute(kind, alu, replica_groups=[list(range(NCORE))],
                                      ins=ins, outs=outs).then_inc(S['s_coll'], 1)
        return v

    def slotA():
        s.iA += 1
        i = s.iA
        return T['pA'][:, ((i - 1) % 2) * 512:(((i - 1) % 2) + 1) * 512], ('s_evA', i - 2)

    def slotB():
        s.iB += 1
        i = s.iB
        return T['pB'][:, 0:512], ('s_evB', i - 1)

    def slotC():
        s.iC += 1
        i = s.iC
        return T['pC'][:, ((i - 1) % 2) * 512:(((i - 1) % 2) + 1) * 512], ('s_vC', i - 2)

    def slotTG():
        s.iT += 1
        g = s.iT
        return 0, ('s_tpe', g - 1)

    def tp_group(src_tile, blocks, dst_fn, vwait):
        base, gate = slotTG()
        vt = 0
        for bi, blk in enumerate(blocks):
            ps = T['pT'][:, base + bi * 128:base + (bi + 1) * 128]
            g0 = [vwait, ('s_vcast', v_idb), gate] if bi == 0 else []
            vt = op('tensor', 's_mmT', g0,
                    lambda e, ps=ps, blk=blk, src_tile=src_tile: e.transpose(
                        ps, src_tile[:, blk * 128:(blk + 1) * 128], T['identb'][:]))
        bank = T['pT'][:, base:base + 1024]
        return op('scalar', 's_tpe', [('s_mmT', vt)],
                  lambda e, bank=bank: e.copy(out=dst_fn(), in_=bank))

    def winT(k, o):
        return T['winT'][:].rearrange("p (k o c) -> p k o c", k=16, o=10)[:, k, o, :]

    def xnfm(k):
        return T['xn_fm'][:].rearrange("p (k c) -> p k c", k=16)[:, k, :]

    def xqfm(k, q):
        return T['xq_fm'][:].rearrange("p (k c) -> p k c", k=16)[:, k, q * 128:(q + 1) * 128]

    def stv(kk):
        return T['stc'][:, kk * 8:(kk + 1) * 8]

    def stcol(kk, tt):
        return T['stc'][:, kk * 8 + tt:kk * 8 + tt + 1]

    if me('vector'):
        eng.memset(T['wabs'][:], 0.0)
        eng.memset(T['wsum_row'][:], 0.0)
        eng.memset(T['fcol'][:], 0.0)
        eng.memset(T['onescol'][:], 1.0)
        eng.memset(T['onescol32'][:], 1.0)
        eng.memset(T['utail'][:], 0.0)
        eng.memset(T['u_in'][:], 0.0)
        eng.memset(T['carry'][:], 0.0)
        eng.memset(T['hpm'][:], 0.0)
        eng.memset(T['hps'][:], 0.0)

    dma('gpsimd', 'G_c', T['identb'][:], T['ident_d'])
    dma('gpsimd', 'G_c', T['indB'][:], T['indB_d'])
    dma('gpsimd', 'G_c', T['indC'][:].rearrange("p (a c) -> p a c", a=16), T['indC_d'].transpose([1, 0, 2]))
    g_consts = dma('gpsimd', 'G_c', T['indU'][:].rearrange("p (a c) -> p a c", a=4), T['indU_d'].transpose([1, 0, 2]))
    for j in range(NJ):
        dma('sync', 'L_c', T['bdtc'][:, j:j + 1],
            T['bdt_d'].rearrange("(a p) -> a p", p=128)[j].unsqueeze(1))
        dma('sync', 'L_c', T['convbc'][:, j:j + 1],
            T['convb_d'].rearrange("(a p) -> a p", p=128)[j].unsqueeze(1))
        dma('sync', 'L_c', T['dparc'][:, j:j + 1],
            T['dpar_d'].rearrange("(a p) -> a p", p=128)[j].unsqueeze(1))
        dma('sync', 'L_c', T['convwc'][:, j * 4:(j + 1) * 4], T['convw_d'][j * 128:(j + 1) * 128, :])
        dma('sync', 'L_c', T['alogs'][:, j * 16:(j + 1) * 16], T['alog_d'][j * 128:(j + 1) * 128, :])
    dma('sync', 'L_c', T['tokb_s'][:], T['tokb_d'])
    dma('sync', 'L_c', T['tokb2_s'][:], T['tokb2_d'])
    v_consts = ('L_c', s.cnt['L_c'])
    v_tokb = v_consts
    v_gconsts = g_consts

    v_idb = op('vector', 's_vcast', [g_consts],
               lambda e: e.engine_nop())
    va = op('scalar', 's_scm', [v_consts],
            lambda e: e.activation(out=T['A_t'][:], in_=T['alogs'][:], func=AF.Exp))
    op('vector', 's_vcast', [('s_scm', va)],
       lambda e: e.tensor_scalar_mul(T['A_t'][:], T['A_t'][:], -1.0))
    op('vector', 's_vcast', [v_consts],
       lambda e: e.tensor_scalar_mul(T['bdt2c'][:], T['bdtc'][:], 2.0))

    scrhP = [T['scrh'], T['scrh2']]
    L_wP = ['L_w', 'L_w2']

    v = dma('sync', 'L_w', T['scrh'][:, 0:512], T['wdtT_d'], waits=[('s_vcast', v_idb)])
    vcast_h = [0, 0]
    vcast_h[0] = op('vector', 's_vcast', [v],
                    lambda e: e.tensor_copy(out=T['wdtT'][:], in_=T['scrh'][:, 0:512]))

    for o in range(10):
        p = o % 2
        src = T['winT_d'].rearrange("(k p) c -> p k c", p=128)[:, :, o * 128:(o + 1) * 128]
        v = dma('sync', L_wP[p], scrhP[p][:].rearrange("p (k c) -> p k c", c=128)[:, 0:16, :],
                src, waits=[('s_vcast', vcast_h[p])])
        vcast_h[p] = op('vector', 's_vcast', [v],
                        lambda e, o=o, p=p: e.tensor_copy(
                            out=T['winT'][:].rearrange("p (k o c) -> p k o c", k=16, o=10)[:, :, o, :],
                            in_=scrhP[p][:].rearrange("p (k c) -> p k c", c=128)[:, 0:16, :]))
    v_wincast = s.cnt['s_vcast']

    v_gd_w = ('G_wo', 0)
    for k in range(32):
        p = k % 2
        v = dma('sync', L_wP[p], scrhP[p][:, 0:2048], T['woutT_d'][k * 128:(k + 1) * 128, :],
                waits=[('s_vcast', vcast_h[p])])
        vcast = op('vector', 's_vcast', [v, v_gd_w],
                   lambda e, p=p: e.tensor_copy(out=T['scrhb'][:, 0:2048],
                                                in_=scrhP[p][:, 0:2048]))
        vcast_h[p] = vcast
        v_gd_w = dma('gpsimd', 'G_wo', T['woutTb_d'][k * 128:(k + 1) * 128, :],
                     T['scrhb'][:, 0:2048], waits=[('s_vcast', vcast)])

    v_stt = 0
    v_sqq = 0
    v_xload = 0
    for tt in range(NTC):
        v_xload = dma('sync', 'L_xt', T['xtok'][:], T['xtok_d'][tt * 128:(tt + 1) * 128, :],
                      waits=[('s_vstat', v_stt), ('s_scm', v_sqq)])
        v_sqq = op('scalar', 's_scm', [v_xload, ('s_vstat', v_stt)],
                   lambda e, tt=tt: e.activation(
                       out=T['scr32'][:, 0:2048], in_=T['xtok'][:], func=AF.Square,
                       accum_out=stcol(1, tt)))
        vred = op('vector', 's_vstat', [v_xload],
                  lambda e, tt=tt: e.reduce_sum(out=stcol(0, tt), in_=T['xtok'][:],
                                                axis=AX.X))
        op('vector', 's_vstat', [('s_vstat', vred)],
           lambda e, tt=tt: e.tensor_scalar_mul(stcol(2, tt), stcol(0, tt), 1.0 / D))
        op('vector', 's_vstat', [('s_scm', v_sqq)],
           lambda e, tt=tt: e.tensor_scalar_mul(stcol(3, tt), stcol(1, tt), 1.0 / D),
           dep=True)
        op('vector', 's_vstat', [],
           lambda e, tt=tt: e.tensor_mul(T['tmpc'][:, 0:1], stcol(2, tt), stcol(2, tt)),
           dep=True)
        op('vector', 's_vstat', [],
           lambda e, tt=tt: e.tensor_sub(stcol(3, tt), stcol(3, tt), T['tmpc'][:, 0:1]),
           dep=True)
        op('vector', 's_vstat', [('s_scm', v_sqq)],
           lambda e, tt=tt: e.tensor_scalar_sub(T['scr32'][:, 0:2048], T['xtok'][:],
                                                stcol(2, tt)), dep=True)
        v_stt = op('vector', 's_vstat', [],
                   lambda e, tt=tt: e.reduce_max(
                       out=stcol(7, tt), in_=T['scr32'][:, 0:2048],
                       axis=AX.X, apply_absolute_value=True), dep=True)

    v_eps = op('vector', 's_vstat', [],
               lambda e: e.tensor_scalar_add(stv(4), stv(3), 1e-5), dep=True)
    vr0 = op('scalar', 's_scm', [('s_vstat', v_eps)],
             lambda e: e.activation(out=stv(4), in_=stv(4), func=AF.Sqrt))
    op('vector', 's_vstat', [('s_scm', vr0)],
       lambda e: e.reciprocal(out=stv(4), in_=stv(4)))
    vr = op('scalar', 's_scm', [('s_vstat', s.cnt['s_vstat'])],
            lambda e: e.nop())
    op('vector', 's_vstat', [('s_scm', vr)], lambda e: e.tensor_mul(stv(5), stv(2), stv(4)),
       dep=True)
    op('vector', 's_vstat', [], lambda e: e.tensor_scalar_mul(stv(5), stv(5), -1.0),
       dep=True)
    op('vector', 's_vstat', [], lambda e: e.reciprocal(out=stv(6), in_=stv(7)), dep=True)
    v_statq = op('vector', 's_vstat', [],
                 lambda e: e.tensor_scalar_mul(stv(6), stv(6), 127.0), dep=True)

    v_pay = 0
    for tt in range(NTC):
        dma('gpsimd', 'G_st', T['agst_in'][0, tt * 128:(tt + 1) * 128].rearrange("(p o) -> p o", o=1),
            T['stc'][:, 4 * 8 + tt:4 * 8 + tt + 1], waits=[('s_vstat', v_statq)])
        v_pay = dma('gpsimd', 'G_st', T['agst_in'][1, tt * 128:(tt + 1) * 128].rearrange("(p o) -> p o", o=1),
                    T['stc'][:, 5 * 8 + tt:5 * 8 + tt + 1], waits=[('s_vstat', v_statq)])
    v_stAG = coll("AllGather", ALU.bypass, [T['agst_in'][:]], [T['agst_out'][:]],
                  [v_pay])
    v_rb = 0
    for cc in range(NCORE):
        for kk in range(2):
            for tl in range(8):
                v_rb = dma('gpsimd', 'G_rb',
                           T['coltiles'][:].rearrange("p (c k t) -> p c k t", c=8, k=2)
                           [:, cc, kk, tl:tl + 1],
                           T['agst_out'][cc, kk * TOK + tl * 128:kk * TOK + (tl + 1) * 128]
                           .rearrange("(p o) -> p o", o=1),
                           waits=[('s_coll', v_stAG)])

    def dbg_taps(keys):
        if not T.get('dbg'):
            return
        allw = [('s_vstat', s.cnt['s_vstat']), ('s_scm', s.cnt['s_scm']),
                ('s_vq', s.cnt['s_vq']), ('s_vout', s.cnt['s_vout']),
                ('s_vh', s.cnt['s_vh']), ('s_coll', s.cnt['s_coll']),
                ('G_st', s.cnt['G_st']), ('G_wq', s.cnt['G_wq']),
                ('G_wq2', s.cnt['G_wq2']),
                ('G_gy0', s.cnt['G_gy0']), ('G_gy1', s.cnt['G_gy1']),
                ('G_h0', s.cnt['G_h0']), ('G_h1', s.cnt['G_h1']),
                ('G_g', s.cnt['G_g'])]
        v_dbg = None
        for k, src in keys:
            v_dbg = dma('gpsimd', 'G_dbg', T[k][:], src, waits=allw)
        if v_dbg is not None and me('gpsimd'):
            s.wait(*v_dbg)

    def taps_l1():
        return [('d_stc', T['stc'][:]), ('d_agst', T['agst_in'][:])]

    def taps_l2():
        return taps_l1() + [('d_wq', T['agwq_in'][:]), ('d_fcol', T['fcol'][:])]

    if stage < 1:
        dbg_taps(taps_l1())
        return s

    def r_col(tt):
        return T['coltiles'][:].rearrange("p (c k t) -> p c k t", c=8, k=2)[:, tt // 8, 0,
                                                                            tt % 8:tt % 8 + 1]

    def b_col(tt):
        return T['coltiles'][:].rearrange("p (c k t) -> p c k t", c=8, k=2)[:, tt // 8, 1,
                                                                            tt % 8:tt % 8 + 1]

    wsrc = [('w1T_d', 16, 1024, 0), ('w2T_d', 16, 1024, DFC * D),
            ('w3T_d', 8, 2048, 2 * DFC * D)]
    v_ab = 0
    v_abS_p = [0, 0]
    acc_col = [T['tmpc'][:, 1:2], T['tmpc'][:, 4:5]]
    ti = 0
    for mi, (wd, ntile, wid, off) in enumerate(wsrc):
        for t in range(ntile):
            p = ti % 2
            ti += 1
            v = dma('sync', L_wP[p], scrhP[p][:, 0:wid], T[wd][t * 128:(t + 1) * 128, :],
                    waits=[('s_scm', v_abS_p[p]), ('s_vcast', vcast_h[p])])
            v_abS_p[p] = op('scalar', 's_scm', [v, ('s_vq', v_ab), v_gd_w],
                            lambda e, wid=wid, p=p: e.activation(
                                out=T['scrhb'][:, 0:wid], in_=scrhP[p][:, 0:wid], func=AF.Abs,
                                accum_out=acc_col[p]))
            v_ab = op('vector', 's_vq', [('s_scm', v_abS_p[p])],
                      lambda e, mi=mi, p=p: e.tensor_add(T['wabs'][:, mi:mi + 1],
                                                         T['wabs'][:, mi:mi + 1], acc_col[p]),
                      dep=True)
    v_absF = max(v_abS_p)
    psS, gateS = slotA()
    vmS = op('tensor', 's_mmS', [('s_vq', v_ab), ('s_vcast', v_idb), gateS],
             lambda e, psS=psS: e.matmul(psS[0:1, 0:4], T['onescol32'][:], T['wabs'][:, 0:4],
                                         start=True, stop=True))
    vev = op('scalar', 's_evA', [('s_mmS', vmS)],
             lambda e, psS=psS: e.copy(out=T['wsum_row'][:, 0:4], in_=psS[0:1, 0:4]))
    v_gdr = dma('gpsimd', 'G_wr', T['agwr_in'][:], T['wsum_row'][:], waits=[('s_evA', vev)])
    v_arW = coll("AllReduce", ALU.add, [T['agwr_in'][:]], [T['agwr_out'][:]],
                 [v_gdr])
    v_wsr = dma('gpsimd', 'G_wr', T['wsum_row'][:], T['agwr_out'][:], waits=[('s_coll', v_arW)])
    op('vector', 's_vq', [v_wsr],
       lambda e: e.tensor_scalar_mul(T['wsum_row'][:, 0:4], T['wsum_row'][:, 0:4],
                                     1.0 / (DFF * D)))
    op('vector', 's_vq', [],
       lambda e: e.tensor_scalar_max(T['wsum_row'][:, 0:4], T['wsum_row'][:, 0:4], 1e-5),
       dep=True)
    vW3b = op('vector', 's_vq', [],
              lambda e: e.reciprocal(out=T['wsum_row'][:, 4:8], in_=T['wsum_row'][:, 0:4]),
              dep=True)
    psS2, gateS2 = slotA()
    vmS2 = op('tensor', 's_mmS', [('s_vq', vW3b), gateS2],
              lambda e, psS2=psS2: e.matmul(psS2[:, 0:8],
                                            T['onescol32'][0:1, 0:1].broadcast_to([1, 128]),
                                            T['wsum_row'][:, 0:8], start=True, stop=True))
    vsc = op('scalar', 's_evA', [('s_mmS', vmS2)],
             lambda e, psS2=psS2: e.copy(out=T['swcol'][:], in_=psS2[:, 0:8]))

    v_qd = ('G_wq', 0)
    v_qv_p = [0, 0]
    ti = 0
    for mi, (wd, ntile, wid, off) in enumerate(wsrc):
        for t in range(ntile):
            p = ti % 2
            ti += 1
            v = dma('sync', L_wP[p], scrhP[p][:, 0:wid], T[wd][t * 128:(t + 1) * 128, :],
                    waits=[('s_vq', v_qv_p[p]), ('s_scm', v_absF), ('s_vcast', vcast_h[p])])
            op('vector', 's_vq', [v, ('s_evA', vsc)],
               lambda e, mi=mi, wid=wid, p=p: e.tensor_scalar(
                   out=T['scr32'][:, 0:wid], in0=scrhP[p][:, 0:wid],
                   scalar1=T['swcol'][:, 4 + mi:5 + mi], scalar2=MAGIC,
                   op0=ALU.mult, op1=ALU.add))
            op('vector', 's_vq', [],
               lambda e, wid=wid: e.tensor_scalar_sub(
                   T['scr32'][:, 0:wid], T['scr32'][:, 0:wid], MAGIC))
            for hh in range(0, wid, 1024):
                op('vector', 's_vq', [],
                   lambda e, hh=hh: e.tensor_copy(out=T['scri'][:],
                                                  in_=T['scr32'][:, hh:hh + 1024]))
                op('vector', 's_vq', [v_qd] if hh == 0 else [],
                   lambda e, hh=hh: e.tensor_copy(out=T['scrhb'][:, hh:hh + 1024],
                                                  in_=T['scri'][:]))
            v_qv_p[p] = op('vector', 's_vq', [],
                           lambda e, wid=wid: e.tensor_scalar(
                               out=T['scrhb'][:, 0:wid], in0=T['scrhb'][:, 0:wid],
                               scalar1=1.0, scalar2=-1.0, op0=ALU.min, op1=ALU.max))
            v_qd = dma('gpsimd', 'G_wq',
                       T['agwq_in'].rearrange("(r c) -> r c", c=wid)
                       [(off // wid) + t * 128:(off // wid) + (t + 1) * 128, :],
                       T['scrhb'][:, 0:wid], waits=[('s_vq', v_qv_p[p])])
    v_wqAG = coll("AllGather", ALU.bypass, [T['agwq_in'][:]], [T['agwq_out'][:]],
                  [v_qd])

    op('vector', 's_vstat', [('s_vstat', v_statq)],
       lambda e: e.tensor_mul(T['fcol'][:, 0:8], stv(4), stv(4)), dep=True)
    op('vector', 's_vstat', [],
       lambda e: e.tensor_mul(T['fcol'][:, 0:8], T['fcol'][:, 0:8], stv(3)), dep=True)
    op('vector', 's_vstat', [],
       lambda e: e.tensor_scalar_add(T['fcol'][:, 0:8], T['fcol'][:, 0:8], 1e-6), dep=True)
    vrs0 = op('scalar', 's_scm', [('s_vstat', s.cnt['s_vstat'])],
              lambda e: e.activation(out=T['fcol'][:, 0:8], in_=T['fcol'][:, 0:8],
                                     func=AF.Sqrt))
    op('vector', 's_vstat', [('s_scm', vrs0)],
       lambda e: e.reciprocal(out=T['fcol'][:, 0:8], in_=T['fcol'][:, 0:8]))
    vrs = op('scalar', 's_scm', [('s_vstat', s.cnt['s_vstat'])],
             lambda e: e.nop())
    op('vector', 's_vstat', [('s_scm', vrs)],
       lambda e: e.tensor_mul(T['fcol'][:, 8:16], T['fcol'][:, 0:8], stv(4)), dep=True)
    op('vector', 's_vstat', [],
       lambda e: e.tensor_mul(T['fcol'][:, 8:16], T['fcol'][:, 8:16], stv(7)), dep=True)
    op('vector', 's_vstat', [],
       lambda e: e.tensor_scalar_mul(T['fcol'][:, 8:16], T['fcol'][:, 8:16], 1.0 / 127.0),
       dep=True)
    op('vector', 's_vstat', [('s_evA', vsc)],
       lambda e: e.tensor_scalar_mul(T['fcol'][:, 16:24], T['fcol'][:, 8:16],
                                     T['swcol'][:, 0:1]), dep=True)
    v_fc = op('vector', 's_vstat', [],
              lambda e: e.tensor_scalar_mul(T['fcol'][:, 24:32], T['fcol'][:, 8:16],
                                            T['swcol'][:, 1:2]), dep=True)

    def d1_col(ft):
        return T['fcol'][:, 16 + ft:17 + ft]

    if stage < 2:
        dbg_taps(taps_l2())
        return s

    tokreg2 = [None]
    if me('gpsimd'):
        s.wait(*v_tokb)
        reg2 = eng.alloc_register("tokbase2")
        eng.reg_load(reg2, T['tokb2_s'][0:1, 0:1])
        tokreg2[0] = eng.snap(reg2, min_val=0, max_val=(B * L) // 512 - 2)

    def prep_xq(fh):
        vt_last = 0
        v_prev = 0
        for q in range(4):
            ft = fh * 4 + q
            vxl = dma('sync', 'L_xt', T['xtok'][:], T['xtok_d'][ft * 128:(ft + 1) * 128, :],
                      waits=[('s_vq', v_prev), ('s_vstat', v_statq)])
            op('vector', 's_vq', [vxl, ('s_vstat', v_statq)],
               lambda e, ft=ft: e.tensor_scalar(
                   out=T['scr32'][:, 0:2048], in0=T['xtok'][:],
                   scalar1=T['stc'][:, 2 * 8 + ft:2 * 8 + ft + 1],
                   scalar2=T['stc'][:, 6 * 8 + ft:6 * 8 + ft + 1],
                   op0=ALU.subtract, op1=ALU.mult))
            op('vector', 's_vq', [],
               lambda e: e.tensor_scalar_add(
                   T['scr32'][:, 0:2048], T['scr32'][:, 0:2048], MAGIC))
            op('vector', 's_vq', [],
               lambda e: e.tensor_scalar_sub(
                   T['scr32'][:, 0:2048], T['scr32'][:, 0:2048], MAGIC))
            for hh in range(0, 2048, 1024):
                op('vector', 's_vq', [],
                   lambda e, hh=hh: e.tensor_copy(out=T['scri'][:],
                                                  in_=T['scr32'][:, hh:hh + 1024]))
                v_prev = op('vector', 's_vq',
                            [('s_tpe', vt_last)] if hh == 0 else [],
                            lambda e, hh=hh: e.tensor_copy(
                                out=T['xq_tok'][:, hh:hh + 1024], in_=T['scri'][:]))
            for h in range(2):
                def dstf(q=q, h=h):
                    return T['xq_fm'][:].rearrange("p (k c) -> p k c", k=16)[
                        :, h * 8:(h + 1) * 8, q * 128:(q + 1) * 128]
                vt_last = tp_group(T['xq_tok'], list(range(h * 8, (h + 1) * 8)),
                                   dstf, ('s_vq', v_prev))
        return vt_last

    do_ffn = stage >= 3
    fh_ready = [prep_xq(0) if do_ffn else 0, None]

    ffn_jobs = [(fh, cd, pr) for fh in range(2) for cd in range(16) for pr in range(2)]
    wch_lastmm = []
    job_ix = [0]
    kstep = [0]
    v_h_done = [('G_h0', 0), ('G_h1', 0)]
    v_sil_par = [('s_vF', 0), ('s_vF', 0)]
    v_tmpc_free = [0]
    v_pf_free = [('s_vF', 0), ('s_vF', 0)]
    jstate = {'vsl': [0, 0]}

    def ffn_step():
        fh, cd, pr = ffn_jobs[job_ix[0]]
        ks = kstep[0]
        mi, k = divmod(ks, 16)
        fts = [fh * 4 + pr * 2, fh * 4 + pr * 2 + 1]
        rd, cc2 = cd // 2, cd % 2
        moff = mi * DFC * D
        src = T['agwq_out'].rearrange("(r e) -> r e", r=NCORE)[rd] \
            .rearrange("(q c) -> q c", c=1024)[moff // 1024 + k * 128:
                                               moff // 1024 + (k + 1) * 128,
                                               cc2 * 512:(cc2 + 1) * 512]
        wsem = 's_mmA' if mi == 0 else 's_mmC'
        wchg = [wch_lastmm[-2]] if len(wch_lastmm) >= 2 else []
        vload = dma('sync', 'L_wc%d' % (ks % 2), T['wch'][ks % 2][:], src,
                    waits=[('s_coll', v_wqAG)] + wchg)
        vm = 0
        for q, ft in enumerate(fts):
            g0 = []
            if k == 0:
                g0 = [('s_tpe', fh_ready[fh]),
                      v_pf_free[q] if mi == 0 else ('s_scm', jstate['vsl'][q])]
            vm = op('tensor', wsem, [vload] + g0,
                    lambda e, q=q, ks=ks, k=k, ft=ft: e.matmul(
                        T['pF'][:, q * 512:(q + 1) * 512], xqfm(k, ft % 4),
                        T['wch'][ks % 2][:], start=(k == 0), stop=(k == 15)))
        wch_lastmm.append((wsem, vm))
        jp = (job_ix[0] % 2) * 1024
        if k == 15 and mi == 0:
            vsl2 = []
            for q, ft in enumerate(fts):
                vsl2.append(op('scalar', 's_scm',
                               [(wsem, vm), ('s_vstat', v_fc), v_sil_par[job_ix[0] % 2]],
                               lambda e, q=q, ft=ft, jp=jp: e.activation(
                                   out=T['silu1'][:, jp + q * 512:jp + (q + 1) * 512],
                                   in_=T['pF'][:, q * 512:(q + 1) * 512],
                                   func=AF.Silu, scale=d1_col(ft))))
            jstate['vsl'] = vsl2
        if k == 15 and mi == 1:
            for q, ft in enumerate(fts):
                vh = op('vector', 's_vF', [(wsem, vm), ('s_scm', jstate['vsl'][q]),
                                           v_h_done[q]],
                        lambda e, q=q, jp=jp: e.tensor_mul(
                            T['hchunk'][q][:], T['silu1'][:, jp + q * 512:jp + (q + 1) * 512],
                            T['pF'][:, q * 512:(q + 1) * 512]))
                v_pf_free[q] = ('s_vF', vh)
                v_sil_par[job_ix[0] % 2] = ('s_vF', vh)
                vsq2 = op('scalar', 's_scm', [('s_vF', vh), ('s_vh', v_tmpc_free[0])],
                          lambda e, q=q: e.activation(
                              out=T['scrhb'][:, 0:512], in_=T['hchunk'][q][:],
                              func=AF.Square, accum_out=T['tmpc'][:, 2:3]))
                op('vector', 's_vh', [('s_scm', vsq2)],
                   lambda e, ft=ft: e.tensor_add(T['hps'][:, ft:ft + 1],
                                                 T['hps'][:, ft:ft + 1], T['tmpc'][:, 2:3]),
                   dep=True)
                vrm = op('vector', 's_vh', [],
                         lambda e, q=q: e.reduce_max(out=T['tmpc'][:, 3:4],
                                                     in_=T['hchunk'][q][:],
                                                     axis=AX.X, apply_absolute_value=True),
                         dep=True)
                v_tmpc_free[0] = op('vector', 's_vh', [('s_vh', vrm)],
                                    lambda e, ft=ft: e.tensor_max(
                                        T['hpm'][:, ft:ft + 1], T['hpm'][:, ft:ft + 1],
                                        T['tmpc'][:, 3:4]))
                v_h_done[q] = dma('gpsimd', 'G_h%d' % q,
                                  T['h_d'][ft * 128:(ft + 1) * 128, cd * 512:(cd + 1) * 512],
                                  T['hchunk'][q][:], waits=[('s_vh', v_tmpc_free[0])])
        kstep[0] += 1
        if kstep[0] == 32:
            kstep[0] = 0
            job_ix[0] += 1

    def ffn_pump(n, fh_cap):
        if not do_ffn:
            return
        for _ in range(n):
            if job_ix[0] >= len(ffn_jobs) or ffn_jobs[job_ix[0]][0] > fh_cap:
                return
            ffn_step()

    v_gy_hist = [('G_gy0', 0), ('G_gy1', 0)]
    ixg = 0
    v_xn_hist = []
    v_tp_hist = []
    win_mark = None
    for w in range(nwin):
        bat, wb = w // WPB, w % WPB
        tts = [w * 4 + q for q in range(4)]
        if w == 8 and do_ffn:
            fh_ready[1] = prep_xq(1)
        if wstage < 1:
            continue

        v_tp_last = 0
        for q, tt in enumerate(tts):
            gq = []
            if len(v_xn_hist) >= 1:
                gq.append(('s_xn', v_xn_hist[-1]))
            v = dma('sync', 'L_x', T['xg'][0][:], T['x_all_d'][tt * 128:(tt + 1) * 128, :],
                    waits=gq)
            if wstage == 11:
                continue
            gtp = [('s_tpe', v_tp_hist[-1])] if len(v_tp_hist) >= 1 else []
            vx = op('scalar', 's_xn', [v, v_rb] + gtp,
                    lambda e, tt=tt: e.activation(
                        out=T['xn_tok'][0][:], in_=T['xg'][0][:], func=AF.Identity,
                        scale=r_col(tt), bias=b_col(tt)))
            v_xn_hist.append(vx)
            if wstage == 12:
                ixg += 1
                continue
            for h in range(2):
                def dstf(q=q, h=h):
                    return T['xn_fm'][:].rearrange("p (k c) -> p k c", k=16)[
                        :, h * 8:(h + 1) * 8, q * 128:(q + 1) * 128]
                v_tp_last = tp_group(T['xn_tok'][0], list(range(h * 8, (h + 1) * 8)),
                                     dstf, ('s_xn', vx))
            v_tp_hist.append(v_tp_last)
            ixg += 1

        pg = []
        if win_mark is not None:
            pg = [('s_vgy', win_mark['s_vgy']), ('s_mmB', win_mark['s_mmB']),
                  ('s_mmC', win_mark['s_mmC']), ('s_dA', win_mark['s_dA']),
                  ('s_vbt', win_mark['s_vbt'])]

        if wstage < 2:
            continue
        vevA_o = {}
        for o in range(10):
            ps, gate = slotA()
            vm = 0
            for k in range(16):
                g0 = [('s_tpe', v_tp_last), gate] if k == 0 else []
                vm = op('tensor', 's_mmA' if k == 15 else 's_mmS', g0,
                        lambda e, ps=ps, o=o, k=k: e.matmul(
                            ps, winT(k, o), xnfm(k), start=(k == 0), stop=(k == 15)))
            if o < 4:
                vevA_o[o] = op('scalar', 's_evA', [('s_mmA', vm)] + pg,
                               lambda e, ps=ps, o=o: e.activation(
                                   out=T['z_sil'][:, o * 512:(o + 1) * 512], in_=ps,
                                   func=AF.Silu))
            elif o == 4:
                vevA_o[o] = op('scalar', 's_evA', [('s_mmA', vm)] + pg,
                               lambda e, ps=ps: e.copy(out=T['B_fm'][:], in_=ps))
            elif o < 9:
                vevA_o[o] = op('scalar', 's_evA', [('s_mmA', vm)] + pg,
                               lambda e, ps=ps, o=o: e.copy(
                                   out=T['C_fm'][:, (o - 5) * 512:(o - 4) * 512], in_=ps))
            else:
                vevA_o[o] = op('scalar', 's_evA', [('s_mmA', vm)] + pg,
                               lambda e, ps=ps: e.copy(out=T['dtl_fm'][:], in_=ps))

        if wstage < 3:
            continue
        vdelta = {}
        for j in range(NJ):
            ps, gate = slotA()
            vm = op('tensor', 's_mmA', [('s_evA', vevA_o[9]), gate],
                    lambda e, ps=ps, j=j: e.matmul(
                        ps, T['wdtT'][:, j * 128:(j + 1) * 128], T['dtl_fm'][:],
                        start=True, stop=True))
            ve1 = op('scalar', 's_scm', [('s_mmA', vm)],
                     lambda e, ps=ps, j=j: e.activation(
                         out=T['e_tmp'][:], in_=ps, func=AF.Exp, bias=T['bdt2c'][:, j:j + 1]))
            vdelta[j] = op('scalar', 's_evA', [('s_scm', ve1)] + pg,
                           lambda e, j=j: e.activation(
                               out=T['delta'][:, j * 512:(j + 1) * 512], in_=T['e_tmp'][:],
                               func=AF.Ln, bias=1.0))

        if wstage < 4:
            continue
        v_du = 0
        for j in range(NJ):
            ps, gate = slotA()
            vm = op('tensor', 's_mmA', [('s_evA', vevA_o[4]), v_gconsts, gate],
                    lambda e, ps=ps, j=j: e.matmul(
                        ps, T['indU'][:].rearrange("p (a c) -> p a c", a=4)[:, j, :],
                        T['B_fm'][:], start=True, stop=True))
            base = j * 516
            vt1 = op('vector', 's_vmisc', [],
                     lambda e, j=j, base=base: e.tensor_copy(
                         out=T['utail'][:, j * 4:j * 4 + 3],
                         in_=T['u_in'][:, base + 512:base + 515]))
            vev2 = op('scalar', 's_evA', [('s_mmA', vm), ('s_vmisc', vt1)] + pg,
                      lambda e, ps=ps, base=base: e.copy(
                          out=T['u_in'][:, base + 3:base + 515], in_=ps))
            if wb == 0:
                op('vector', 's_vmisc', [('s_evA', vev2)],
                   lambda e, base=base: e.memset(T['u_in'][:, base:base + 3], 0.0))
            else:
                op('vector', 's_vmisc', [('s_evA', vev2)],
                   lambda e, j=j, base=base: e.tensor_copy(
                       out=T['u_in'][:, base:base + 3], in_=T['utail'][:, j * 4:j * 4 + 3]))
            op('vector', 's_vmisc', [],
               lambda e, j=j, base=base: e.tensor_scalar_mul(
                   T['cacc'][:], T['u_in'][:, base:base + 512],
                   T['convwc'][:, j * 4:j * 4 + 1]), dep=True)
            for tp in range(1, 4):
                op('vector', 's_vmisc', [],
                   lambda e, j=j, base=base, tp=tp: e.scalar_tensor_tensor(
                       out=T['cacc'][:], in0=T['u_in'][:, base + tp:base + tp + 512],
                       scalar=T['convwc'][:, j * 4 + tp:j * 4 + tp + 1], in1=T['cacc'][:],
                       op0=ALU.mult, op1=ALU.add))
            vcv = s.cnt['s_vmisc']
            vu = op('scalar', 's_scm', [('s_vmisc', vcv)] + pg,
                    lambda e, j=j: e.activation(
                        out=T['u_exp'][:, j * 512:(j + 1) * 512], in_=T['cacc'][:],
                        func=AF.Silu, bias=T['convbc'][:, j:j + 1]))
            v_du = op('vector', 's_vmisc', [('s_evA', vdelta[j]), ('s_scm', vu)],
                      lambda e, j=j: e.tensor_mul(
                          T['du'][:, j * 512:(j + 1) * 512],
                          T['delta'][:, j * 512:(j + 1) * 512],
                          T['u_exp'][:, j * 512:(j + 1) * 512]))

        if wstage < 5:
            continue

        def emit_B(jn):
            psb, gateB = slotB()
            vmb = op('tensor', 's_mmB', [('s_evA', vevA_o[4]), v_gconsts, gateB],
                     lambda e, psb=psb, jn=jn: e.matmul(
                         psb, T['indB'][:, jn:jn + 1].broadcast_to([128, 128]),
                         T['B_fm'][:], start=True, stop=True))
            ib = s.cnt['s_evB']
            vbe = op('scalar', 's_evB',
                     [('s_mmB', vmb), ('s_vbt', max(0, s.cnt['s_vbt'] - 1))],
                     lambda e, psb=psb, ib=ib: e.copy(out=T['Bb'][ib % 2][:], in_=psb))
            return (vbe, ib)

        def emit_DA(jn):
            jj = jn // DST
            ida = s.cnt['s_dA']
            vda = op('scalar', 's_dA',
                     [('s_evA', vdelta[jj]), ('s_vscan', max(0, s.cnt['s_vscan'] - 1))],
                     lambda e, jj=jj, jn=jn, ida=ida: e.activation(
                         out=T['dAt'][ida % 2][:], in_=T['delta'][:, jj * 512:(jj + 1) * 512],
                         func=AF.Exp,
                         scale=T['A_t'][:, jn:jn + 1]))
            return (vda, ida)

        preDA = emit_DA(0)
        preB = emit_B(0)
        for j in range(NJ):
            for n in range(DST):
                jn = j * DST + n
                vda, ida = preDA
                vbe, ib = preB
                psc, gateC = slotC()
                vmc = op('tensor', 's_mmC', [('s_evA', vevA_o[5 + j]), v_gconsts, gateC],
                        lambda e, psc=psc, n=n, j=j: e.matmul(
                            psc, T['indC'][:].rearrange("p (a c) -> p a c", a=16)[:, n, :],
                            T['C_fm'][:, j * 512:(j + 1) * 512], start=True, stop=True))
                ibt = s.cnt['s_vbt']
                vb2 = op('vector', 's_vbt', [('s_evB', vbe), ('s_vmisc', v_du)],
                         lambda e, j=j, ib=ib, ibt=ibt: e.tensor_mul(
                             T['bt'][ibt % 2][:], T['du'][:, j * 512:(j + 1) * 512],
                             T['Bb'][ib % 2][:]))
                init = 0.0 if wb == 0 else T['carry'][:, jn:jn + 1]
                isc = s.cnt['s_vscan']
                vscn = op('vector', 's_vscan', [('s_dA', vda)],
                          lambda e, ida=ida, ibt=ibt, isc=isc, init=init: e.tensor_tensor_scan(
                              out=T['st_s'][isc % 2][:], data0=T['dAt'][ida % 2][:],
                              data1=T['bt'][ibt % 2][:], initial=init,
                              op0=ALU.mult, op1=ALU.add))
                op('vector', 's_vmisc', [('s_vscan', vscn)],
                   lambda e, jn=jn, isc=isc: e.tensor_copy(
                       out=T['carry'][:, jn:jn + 1], in_=T['st_s'][isc % 2][:, WN - 1:WN]))
                op('vector', 's_vC', [('s_mmC', vmc), ('s_vscan', vscn)],
                   lambda e, psc=psc, n=n, isc=isc: e.tensor_mul(
                       T['prods'][:, (n % 8) * 512:(n % 8 + 1) * 512], psc,
                       T['st_s'][isc % 2][:]))
                if jn < NJ * DST - 1:
                    preDA = emit_DA(jn + 1)
                    preB = emit_B(jn + 1)
                ffn_pump(0 if w < 2 else (3 if w < 8 else 2), 0 if w < 8 else 1)
                if n % 8 == 7:
                    for stp, cnt2 in ((1, 4), (2, 2), (4, 1)):
                        for q2 in range(cnt2):
                            a0 = (q2 * 2 * stp) * 512
                            a1 = (q2 * 2 * stp + stp) * 512
                            op('vector', 's_vmisc', [],
                               lambda e, a0=a0, a1=a1: e.tensor_add(
                                   T['prods'][:, a0:a0 + 512], T['prods'][:, a0:a0 + 512],
                                   T['prods'][:, a1:a1 + 512]))
                    if n == 7:
                        op('vector', 's_vmisc', [],
                           lambda e: e.tensor_copy(out=T['yhalf'][:],
                                                   in_=T['prods'][:, 0:512]))
                    else:
                        op('vector', 's_vmisc', [],
                           lambda e: e.tensor_add(T['ytmp'][:], T['prods'][:, 0:512],
                                                  T['yhalf'][:]))
            op('vector', 's_vmisc', [],
               lambda e, j=j: e.scalar_tensor_tensor(
                   out=T['ytmp'][:], in0=T['u_exp'][:, j * 512:(j + 1) * 512],
                   scalar=T['dparc'][:, j:j + 1], in1=T['ytmp'][:],
                   op0=ALU.mult, op1=ALU.add))
            igy = s.cnt['s_vgy']
            vgy = op('vector', 's_vgy',
                     [('s_evA', vevA_o[j]), v_gy_hist[igy % 2]],
                     lambda e, j=j, igy=igy: e.tensor_mul(
                         T['gy_t'][igy % 2][:], T['ytmp'][:],
                         T['z_sil'][:, j * 512:(j + 1) * 512]))
            v_gy_hist[igy % 2] = dma('gpsimd', 'G_gy%d' % (igy % 2),
                            T['gy_in_d'][j * 128:(j + 1) * 128,
                                         bat * L + wb * WN:bat * L + wb * WN + WN],
                            T['gy_t'][igy % 2][:], waits=[('s_vgy', vgy)])

        win_mark = {k: s.cnt[k] for k in ('s_vgy', 's_mmB', 's_mmC', 's_dA', 's_vbt')}

    if do_ffn and fh_ready[1] is None:
        fh_ready[1] = prep_xq(1)
    while do_ffn and job_ix[0] < 64:
        ffn_step()
    if stage < 4:
        dbg_taps(taps_l2() + [('d_gy', T['gy_in_d'][:]), ('d_h', T['h_d'][:])])
        return s

    v_gyAG = coll("AllGather", ALU.bypass, [T['gy_in_d'][:]], [T['gy_all_d'][:]],
                  [v_gy_hist[0], v_gy_hist[1]])

    v_hs = s.cnt['s_vh']
    op('vector', 's_vh', [('s_vh', v_hs), v_h_done[0], v_h_done[1]],
       lambda e: e.reciprocal(out=T['fcol'][:, 32:40], in_=T['hpm'][:]), dep=True)
    op('vector', 's_vh', [],
       lambda e: e.tensor_scalar_mul(T['fcol'][:, 32:40], T['fcol'][:, 32:40], 127.0),
       dep=True)
    op('vector', 's_vh', [],
       lambda e: e.tensor_mul(T['fcol'][:, 40:48], T['fcol'][:, 24:32], T['fcol'][:, 24:32]),
       dep=True)
    op('vector', 's_vh', [],
       lambda e: e.tensor_mul(T['fcol'][:, 40:48], T['fcol'][:, 40:48], T['hps'][:]),
       dep=True)
    op('vector', 's_vh', [],
       lambda e: e.tensor_scalar_mul(T['fcol'][:, 40:48], T['fcol'][:, 40:48],
                                     1.0 / DFF), dep=True)
    v_pre = op('vector', 's_vh', [],
               lambda e: e.tensor_scalar_add(T['fcol'][:, 40:48], T['fcol'][:, 40:48],
                                             1e-6), dep=True)
    vrs2a = op('scalar', 's_scm', [('s_vh', v_pre)],
               lambda e: e.activation(out=T['fcol'][:, 40:48], in_=T['fcol'][:, 40:48],
                                      func=AF.Sqrt))
    op('vector', 's_vh', [('s_scm', vrs2a)],
       lambda e: e.reciprocal(out=T['fcol'][:, 40:48], in_=T['fcol'][:, 40:48]))
    vrs2 = op('scalar', 's_scm', [('s_vh', s.cnt['s_vh'])],
              lambda e: e.nop())
    op('vector', 's_vh', [('s_scm', vrs2)],
       lambda e: e.tensor_mul(T['fcol'][:, 48:56], T['fcol'][:, 40:48], T['fcol'][:, 24:32]),
       dep=True)
    op('vector', 's_vh', [],
       lambda e: e.tensor_mul(T['fcol'][:, 48:56], T['fcol'][:, 48:56], T['hpm'][:]),
       dep=True)
    op('vector', 's_vh', [],
       lambda e: e.tensor_scalar_mul(T['fcol'][:, 48:56], T['fcol'][:, 48:56], 1.0 / 127.0),
       dep=True)
    v_dq3 = op('vector', 's_vh', [],
               lambda e: e.tensor_scalar_mul(T['fcol'][:, 48:56], T['fcol'][:, 48:56],
                                             T['swcol'][:, 2:3]), dep=True)

    def dq3_col(ft):
        return T['fcol'][:, 48 + ft:49 + ft]

    v_xqh_done = ('G_xq', 0)
    v_hq = 0
    for ft in range(NTC):
        for hc in range(4):
            v = dma('sync', 'L_h', T['scr32'][:, 0:2048],
                    T['h_d'][ft * 128:(ft + 1) * 128, hc * 2048:(hc + 1) * 2048],
                    waits=[('s_vh', v_hq), v_h_done[0], v_h_done[1]])
            op('vector', 's_vh', [v, ('s_vh', v_dq3)],
               lambda e, ft=ft: e.tensor_scalar(
                   out=T['scr32'][:, 0:2048], in0=T['scr32'][:, 0:2048],
                   scalar1=T['fcol'][:, 32 + ft:33 + ft], scalar2=MAGIC,
                   op0=ALU.mult, op1=ALU.add))
            op('vector', 's_vh', [],
               lambda e: e.tensor_scalar_sub(
                   T['scr32'][:, 0:2048], T['scr32'][:, 0:2048], MAGIC))
            for hh in range(0, 2048, 1024):
                op('vector', 's_vh', [],
                   lambda e, hh=hh: e.tensor_copy(out=T['scri'][:],
                                                  in_=T['scr32'][:, hh:hh + 1024]))
                v_hq = op('vector', 's_vh', [v_xqh_done] if hh == 0 else [],
                          lambda e, hh=hh: e.tensor_copy(out=T['xq_tok'][:, hh:hh + 1024],
                                                         in_=T['scri'][:]))
            vev3 = 0
            for h in range(2):
                def dstf(h=h):
                    return T['xqh_w'][:, h * 1024:(h + 1) * 1024]
                vev3 = tp_group(T['xq_tok'], list(range(h * 8, (h + 1) * 8)),
                                dstf, ('s_vh', v_hq))
            v_xqh_done = dma('gpsimd', 'G_xq',
                             T['xqh_d'].rearrange("(f p k c) -> f p k c", f=NTC, k=64, p=128)
                             [ft][:, hc * 16:(hc + 1) * 16, :],
                             T['xqh_w'][:].rearrange("p (k c) -> p k c", k=16),
                             waits=[('s_tpe', vev3)])

    if stage < 5:
        dbg_taps(taps_l2() + [('d_gy', T['gy_in_d'][:]), ('d_h', T['h_d'][:])])
        return s

    v_gw = ('G_g', 0)
    l3A = []
    l3B = []
    v_zfree = s.cnt['s_vgy']
    for fp in range(4):
        fts = [fp * 2, fp * 2 + 1]
        lb3w = [v_xqh_done]
        if fp == 0:
            lb3w += [('s_mmA', s.cnt['s_mmA']), ('s_mmS', s.cnt['s_mmS']),
                     ('s_mmB', s.cnt['s_mmB']), ('s_vgy', v_zfree)]
        else:
            lb3w += [('s_mmA', l3A[-1]), ('s_mmB', l3B[-1])]
        v_l3b = 0
        for q in range(2):
            v_l3b = dma('scalar', 'G_l3%d' % q,
                        T['winT'][:].rearrange("p (k c) -> p k c", c=128)
                        [:, q * 64:(q + 1) * 64, :],
                        T['xqh_d'].rearrange("(f p k c) -> f p k c", f=NTC, k=64, p=128)
                        [fts[q]], waits=lb3w)
        v_l3b = [('G_l30', s.cnt['G_l30']), ('G_l31', s.cnt['G_l31'])]
        for mh in range(2):
            s.iA += 2
            gA = ('s_evA', s.iA - 2)
            s.iB += 2
            gB = ('s_evB', s.iB - 2)
            vmmA = vmmB = 0
            for k in range(64):
                rdk, lk = k // 8, k % 8
                src = T['agwq_out'].rearrange("(r e) -> r e", r=NCORE)[rdk] \
                    .rearrange("(q c) -> q c", c=2048)[(2 * DFC * D) // 2048 + lk * 128:
                                                       (2 * DFC * D) // 2048 + (lk + 1) * 128,
                                                       mh * 1024:(mh + 1) * 1024]
                g3 = ([('s_mmA', l3A[-2]), ('s_mmB', l3B[-2])]
                      if len(l3A) >= 2 else [('s_vgy', v_zfree)])
                vload = dma('sync', 'L_w3%d' % (k % 2), T['w3ch'][k % 2], src, waits=g3)
                for hx in range(2):
                    vmmA = op('tensor', 's_mmA',
                              [vload, gA] + v_l3b if (k == 0 and hx == 0)
                              else [vload],
                              lambda e, k=k, hx=hx: e.matmul(
                                  T['pA'][:, hx * 512:(hx + 1) * 512],
                                  T['winT'][:, k * 128:(k + 1) * 128],
                                  T['w3ch'][k % 2][:, hx * 512:(hx + 1) * 512],
                                  start=(k == 0), stop=(k == 63)))
                    vmmB = op('tensor', 's_mmB',
                              [gB, v_pf_free[0], v_pf_free[1]]
                              if (k == 0 and hx == 0) else [],
                              lambda e, k=k, hx=hx: e.matmul(
                                  T['pF'][:, hx * 512:(hx + 1) * 512],
                                  T['winT'][:, 8192 + k * 128:8192 + (k + 1) * 128],
                                  T['w3ch'][k % 2][:, hx * 512:(hx + 1) * 512],
                                  start=(k == 0), stop=(k == 63)))
                l3A.append(vmmA)
                l3B.append(vmmB)
            for q, (psname, msem, vmm, esem) in enumerate(
                    [('pA', 's_mmA', vmmA, 's_evA'), ('pF', 's_mmB', vmmB, 's_evB')]):
                for half in range(2):
                    op('scalar', esem, [(msem, vmm), ('s_vh', v_dq3),
                                        v_gw],
                       lambda e, psname=psname, half=half, ft=fts[q], mh=mh: e.activation(
                           out=T['g_t'][:, (ft % 2) * 2048 + mh * 1024 + half * 512:
                                         (ft % 2) * 2048 + mh * 1024 + (half + 1) * 512],
                           in_=T[psname][:, half * 512:(half + 1) * 512],
                           func=AF.Silu, scale=dq3_col(fts[q])))
        for q, ft in enumerate(fts):
            v_gw = dma('gpsimd', 'G_g', T['g_d'][ft * 128:(ft + 1) * 128, :],
                       T['g_t'][:, (ft % 2) * 2048:(ft % 2) * 2048 + 2048],
                       waits=[('s_evA', s.cnt['s_evA']), ('s_evB', s.cnt['s_evB'])])

    if stage < 6:
        dbg_taps(taps_l2() + [('d_gy', T['gy_in_d'][:]), ('d_h', T['h_d'][:]),
                              ('d_g', T['g_d'][:])])
        return s

    banks = [T['pA'][:, 0:512], T['pA'][:, 512:1024],
             T['pF'][:, 0:512], T['pF'][:, 512:1024]]
    v_otmp = [('s_vC', 0)] * 4
    v_tg_mm = 0
    nxg = [0]
    for tg in range(2):
        glv = 0
        for ka in range(4):
            glv = s.bump('G_gyl', 16)
            s.checks.append(('G_gyl', glv))
            if me('gpsimd'):
                if ka == 0:
                    s.wait('s_coll', v_gyAG)
                    if tg == 0:
                        s.wait('s_mmA', s.cnt['s_mmA'])
                        s.wait('s_mmS', s.cnt['s_mmS'])
                        s.wait('s_mmC', s.cnt['s_mmC'])
                    else:
                        s.wait('s_mmC', v_tg_mm)
                tbv = tokreg2[0] + tg
                eng.dma_start(
                    out=T['winT'][:].rearrange("p (k c) -> p k c", k=40)
                    [:, ka * 8:(ka + 1) * 8, :],
                    in_=T['gy_all_d'].rearrange("(k p) (tb c) -> p k tb c",
                                                p=128, c=512)
                    [:, ka * 8:(ka + 1) * 8, tbv, :]).then_inc(S['G_gyl'], 16)
        v_gyl = ('G_gyl', glv)
        for mc in range(4):
            vload = 0
            vm = 0
            for k in range(32):
                g4 = [wch_lastmm[-4]] if len(wch_lastmm) >= 4 else []
                if tg == 0 and mc == 0 and k == 0:
                    g4 += [('s_mmA', s.cnt['s_mmA']), ('s_mmB', s.cnt['s_mmB']),
                           ('s_mmC', s.cnt['s_mmC'])]
                vload = dma('sync' if k % 2 == 0 else 'scalar', 'L_wc%d' % (k % 4),
                            T['wch4'][k % 4][:],
                            T['woutTb_d'][k * 128:(k + 1) * 128, mc * 512:(mc + 1) * 512],
                            waits=g4 + [v_gd_w])
                for q in range(4):
                    gq = []
                    if k == 0:
                        gq = [v_gyl, v_otmp[q]]
                        if tg == 0 and mc == 0:
                            gq += [('s_evA', s.cnt['s_evA']), ('s_evB', s.cnt['s_evB'])]
                    vm = op('tensor', 's_mmC', [vload] + gq,
                            lambda e, q=q, k=k: e.matmul(
                                banks[q],
                                T['winT'][:, k * 512 + q * 128:k * 512 + (q + 1) * 128],
                                T['wch4'][k % 4][:], start=(k == 0), stop=(k == 31)))
                wch_lastmm.append(('s_mmC', vm))
            v_tg_mm = vm
            for q in range(4):
                tt = tg * 4 + q
                ix = nxg[0] % 2
                nxg[0] += 1
                vxr = dma('sync', 'L_xo%d' % ix, T['xch'][ix][:],
                          T['xtok_d'][tt * 128:(tt + 1) * 128, mc * 512:(mc + 1) * 512],
                          waits=[('s_vout', max(0, s.cnt['s_vout'] - 1))])
                vgr = dma('gpsimd', 'G_gr%d' % ix, T['gch'][ix][:],
                          T['g_d'][tt * 128:(tt + 1) * 128, mc * 512:(mc + 1) * 512],
                          waits=[v_gw, ('s_vC', max(0, s.cnt['s_vC'] - 1))])
                v_otmp[q] = ('s_vC', op('vector', 's_vC', [('s_mmC', vm), vgr],
                                        lambda e, q=q, ix=ix: e.tensor_mul(
                                            T['otmp'][ix], banks[q], T['gch'][ix][:])))
                so = 's_out%d' % ix
                vo = op('vector', 's_vout', [vxr, (so, s.cnt[so])],
                        lambda e, ix=ix: e.tensor_add(
                            T['out_t'][ix], T['otmp'][ix], T['xch'][ix][:]))
                v_out = s.bump(so, 16)
                s.checks.append((so, v_out))
                if me('gpsimd'):
                    s.wait('s_vout', vo)
                    eng.dma_start(out=T['out_d'][tt * 128:(tt + 1) * 128,
                                                 mc * 512:(mc + 1) * 512],
                                  in_=T['out_t'][ix]).then_inc(S[so], 16)

    if T.get('dbg'):
        dma('gpsimd', 'G_dbg', T['d_gy'][:], T['gy_in_d'][:],
            waits=[('s_vout', s.cnt['s_vout'])])
        dma('gpsimd', 'G_dbg', T['d_agst'][:], T['agst_in'][:])
        dma('gpsimd', 'G_dbg', T['d_h'][:], T['h_d'][:])
        dma('gpsimd', 'G_dbg', T['d_g'][:], T['g_d'][:])
        dma('gpsimd', 'G_dbg', T['d_wq'][:], T['agwq_in'][:])
        dma('gpsimd', 'G_dbg', T['d_stc'][:], T['stc'][:])
        v_dbg = dma('gpsimd', 'G_dbg', T['d_fcol'][:], T['fcol'][:])
        if me('gpsimd'):
            s.wait(*v_dbg)
    s.wait('s_out0', s.cnt['s_out0'])
    s.wait('s_out1', s.cnt['s_out1'])
    return s


_cache = {}


def _build(sim=False, stage=99, nwin=NWIN, wstage=99, dbg=False):
    nc = bass.Bass("TRN2", target_bir_lowering=False)
    dt = nc.dram_tensor
    T = {}
    T['x_all_d'] = dt("x_all", [B * L, D], F32, kind="ExternalInput").ap()
    T['xtok_d'] = dt("x_tok", [TOK, D], F32, kind="ExternalInput").ap()
    T['winT_d'] = dt("winT", [D, 1280], F32, kind="ExternalInput").ap()
    T['wdtT_d'] = dt("wdtT", [128, 512], F32, kind="ExternalInput").ap()
    T['woutT_d'] = dt("woutT", [DIN, D], F32, kind="ExternalInput").ap()
    T['w1T_d'] = dt("w1T", [D, DFC], F32, kind="ExternalInput").ap()
    T['w2T_d'] = dt("w2T", [D, DFC], F32, kind="ExternalInput").ap()
    T['w3T_d'] = dt("w3T", [DFC, D], F32, kind="ExternalInput").ap()
    T['bdt_d'] = dt("bdt", [IC], F32, kind="ExternalInput").ap()
    T['convw_d'] = dt("convw", [IC, 4], F32, kind="ExternalInput").ap()
    T['convb_d'] = dt("convb", [IC], F32, kind="ExternalInput").ap()
    T['alog_d'] = dt("alog", [IC, DST], F32, kind="ExternalInput").ap()
    T['dpar_d'] = dt("dpar", [IC], F32, kind="ExternalInput").ap()
    T['ident_d'] = dt("ident", [128, 128], F32, kind="ExternalInput").ap()
    T['indB_d'] = dt("indB", [128, 64], F32, kind="ExternalInput").ap()
    T['indC_d'] = dt("indC", [16, 128, 128], F32, kind="ExternalInput").ap()
    T['indU_d'] = dt("indU", [4, 128, 128], F32, kind="ExternalInput").ap()
    T['tokb_d'] = dt("tokbase", [1, 1], U32, kind="ExternalInput").ap()
    T['tokb2_d'] = dt("tokbase2", [1, 1], U32, kind="ExternalInput").ap()
    T['out_d'] = dt("out", [TOK, D], F32, kind="ExternalOutput").ap()
    T['woutTb_d'] = dt("woutTb", [DIN, D], BF16, kind="Internal").ap()
    T['gy_in_d'] = dt("gy_in", [IC, B * L], BF16, kind="Internal").ap()
    T['gy_all_d'] = dt("gy_all", [DIN, B * L], BF16, kind="Internal",
                       addr_space="Shared").ap()
    T['agst_in'] = dt("agst_in", [2, TOK], F32, kind="Internal").ap()
    T['agst_out'] = dt("agst_out", [NCORE, 2 * TOK], F32, kind="Internal").ap()
    T['agwr_in'] = dt("agwr_in", [1, 8], F32, kind="Internal").ap()
    T['agwr_out'] = dt("agwr_out", [1, 8], F32, kind="Internal").ap()
    T['agwq_in'] = dt("agwq_in", [3 * DFC * D], BF16, kind="Internal").ap()
    T['agwq_out'] = dt("agwq_out", [NCORE * 3 * DFC * D], BF16, kind="Internal",
                       addr_space="Shared").ap()
    T['h_d'] = dt("h_scr", [TOK, DFF], F32, kind="Internal").ap()
    T['xqh_d'] = dt("xqh_scr", [NTC * 64 * 128 * 128], BF16, kind="Internal").ap()
    T['g_d'] = dt("g_scr", [TOK, D], BF16, kind="Internal").ap()
    if dbg:
        T['d_gy'] = dt("d_gy", [IC, B * L], BF16, kind="ExternalOutput").ap()
        T['d_agst'] = dt("d_agst", [2, TOK], F32, kind="ExternalOutput").ap()
        T['d_h'] = dt("d_h", [TOK, DFF], F32, kind="ExternalOutput").ap()
        T['d_g'] = dt("d_g", [TOK, D], BF16, kind="ExternalOutput").ap()
        T['d_wq'] = dt("d_wq", [3 * DFC * D], BF16, kind="ExternalOutput").ap()
        T['d_stc'] = dt("d_stc", [128, 64], F32, kind="ExternalOutput").ap()
        T['d_fcol'] = dt("d_fcol", [128, 64], F32, kind="ExternalOutput").ap()
    T['dbg'] = dbg

    from contextlib import ExitStack
    ctx = ExitStack()
    with ctx:
        def sb(name, shape, dtp):
            return ctx.enter_context(nc.sbuf_tensor(name, shape, dtp))

        T['winT'] = sb("winT_s", [128, 16 * 10 * 128], BF16)
        T['wdtT'] = sb("wdtT_s", [128, 512], BF16)
        T['identb'] = sb("identb_s", [128, 128], BF16)
        T['indB'] = sb("indB_s", [128, 64], BF16)
        T['indC'] = sb("indC_s", [128, 16 * 128], BF16)
        T['indU'] = sb("indU_s", [128, 4 * 128], BF16)
        T['bdtc'] = sb("bdtc", [128, 4], F32)
        T['bdt2c'] = sb("bdt2c", [128, 4], F32)
        T['convbc'] = sb("convbc", [128, 4], F32)
        T['dparc'] = sb("dparc", [128, 4], F32)
        T['convwc'] = sb("convwc", [128, 16], F32)
        T['alogs'] = sb("alogs", [128, 64], F32)
        T['A_t'] = sb("A_t", [128, 64], F32)
        T['tokb_s'] = sb("tokb_s", [1, 1], U32)
        T['tokb2_s'] = sb("tokb2_s", [1, 1], U32)
        T['scr32'] = sb("scr32", [128, 2048], F32)
        T['scri'] = sb("scri", [128, 1024], I32)
        T['scrh'] = sb("scrh", [128, 2048], F32)
        T['scrh2'] = sb("scrh2", [128, 2048], F32)
        T['scrhb'] = sb("scrhb", [128, 2048], BF16)
        T['xtok'] = sb("xtok", [128, 2048], F32)
        T['stc'] = sb("stc", [128, 64], F32)
        T['tmpc'] = sb("tmpc", [128, 8], F32)
        T['coltiles'] = sb("coltiles", [128, 128], F32)
        T['fcol'] = sb("fcol", [128, 64], F32)
        T['wabs'] = sb("wabs", [128, 4], F32)
        T['onescol'] = sb("onescol", [128, 1], BF16)
        T['onescol32'] = sb("onescol32", [128, 1], F32)
        T['wsum_row'] = sb("wsum_row", [1, 8], F32)
        T['wsum_row2'] = sb("wsum_row2", [1, 4], F32)
        T['wsum_row2b'] = sb("wsum_row2b", [1, 4], BF16)
        T['swcol'] = sb("swcol", [128, 8], F32)
        T['xg'] = [sb("xg0", [128, 2048], F32)]
        T['xn_tok'] = [sb("xnt0", [128, 2048], BF16)]
        T['xn_fm'] = sb("xn_fm", [128, 16 * 512], BF16)
        T['z_sil'] = sb("z_sil", [128, 4 * 512], BF16)
        T['B_fm'] = sb("B_fm", [128, 512], BF16)
        T['C_fm'] = sb("C_fm", [128, 4 * 512], BF16)
        T['dtl_fm'] = sb("dtl_fm", [128, 512], BF16)
        T['delta'] = sb("delta", [128, 4 * 512], BF16)
        T['e_tmp'] = sb("e_tmp", [128, 512], F32)
        T['u_in'] = sb("u_in", [128, 4 * 516], F32)
        T['utail'] = sb("utail", [128, 16], F32)
        T['cacc'] = sb("cacc", [128, 512], F32)
        T['u_exp'] = sb("u_exp", [128, 4 * 512], BF16)
        T['du'] = sb("du", [128, 4 * 512], BF16)
        T['dAt'] = [sb("dAt0", [128, 512], F32), sb("dAt1", [128, 512], F32)]
        T['Bb'] = [sb("Bb0", [128, 512], BF16), sb("Bb1", [128, 512], BF16)]
        T['bt'] = [sb("bt0", [128, 512], BF16), sb("bt1", [128, 512], BF16)]
        T['st_s'] = [sb("st0", [128, 512], F32), sb("st1", [128, 512], F32)]
        T['prods'] = sb("prods", [128, 8 * 512], BF16)
        T['yhalf'] = sb("yhalf", [128, 512], F32)
        T['ytmp'] = sb("ytmp", [128, 512], F32)
        T['gy_t'] = [sb("gyt0", [128, 512], BF16), sb("gyt1", [128, 512], BF16)]
        T['carry'] = sb("carry", [128, 64], F32)
        T['xq_tok'] = sb("xq_tok", [128, 2048], BF16)
        T['xq_fm'] = sb("xq_fm", [128, 16 * 512], BF16)
        T['wch'] = [sb("wch0", [128, 512], BF16), sb("wch1", [128, 512], BF16)]
        T['silu1'] = sb("silu1", [128, 2048], BF16)
        T['hchunk'] = [sb("hch0", [128, 512], F32), sb("hch1", [128, 512], F32)]
        T['hps'] = sb("hps", [128, 8], F32)
        T['hpm'] = sb("hpm", [128, 8], F32)
        T['l3l0'] = [sb("l3l0a", [128, 128], BF16), sb("l3l0b", [128, 128], BF16)]
        T['l3l1'] = [sb("l3l1a", [128, 128], BF16), sb("l3l1b", [128, 128], BF16)]


        T['xqh_w'] = T['scrhb']
        T['g_t'] = T['xn_fm'][:, 0:4096]
        T['otmp'] = [T['scr32'][:, 0:512], T['scr32'][:, 1536:2048]]
        T['out_t'] = [T['scr32'][:, 512:1024], T['scr32'][:, 1024:1536]]
        T['w3ch'] = [T['z_sil'][:, 0:1024], T['z_sil'][:, 1024:2048]]
        T['gyb'] = T['winT'][:, 0:32 * 512]
        T['wch4'] = T['wch'] + [T['xq_fm'][:, 0:512], T['xq_fm'][:, 512:1024]]
        T['gch'] = [T['xq_fm'][:, 1024:1536], T['xq_fm'][:, 1536:2048]]
        T['xch'] = [T['scrh'][:, 0:512], T['scrh'][:, 512:1024]]
        T['pA'] = ctx.enter_context(nc.psum_tensor("pA", [128, 1024], F32))
        T['pB'] = ctx.enter_context(nc.psum_tensor("pB", [128, 512], F32))
        T['pC'] = ctx.enter_context(nc.psum_tensor("pC", [128, 1024], F32))
        T['pF'] = ctx.enter_context(nc.psum_tensor("pF", [128, 1024], F32))
        T['pT'] = ctx.enter_context(nc.psum_tensor("pT", [128, 1024], BF16))

        sems = {nm: ctx.enter_context(nc.semaphore(nm)) for nm in SEM_NAMES}
        block = ctx.enter_context(nc.Block())
        walks = {}

        @block.sync
        def _(eng):
            walks['sync'] = _emit(nc, sems, 'sync', eng, T, sim=sim, stage=stage, nwin=nwin, wstage=wstage)

        @block.gpsimd
        def _(eng):
            walks['gpsimd'] = _emit(nc, sems, 'gpsimd', eng, T, sim=sim, stage=stage, nwin=nwin, wstage=wstage)

        @block.scalar
        def _(eng):
            walks['scalar'] = _emit(nc, sems, 'scalar', eng, T, sim=sim, stage=stage, nwin=nwin, wstage=wstage)

        @block.vector
        def _(eng):
            walks['vector'] = _emit(nc, sems, 'vector', eng, T, sim=sim, stage=stage, nwin=nwin, wstage=wstage)

        @block.tensor
        def _(eng):
            walks['tensor'] = _emit(nc, sems, 'tensor', eng, T, sim=sim, stage=stage, nwin=nwin, wstage=wstage)

        ref = walks['sync'].checks
        for nm2, wk in walks.items():
            assert wk.checks == ref, f"schedule divergence in {nm2}"
    return nc


def kernel(**inputs):
    in_maps = _host_prepare(inputs)
    if 'nc' not in _cache:
        _cache['nc'] = _build()
    res = run_bass_kernel_spmd(_cache['nc'], in_maps, core_ids=list(range(NCORE)))
    out = np.concatenate([r["out"] for r in res.results], axis=0)
    return np.ascontiguousarray(out.reshape(B, L, D).astype(np.float32))
